# version 62
# baseline (speedup 1.0000x reference)
import contextlib
from collections import defaultdict

import numpy as np
import ml_dtypes
import concourse.bass as bass
import concourse.mybir as mybir
from concourse.bass_utils import run_bass_kernel_spmd

F32 = mybir.dt.float32
BF16 = mybir.dt.bfloat16
FP8 = mybir.dt.float8e4
AF = mybir.ActivationFunctionType
ALU = mybir.AluOpType
AX = mybir.AxisListType

BATCH, SEQ, D, H, HD, DMLP = 2, 8192, 1024, 16, 64, 4096
TOK = 2048
NT = TOK // 128
NC = D // 128
NM = DMLP // 128
NP = 8
KEPS = 1e-3
LNEPS = 1e-6
PLNEPS = 1e-5
BF16_NP = ml_dtypes.bfloat16


class Buf:
    __slots__ = ("w", "r")

    def __init__(self):
        self.w = None
        self.r = []


class Sched:
    ENGS = ("T", "V", "S", "G", "L")
    NDMA = 4

    def __init__(self):
        self.q = {e: [] for e in self.ENGS}
        self.cnt = defaultdict(int)
        self.last_wait = {e: defaultdict(int) for e in self.ENGS}
        self.dma_idx = defaultdict(int)

    def sem_names(self):
        names = list(self.ENGS)
        for qn in ("L", "G", "S"):
            names += [f"{qn}{j}" for j in range(self.NDMA)]
        return names

    def emit(self, eng, body, reads=(), writes=(), waits=(), dma=False,
             force_waits=(), noinc=False):
        wl = list(waits)
        for b in reads:
            if b.w is not None:
                wl.append(b.w)
        for b in writes:
            if b.w is not None:
                wl.append(b.w)
            wl.extend(b.r)
        if noinc:
            assert not reads and not writes and not dma
            wl2 = list(force_waits)
            for s, v in wl:
                if s == eng and eng in ("T", "L"):
                    continue
                if v > self.last_wait[eng][s]:
                    self.last_wait[eng][s] = v
                    wl2.append((s, v))

            def thunk0(h, sems, _w=tuple(wl2), _b=body):
                for s_, v_ in _w:
                    h.wait_ge(sems[s_], v_)
                _b(h)
            self.q[eng].append(thunk0)
            return None
        if dma:
            k = self.dma_idx[eng]
            self.dma_idx[eng] += 1
            slot_sem = f"{eng}{k % self.NDMA}"
            prev = (k // self.NDMA) * 16
            if prev > 0:
                wl.append((slot_sem, prev))
            self.cnt[slot_sem] += 16
            tok = self.cnt[slot_sem]
            sem_name = slot_sem
            amt = 16
        else:
            self.cnt[eng] += 1
            tok = self.cnt[eng]
            sem_name = eng
            amt = 1
        wl2 = list(force_waits)
        for s, v in wl:
            if s == eng and eng in ("T", "L"):
                continue
            if v > self.last_wait[eng][s]:
                self.last_wait[eng][s] = v
                wl2.append((s, v))

        def thunk(h, sems, _w=tuple(wl2), _b=body, _s=sem_name, _a=amt):
            for s_, v_ in _w:
                h.wait_ge(sems[s_], v_)
            instr = _b(h)
            assert instr is not None, "emit body must return its last instruction"
            instr.then_inc(sems[_s], _a)
        self.q[eng].append(thunk)
        for b in reads:
            b.r.append((sem_name, tok))
        for b in writes:
            b.w = (sem_name, tok)
            b.r = []
        return (sem_name, tok)

    def barrier(self, engs=("T", "V", "S", "G", "L")):
        toks = [(e, self.cnt[e]) for e in engs]
        for qn in ("L", "G", "S"):
            if qn in engs:
                toks += [(f"{qn}{j}", self.cnt[f"{qn}{j}"])
                         for j in range(self.NDMA)]
        for e in engs:
            wl = [(s, v) for (s, v) in toks if s != e and v > 0]
            self.emit(e, lambda h: h.nop(), waits=wl, noinc=True)

    def install(self, block, sems):
        for eng_name, attr in [("L", "sync"), ("T", "tensor"), ("V", "vector"),
                               ("S", "scalar"), ("G", "gpsimd")]:
            def run_queue(h, q=tuple(self.q[eng_name])):
                for thunk in q:
                    thunk(h, sems)
            getattr(block, attr)(run_queue)


def ln_stats(sc, xt, bxt, scr, bscr, ssum, bsum, ssq, bsq, srstd, brstd,
             snmr, bnmr, eps):
    sc.emit("V", lambda h: h.reduce_sum(ssum[:], xt[:], axis=AX.X),
            reads=[bxt], writes=[bsum])
    sc.emit("S", lambda h: h.activation(scr[:], xt[:], AF.Square, accum_out=ssq[:]),
            reads=[bxt], writes=[bscr, bsq])
    inv = 1.0 / float(D)
    sc.emit("V", lambda h: h.tensor_scalar(ssum[:], ssum[:], inv, None, ALU.mult),
            reads=[bsum], writes=[bsum])
    sc.emit("V", lambda h: h.tensor_scalar(ssq[:], ssq[:], inv, float(eps),
                                           ALU.mult, ALU.add),
            reads=[bsq], writes=[bsq])
    sc.emit("V", lambda h: h.tensor_tensor(srstd[:], ssum[:], ssum[:], ALU.mult),
            reads=[bsum], writes=[brstd])
    sc.emit("V", lambda h: h.tensor_sub(ssq[:], ssq[:], srstd[:]),
            reads=[bsq, brstd], writes=[bsq])
    sc.emit("S", lambda h: h.activation(srstd[:], ssq[:], AF.Sqrt),
            reads=[bsq], writes=[brstd])
    sc.emit("V", lambda h: h.reciprocal(srstd[:], srstd[:]),
            reads=[brstd], writes=[brstd])
    sc.emit("V", lambda h: h.scalar_tensor_tensor(snmr[:], ssum[:], -1.0, srstd[:],
                                                  op0=ALU.mult, op1=ALU.mult),
            reads=[bsum, brstd], writes=[bnmr])


def build_launch_A():
    nc = bass.Bass("TRN2", target_bir_lowering=False)
    ap = {}
    for name, shape, dt in [
        ("xq", [TOK, D], F32), ("wq", [D, D], F32), ("wk", [D, D], F32),
        ("wv", [D, D], F32), ("g1fm", [128, NC], F32), ("b1fm", [128, NC], F32),
        ("bqfm", [128, NC], F32), ("bkrep", [128, D], F32), ("bvrep", [128, D], F32),
        ("ident", [128, 128], BF16),
    ]:
        ap[name] = nc.dram_tensor(name, shape, dt, kind="ExternalInput").ap()
    ap["phiq"] = nc.dram_tensor("phiq", [128, NC * TOK], BF16, kind="ExternalOutput").ap()
    ap["kvp"] = nc.dram_tensor("kvp", [128, NP * 129], F32, kind="ExternalOutput").ap()

    es = contextlib.ExitStack()
    sc = Sched()
    with es:
        S_ = lambda n, sh, dt=F32: es.enter_context(nc.sbuf_tensor(n, sh, dt))
        sx = [S_(f"sx{i}", [128, D]) for i in range(2)]
        scr = [S_(f"scr_{i}", [128, D]) for i in range(2)]
        scrk = [S_(f"scrk{i}", [128, D]) for i in range(2)]
        ssum = [S_(f"ssum{i}", [128, 1]) for i in range(2)]
        ssq = [S_(f"ssq{i}", [128, 1]) for i in range(2)]
        srstd = [S_(f"srstd{i}", [128, 1]) for i in range(2)]
        snmr = [S_(f"snmr{i}", [128, 1]) for i in range(2)]
        sxln = [S_(f"sxln{i}", [128, D], BF16) for i in range(2)]
        xfm = S_("xfm", [128, NC * TOK], BF16)
        g1 = S_("g1", [128, NC]); b1f = S_("b1f", [128, NC])
        bqe = S_("bqe", [128, NC]); bke = S_("bke", [128, D]); bve = S_("bve", [128, D])
        ident = S_("identt", [128, 128], BF16)
        wtmp = [S_(f"wtmp{i}", [128, D]) for i in range(2)]
        wqb = S_("wqb", [128, NC * D], BF16)
        wkb = S_("wkb", [128, NC * D], BF16)
        wvb = S_("wvb", [128, NC * D], BF16)
        phik = [S_(f"phik{i}", [128, D], BF16) for i in range(2)]
        vsb = [S_(f"vsb{i}", [128, NP * 129], BF16) for i in range(2)]
        phiq = S_("phiqs", [128, NC * TOK], BF16)
        kvout = S_("kvout", [128, NP * 129])

        ps = contextlib.ExitStack()
        P_ = lambda n, sh: ps.enter_context(nc.psum_tensor(n, sh, F32))
        ptr = P_("ptr", [128, 512])
        pk = P_("pk", [128, D])
        pv = P_("pv", [128, D])
        kvb = [P_(f"kvb{i}", [128, 512]) for i in range(3)]

        sems = {s: es.enter_context(nc.semaphore(s)) for s in sc.sem_names()}
        block = es.enter_context(nc.Block())

        B = lambda: Buf()
        b_sx = [B(), B()]
        b_scr = [B(), B()]; b_scrk = [B(), B()]
        b_sum = [B(), B()]; b_sq = [B(), B()]
        b_rstd = [B(), B()]; b_nmr = [B(), B()]
        b_xln = [B(), B()]
        b_xfm = [B() for _ in range(NT)]
        b_consts = B(); b_wtmp = [B(), B()]
        b_wqb = B(); b_wkb = B(); b_wvb = B()
        b_phik = [B(), B()]; b_vsb = [B(), B()]
        b_ptr = B(); b_pk = B(); b_pv = B(); b_kv = B()
        b_pq = B(); b_phiq = B(); b_kvout = B()

        def load(dst_ap, src_ap, writes, reads=()):
            return sc.emit("L", lambda h: h.dma_start(out=dst_ap, in_=src_ap),
                           reads=list(reads), writes=list(writes), dma=True)

        for dst, name in [(g1[:], "g1fm"), (b1f[:], "b1fm"), (bqe[:], "bqfm"),
                          (bke[:], "bkrep"), (bve[:], "bvrep"), (ident[:], "ident")]:
            load(dst, ap[name], [b_consts])
        sc.emit("V", lambda h: (
            h.tensor_scalar(bqe[:], bqe[:], KEPS, None, ALU.add),
            h.tensor_scalar(bke[:], bke[:], KEPS, None, ALU.add))[-1],
            reads=[b_consts], writes=[b_consts])
        for i in range(2):
            sc.emit("V", lambda h, ii=i: h.memset(
                vsb[ii][:].rearrange("p (a c) -> p a c", c=129)[:, :, 128:129], 1.0),
                writes=[b_vsb[i]])

        wdst = {"wq": (wqb, b_wqb), "wk": (wkb, b_wkb), "wv": (wvb, b_wvb)}
        for wi, wname in enumerate(("wk", "wv", "wq")):
            wt, bw = wdst[wname]
            for c in range(NC):
                buf = wtmp[(wi * NC + c) % 2]
                bb = b_wtmp[(wi * NC + c) % 2]
                load(buf[:], ap[wname].rearrange("(c p) d -> c p d", p=128)[c], [bb])
                sc.emit("S", lambda h, b=buf, w=wt, cc=c: h.activation(
                    w[:, cc * D:(cc + 1) * D], b[:], AF.Identity),
                    reads=[bb], writes=[bw])

        xq_t = ap["xq"].rearrange("(n p) d -> n p d", p=128)

        def emit_ln_and_tr(i):
            r = i % 2
            load(sx[r][:], xq_t[i], [b_sx[r]])
            ln_stats(sc, sx[r], b_sx[r], scr[r], b_scr[r], ssum[r], b_sum[r],
                     ssq[r], b_sq[r], srstd[r], b_rstd[r], snmr[r], b_nmr[r], LNEPS)
            sc.emit("S", lambda h, rr=r: h.activation(
                sxln[rr][:], sx[rr][:], AF.Identity, scale=srstd[rr][:], bias=snmr[rr][:]),
                reads=[b_sx[r], b_rstd[r], b_nmr[r]], writes=[b_xln[r]])

        def emit_tr(i):
            r = i % 2
            for half in range(2):
                def tr(h, rr=r, hf=half):
                    out = None
                    for j in range(4):
                        c = hf * 4 + j
                        out = h.matmul(ptr[:, j * 128:(j + 1) * 128],
                                       sxln[rr][:, c * 128:(c + 1) * 128],
                                       ident[:], start=True, stop=True)
                    return out
                sc.emit("T", tr, reads=[b_xln[r], b_consts], writes=[b_ptr])

                def ev(h, ii=i, hf=half):
                    out = None
                    for j in range(4):
                        c = hf * 4 + j
                        out = h.tensor_scalar(
                            xfm[:, c * TOK + ii * 128: c * TOK + (ii + 1) * 128],
                            ptr[:, j * 128:(j + 1) * 128],
                            g1[:, c:c + 1], b1f[:, c:c + 1], ALU.mult, ALU.add)
                    return out
                sc.emit("V", ev, reads=[b_ptr, b_consts], writes=[b_xfm[i], b_ptr])

        def emit_kvproj(i):
            def kvproj(h, ii=i):
                out = None
                for c in range(NC):
                    xs_ = xfm[:, c * TOK + ii * 128: c * TOK + (ii + 1) * 128]
                    st, sp = (c == 0), (c == NC - 1)
                    h.matmul(pk[:, 0:512], xs_, wkb[:, c * D:c * D + 512], start=st, stop=sp)
                    h.matmul(pk[:, 512:1024], xs_, wkb[:, c * D + 512:c * D + 1024], start=st, stop=sp)
                    h.matmul(pv[:, 0:512], xs_, wvb[:, c * D:c * D + 512], start=st, stop=sp)
                    out = h.matmul(pv[:, 512:1024], xs_, wvb[:, c * D + 512:c * D + 1024], start=st, stop=sp)
                return out
            sc.emit("T", kvproj, reads=[b_xfm[i], b_wkb, b_wvb], writes=[b_pk, b_pv])

        def emit_kvev(i):
            r = i % 2
            sc.emit("V", lambda h, rr=r: h.scalar_tensor_tensor(
                scr[rr][:], pk[:], 1.0, bke[:], op0=ALU.mult, op1=ALU.add),
                reads=[b_pk, b_consts], writes=[b_scr[r]])
            sc.emit("V", lambda h, rr=r: h.tensor_scalar(
                phik[rr][:], scr[rr][:], KEPS, None, ALU.max),
                reads=[b_scr[r]], writes=[b_phik[r], b_pk])
            sc.emit("V", lambda h, rr=r: h.scalar_tensor_tensor(
                vsb[rr][:].rearrange("p (a c) -> p a c", c=129)[:, :, 0:128],
                pv[:].rearrange("p (a c) -> p a c", c=128),
                1.0, bve[:].rearrange("p (a c) -> p a c", c=128),
                op0=ALU.mult, op1=ALU.add),
                reads=[b_pv, b_consts], writes=[b_vsb[r], b_pv])

        def emit_kvacc_half(i, half):
            r = i % 2
            def kvacc(h, rr=r, hf=half):
                out = None
                for p in range(hf * 4, hf * 4 + 4):
                    bank, q = divmod(p, 3)
                    out = h.matmul(kvb[bank][:, q * 129:q * 129 + 129],
                                   phik[rr][:, p * 128:(p + 1) * 128],
                                   vsb[rr][:, p * 129:(p + 1) * 129],
                                   start=True, stop=True)
                return out
            sc.emit("T", kvacc, reads=[b_phik[r], b_vsb[r]], writes=[b_kv])

        def emit_kvsum(i):
            if i == 0:
                def kvinit(h):
                    h.tensor_copy(kvout[:, 0:387], kvb[0][:, 0:387])
                    h.tensor_copy(kvout[:, 387:774], kvb[1][:, 0:387])
                    return h.tensor_copy(kvout[:, 774:1032], kvb[2][:, 0:258])
                sc.emit("V", kvinit, reads=[b_kv], writes=[b_kvout, b_kv])
            else:
                def kvadd(h):
                    h.tensor_add(kvout[:, 0:387], kvout[:, 0:387], kvb[0][:, 0:387])
                    h.tensor_add(kvout[:, 387:774], kvout[:, 387:774], kvb[1][:, 0:387])
                    return h.tensor_add(kvout[:, 774:1032], kvout[:, 774:1032],
                                        kvb[2][:, 0:258])
                sc.emit("V", kvadd, reads=[b_kv], writes=[b_kvout, b_kv])

        for i in range(NT + 1):
            if i < NT:
                emit_ln_and_tr(i)
            if i >= 1:
                emit_kvproj(i - 1)
                emit_kvev(i - 1)
            if i < NT:
                emit_tr(i)
            if i >= 1:
                emit_kvacc(i - 1)

        sc.emit("G", lambda h: h.dma_start(out=ap["kvp"], in_=kvout[:]),
                reads=[b_kvout], writes=[], dma=True)

        sc.barrier(("T", "V"))
        ps.close()
        ps2 = contextlib.ExitStack()
        pq = ps2.enter_context(nc.psum_tensor("pq", [128, TOK], F32))
        for m in range(NC):
            def qproj(h, mm=m):
                out = None
                for c in range(NC):
                    w_ = wqb[:, c * D + mm * 128: c * D + (mm + 1) * 128]
                    for t4 in range(4):
                        out = h.matmul(pq[:, t4 * 512:(t4 + 1) * 512], w_,
                                       xfm[:, c * TOK + t4 * 512: c * TOK + (t4 + 1) * 512],
                                       start=(c == 0), stop=(c == NC - 1))
                return out
            sc.emit("T", qproj, reads=[b_wqb] + b_xfm, writes=[b_pq])
            sc.emit("V", lambda h, mm=m: h.tensor_scalar(
                phiq[:, mm * TOK:(mm + 1) * TOK], pq[:],
                bqe[:, mm:mm + 1], KEPS, ALU.add, ALU.max),
                reads=[b_pq, b_consts], writes=[b_phiq, b_pq])
        sc.emit("G", lambda h: h.dma_start(out=ap["phiq"], in_=phiq[:]),
                reads=[b_phiq], dma=True)
        sc.emit("G", lambda h: h.nop(),
                force_waits=[(f"G{j}", sc.cnt[f"G{j}"]) for j in range(Sched.NDMA)
                             if sc.cnt[f"G{j}"] > 0], noinc=True)
        ps2.close()

        sc.install(block, sems)
    return nc


def build_launch_B():
    nc = bass.Bass("TRN2", target_bir_lowering=False)
    ap = {}
    for name, shape, dt in [
        ("xq", [TOK, D], F32), ("phiq", [128, NC * TOK], BF16),
        ("kvp4", [4, 128, NP * 129], F32),
        ("wo", [D, D], F32), ("borep", [128, D], F32),
        ("g2fm", [128, NC], F32), ("ln2bfm", [128, NC], F32),
        ("w1", [D, DMLP], F32), ("b1fm", [128, NM], F32),
        ("w2", [DMLP, D], F32), ("b2rep", [128, D], F32),
        ("fker", [D, D], BF16), ("fsin", [1, 1], F32), ("fsk", [1, 1], F32),
        ("fbrep", [128, D], BF16),
        ("ident", [128, 128], BF16), ("ones1", [1, 128], F32),
        ("emat", [16, NC * 128], BF16),
    ]:
        ap[name] = nc.dram_tensor(name, shape, dt, kind="ExternalInput").ap()
    ap["out"] = nc.dram_tensor("out", [TOK, D], F32, kind="ExternalOutput").ap()
    accbuf = nc.dram_tensor("accbuf", [TOK, D], F32, kind="Internal").ap()
    x2buf = nc.dram_tensor("x2buf", [TOK, D], F32, kind="Internal").ap()
    w1c = nc.dram_tensor("w1c", [NM, 128, NC * 128], BF16, kind="Internal").ap()
    w2c = nc.dram_tensor("w2c", [NM, 128, D], BF16, kind="Internal").ap()

    es = contextlib.ExitStack()
    sc = Sched()
    with es:
        S_ = lambda n, sh, dt=F32: es.enter_context(nc.sbuf_tensor(n, sh, dt))
        scr = [S_(f"scr_{i}", [128, D]) for i in range(2)]
        scr2 = [S_(f"scr2_{i}", [128, D], BF16) for i in range(2)]
        ssum = [S_(f"ssum{i}", [128, 1]) for i in range(2)]
        ssq = [S_(f"ssq{i}", [128, 1]) for i in range(2)]
        srstd = [S_(f"srstd{i}", [128, 1]) for i in range(2)]
        snmr = [S_(f"snmr{i}", [128, 1]) for i in range(2)]
        ident = S_("identt", [128, 128], BF16)
        ones1 = S_("ones1s", [1, 128])
        emat = S_("emats", [16, NC * 128], BF16)
        g2 = S_("g2", [128, NC]); ln2b = S_("ln2b", [128, NC])
        b1f = S_("b1fs", [128, NM])
        fbrep = S_("fbreps", [128, D], BF16)
        sfs = S_("sfs", [1, 2])
        scl = S_("scl", [128, 3])
        kvstat = S_("kvstat", [128, NP * 128], BF16)
        denst = S_("denst", [128, NC * 16], BF16)
        wtmp = [S_(f"wtmp{i}", [128, D]) for i in range(2)]
        wc1k = [S_(f"wc1k{i}", [128, D], BF16) for i in range(2)]
        xt = [S_(f"xt{i}", [128, D]) for i in range(2)]
        acc = [S_(f"acc{i}", [128, D]) for i in range(2)]
        xln2 = [S_(f"xln2{i}", [128, D], BF16) for i in range(2)]
        oring = [S_(f"oring{i}", [128, D]) for i in range(2)]
        x2b = [S_(f"x2b{i}", [128, D], BF16) for i in range(2)]
        ytmp = [S_(f"ytmp{i}", [128, D], BF16) for i in range(2)]
        w1blk = [S_(f"w1blk{i}", [128, NC * 128], BF16) for i in range(4)]
        w2blk = [S_(f"w2blk{i}", [128, D], BF16) for i in range(4)]
        x8r = [S_(f"x8r{i}", [128, NC * 128], FP8) for i in range(2)]
        k8 = S_("k8", [128, NC * D], FP8)
        xfm2 = S_("xfm2", [128, NC * TOK], BF16)

        sems = {s: es.enter_context(nc.semaphore(s)) for s in sc.sem_names()}
        block = es.enter_context(nc.Block())

        Bf = lambda: Buf()
        b_scr = [Bf(), Bf()]; b_scr2 = [Bf(), Bf()]
        b_sum = [Bf(), Bf()]; b_sq = [Bf(), Bf()]
        b_rstd = [Bf(), Bf()]; b_nmr = [Bf(), Bf()]
        b_consts = Bf(); b_sfs = Bf(); b_scl = Bf()
        b_kvstat = Bf(); b_denst = Bf()
        b_wtmp = [Bf(), Bf()]; b_wc1k = [Bf(), Bf()]
        b_xt = [Bf(), Bf()]; b_acc = [Bf(), Bf()]; b_xln2 = [Bf(), Bf()]
        b_oring = [Bf(), Bf()]; b_x2r = [Bf() for _ in range(4)]; b_x2b = [Bf(), Bf()]
        b_ytmp = [Bf(), Bf()]
        b_w1blk = [Bf() for _ in range(8)]; b_w2blk = [Bf() for _ in range(8)]
        b_x8r = [Bf(), Bf()]
        b_k8 = Bf()
        b_xfm2 = [Bf() for _ in range(NT)]
        b_w1c = Bf(); b_w2c = Bf()
        b_accd = [Bf() for _ in range(NT)]
        b_x2d = [Bf() for _ in range(NT)]

        def load(dst_ap, src_ap, writes, reads=()):
            return sc.emit("L", lambda h: h.dma_start(out=dst_ap, in_=src_ap),
                           reads=list(reads), writes=list(writes), dma=True)

        w1_src = ap["w1"].rearrange("(c p) d -> c p d", p=128)
        for c in range(NC):
            for qq in range(4):
                r = (c * 4 + qq) % 2
                load(wtmp[r][:], w1_src[c][:, qq * 1024:(qq + 1) * 1024], [b_wtmp[r]])
                sc.emit("S", lambda h, rr=r: h.activation(
                    wc1k[rr][:], wtmp[rr][:], AF.Identity),
                    reads=[b_wtmp[r]], writes=[b_wc1k[r]])
                sc.emit("G", lambda h, cc=c, q_=qq, rr=r: h.dma_start(
                    out=w1c[q_ * 8:(q_ + 1) * 8, cc].rearrange("m p k -> p m k"),
                    in_=wc1k[rr][:].rearrange("p (m k) -> p m k", k=128)),
                    reads=[b_wc1k[r]], writes=[b_w1c], dma=True)
        w2_src = ap["w2"].rearrange("(m p) d -> m p d", p=128)
        for m in range(NM):
            r = m % 2
            load(wtmp[r][:], w2_src[m], [b_wtmp[r]])
            sc.emit("S", lambda h, rr=r: h.activation(
                wc1k[rr][:], wtmp[rr][:], AF.Identity),
                reads=[b_wtmp[r]], writes=[b_wc1k[r]])
            sc.emit("G", lambda h, mm=m, rr=r: h.dma_start(
                out=w2c[mm], in_=wc1k[rr][:]),
                reads=[b_wc1k[r]], writes=[b_w2c], dma=True)

        for dst, name in [(ident[:], "ident"), (ones1[:], "ones1"),
                          (emat[:], "emat"), (borep[:], "borep"),
                          (b2rep[:], "b2rep"), (g2[:], "g2fm"),
                          (ln2b[:], "ln2bfm"), (b1f[:], "b1fm"),
                          (fbrep[:], "fbrep")]:
            load(dst, ap[name], [b_consts])
        load(sfs[:, 0:1], ap["fsin"], [b_sfs])
        load(sfs[:, 1:2], ap["fsk"], [b_sfs])

        sb_ph = contextlib.ExitStack()
        phg = [sb_ph.enter_context(nc.sbuf_tensor(f"phg{i}", [128, NC * 512], BF16))
               for i in range(2)]
        rden = sb_ph.enter_context(nc.sbuf_tensor("rden", [16, 512], BF16))
        kvsum = sb_ph.enter_context(nc.sbuf_tensor("kvsum", [128, NP * 129], F32))
        kvtmp = sb_ph.enter_context(nc.sbuf_tensor("kvtmp", [128, NP * 129], F32))
        bcs = [sb_ph.enter_context(nc.sbuf_tensor(f"bcs{i}", [128, 512], F32))
               for i in range(2)]
        b_phg = [Bf(), Bf()]; b_rden = Bf(); b_kvsum = Bf(); b_kvtmp = Bf()
        b_bcs = [Bf(), Bf()]
        sb_at = contextlib.ExitStack()
        atr = [sb_at.enter_context(nc.sbuf_tensor(f"atr{i}", [128, NC * 512], BF16))
               for i in range(2)]
        b_atr = [Bf(), Bf()]

        load(kvsum[:], ap["kvp4"][0], [b_kvsum])
        for j in (1, 2, 3):
            load(kvtmp[:], ap["kvp4"][j], [b_kvtmp])
            sc.emit("V", lambda h: h.tensor_add(kvsum[:], kvsum[:], kvtmp[:]),
                    reads=[b_kvtmp], writes=[b_kvsum])

        def mkscale2(h):
            h.tensor_tensor(kvtmp[0:1, 0:1], sfs[:, 0:1], sfs[:, 1:2], ALU.mult)
            h.reciprocal(kvtmp[0:1, 1:2], sfs[:, 0:1])
            return h.reciprocal(kvtmp[0:1, 2:3], sfs[:, 1:2])
        sc.emit("V", mkscale2, reads=[b_sfs], writes=[b_kvtmp])

        ps0 = contextlib.ExitStack()
        pscl = ps0.enter_context(nc.psum_tensor("pscl", [128, 512], F32))
        b_pscl = Bf()

        def sclmm(h):
            h.matmul(pscl[:, 0:1], ones1[:], kvtmp[0:1, 0:1], start=True, stop=True)
            h.matmul(pscl[:, 1:2], ones1[:], kvtmp[0:1, 1:2], start=True, stop=True)
            return h.matmul(pscl[:, 2:3], ones1[:], kvtmp[0:1, 2:3], start=True, stop=True)
        sc.emit("T", sclmm, reads=[b_kvtmp, b_consts], writes=[b_pscl])
        sc.emit("V", lambda h: h.tensor_copy(scl[:], pscl[:, 0:3]),
                reads=[b_pscl], writes=[b_scl, b_pscl])

        sc.emit("V", lambda h: h.memset(kvstat[:], 0.0), writes=[b_kvstat])
        sc.emit("V", lambda h: h.memset(denst[:], 0.0), writes=[b_denst])

        def mkkvstat(h):
            out = None
            for p in range(NP):
                h.tensor_copy(kvstat[0:64, p * 128:p * 128 + 64],
                              kvsum[0:64, p * 129:p * 129 + 64])
                out = h.tensor_copy(kvstat[64:128, p * 128 + 64:p * 128 + 128],
                                    kvsum[64:128, p * 129 + 64:p * 129 + 128])
            return out
        sc.emit("V", mkkvstat, reads=[b_kvsum], writes=[b_kvstat])

        def mkdenst(h):
            out = None
            for c in range(NC):
                h.tensor_copy(denst[0:64, c * 16 + 2 * c: c * 16 + 2 * c + 1],
                              kvsum[0:64, c * 129 + 128: c * 129 + 129])
                out = h.tensor_copy(denst[64:128, c * 16 + 2 * c + 1: c * 16 + 2 * c + 2],
                                    kvsum[64:128, c * 129 + 128: c * 129 + 129])
            return out
        sc.emit("V", mkdenst, reads=[b_kvsum], writes=[b_denst])

        fk_src = ap["fker"].rearrange("(c p) d -> c p d", p=128)
        for c in range(NC):
            r = c % 2
            load(wc1k[r][:], fk_src[c], [b_wc1k[r]])
            sc.emit("V", lambda h, cc=c, rr=r: h.tensor_scalar(
                k8[:, cc * D:(cc + 1) * D], wc1k[rr][:], scl[:, 2:3], None, ALU.mult),
                reads=[b_wc1k[r], b_scl], writes=[b_k8])

        sc.barrier(("T", "V"))
        ps0.close()

        ps1 = contextlib.ExitStack()
        pb = ps1.enter_context(nc.psum_tensor("pb", [128, 512], F32))
        pn = [ps1.enter_context(nc.psum_tensor(f"pn{i}", [128, 512], F32)) for i in range(2)]
        po = [ps1.enter_context(nc.psum_tensor(f"po{i}", [128, D], F32)) for i in range(2)]
        ptr3 = ps1.enter_context(nc.psum_tensor("ptr3", [128, 512], F32))
        b_pb = Bf(); b_pn = [Bf(), Bf()]
        b_po = [Bf(), Bf()]; b_ptr3 = Bf()
        sb3 = contextlib.ExitStack()
        wob = sb3.enter_context(nc.sbuf_tensor("wob", [128, NC * D], BF16))
        b_wob = Bf()
        wo_src = ap["wo"].rearrange("(c p) d -> c p d", p=128)
        for c in range(NC):
            r = c % 2
            load(wtmp[r][:], wo_src[c], [b_wtmp[r]])
            sc.emit("S", lambda h, cc=c, rr=r: h.activation(
                wob[:, cc * D:(cc + 1) * D], wtmp[rr][:], AF.Identity),
                reads=[b_wtmp[r]], writes=[b_wob])

        phq_src = ap["phiq"].rearrange("p (c s) -> p c s", s=TOK)
        xq_t = ap["xq"].rearrange("(n p) d -> n p d", p=128)
        acc_t = accbuf.rearrange("(n p) d -> n p d", p=128)

        def emit_attn_group(t4):
            gr = t4 % 2
            load(phg[gr][:].rearrange("p (c s) -> p c s", s=512),
                 phq_src[:, :, t4 * 512:(t4 + 1) * 512], [b_phg[gr]])

            def denmm(h, g=gr):
                out = None
                for c in range(NC):
                    out = h.matmul(pb[0:16, :], denst[:, c * 16:(c + 1) * 16],
                                   phg[g][:, c * 512:(c + 1) * 512],
                                   start=(c == 0), stop=(c == NC - 1))
                return out
            sc.emit("T", denmm, reads=[b_denst, b_phg[gr]], writes=[b_pb])

            def recip_b(h):
                with nc.allow_low_precision(reason="bf16 1/den as bcast operand"):
                    return h.reciprocal(rden[:], pb[0:16, :])
            sc.emit("V", recip_b, reads=[b_pb], writes=[b_rden, b_pb])
            for c in range(NC):
                r = c % 2
                sc.emit("T", lambda h, cc=c: h.matmul(
                    pb[:], emat[:, cc * 128:(cc + 1) * 128],
                    rden[:], start=True, stop=True),
                    reads=[b_consts, b_rden], writes=[b_pb])
                sc.emit("T", lambda h, cc=c, rr=r, g=gr: h.matmul(
                    pn[rr][:], kvstat[:, cc * 128:(cc + 1) * 128],
                    phg[g][:, cc * 512:(cc + 1) * 512],
                    start=True, stop=True),
                    reads=[b_kvstat, b_phg[gr]], writes=[b_pn[r]])
                sc.emit("S", lambda h, rr=r: h.copy(bcs[rr][:], pb[:]),
                        reads=[b_pb], writes=[b_bcs[r], b_pb])
                sc.emit("V", lambda h, cc=c, rr=r, g=gr: h.tensor_mul(
                    atr[g][:, cc * 512:(cc + 1) * 512],
                    pn[rr][:], bcs[rr][:]),
                    reads=[b_pn[r], b_bcs[r]], writes=[b_atr[gr], b_pn[r]])

        def emit_womm(i):
            r = i % 2
            gr = (i // 4) % 2
            js = (i % 4) * 128

            def womm(h, gg=gr, jj=js, rr=r):
                out = None
                for c in range(NC):
                    a_ = atr[gg][:, c * 512 + jj: c * 512 + jj + 128]
                    st, sp = (c == 0), (c == NC - 1)
                    h.matmul(po[rr][:, 0:512], a_, wob[:, c * D:c * D + 512],
                             start=st, stop=sp)
                    out = h.matmul(po[rr][:, 512:1024], a_,
                                   wob[:, c * D + 512:c * D + 1024], start=st, stop=sp)
                return out
            sc.emit("T", womm, reads=[b_atr[gr], b_wob], writes=[b_po[r]])

        def emit_tile_chain(i):
            r = i % 2
            load(xt[r][:], xq_t[i], [b_xt[r]])
            sc.emit("V", lambda h, rr=r: h.scalar_tensor_tensor(
                scr[rr][:], po[rr][:], 1.0, borep[:], op0=ALU.mult, op1=ALU.add),
                reads=[b_po[r], b_consts], writes=[b_scr[r], b_po[r]])
            sc.emit("V", lambda h, rr=r: h.tensor_add(scr[rr][:], scr[rr][:], xt[rr][:]),
                reads=[b_xt[r]], writes=[b_scr[r]])
            sc.emit("V", lambda h, rr=r: h.tensor_add(acc[rr][:], scr[rr][:], xt[rr][:]),
                reads=[b_scr[r], b_xt[r]], writes=[b_acc[r]])
            sc.emit("G", lambda h, ii=i, rr=r: h.dma_start(out=acc_t[ii], in_=acc[rr][:]),
                    reads=[b_acc[r]], writes=[b_accd[i]], dma=True)
            ln_stats(sc, scr[r], b_scr[r], scr2[r], b_scr2[r], ssum[r], b_sum[r],
                     ssq[r], b_sq[r], srstd[r], b_rstd[r], snmr[r], b_nmr[r], LNEPS)
            sc.emit("S", lambda h, rr=r: h.activation(
                xln2[rr][:], scr[rr][:], AF.Identity, scale=srstd[rr][:], bias=snmr[rr][:]),
                reads=[b_scr[r], b_rstd[r], b_nmr[r]], writes=[b_xln2[r]])

        def emit_tr3(i):
            r = i % 2
            for half in range(2):
                def tr(h, rr=r, hf=half):
                    out = None
                    for j in range(4):
                        c = hf * 4 + j
                        out = h.matmul(ptr3[:, j * 128:(j + 1) * 128],
                                       xln2[rr][:, c * 128:(c + 1) * 128],
                                       ident[:], start=True, stop=True)
                    return out
                sc.emit("T", tr, reads=[b_xln2[r], b_consts], writes=[b_ptr3])

                def ev(h, ii=i, hf=half):
                    out = None
                    for j in range(4):
                        c = hf * 4 + j
                        out = h.tensor_scalar(
                            xfm2[:, c * TOK + ii * 128: c * TOK + (ii + 1) * 128],
                            ptr3[:, j * 128:(j + 1) * 128],
                            g2[:, c:c + 1], ln2b[:, c:c + 1], ALU.mult, ALU.add)
                    return out
                sc.emit("V", ev, reads=[b_ptr3, b_consts], writes=[b_xfm2[i], b_ptr3])

        emit_attn_group(0)
        emit_attn_group(1)
        for i in range(NT + 1):
            if i < NT:
                emit_womm(i)
                if i % 4 == 3 and (i // 4) + 2 <= 3:
                    emit_attn_group((i // 4) + 2)
            if i >= 1:
                emit_tr3(i - 1)
            if i < NT:
                emit_tile_chain(i)

        sc.barrier(("T", "V", "S"))
        sb3.close()
        sb_at.close()
        sb_ph.close()
        ps1.close()
        sbk8 = contextlib.ExitStack()
        k8 = sbk8.enter_context(nc.sbuf_tensor("k8", [128, NC * D], FP8))
        x2r = [sbk8.enter_context(nc.sbuf_tensor(f"x2r{i}", [128, D], F32))
               for i in range(4)]
        sb4 = contextlib.ExitStack()
        y1g = sb4.enter_context(nc.sbuf_tensor("y1g", [128, NM * 512], BF16))
        b2rep = sb4.enter_context(nc.sbuf_tensor("b2reps", [128, D], F32))
        y2o = [sb4.enter_context(nc.sbuf_tensor(f"y2o{i}", [128, D], F32))
               for i in range(4)]
        b_y2o = [Bf() for _ in range(4)]
        fk_src = ap["fker"].rearrange("(c p) d -> c p d", p=128)
        for c in range(NC):
            r = c % 2
            load(wc1k[r][:], fk_src[c], [b_wc1k[r]])
            sc.emit("V", lambda h, cc=c, rr=r: h.tensor_scalar(
                k8[:, cc * D:(cc + 1) * D], wc1k[rr][:], scl[:, 2:3], None, ALU.mult),
                reads=[b_wc1k[r], b_scl], writes=[b_k8])
        w1blk = [sb4.enter_context(nc.sbuf_tensor(f"w1blk{i}", [128, NC * 128], BF16))
                 for i in range(8)]
        w2blk = [sb4.enter_context(nc.sbuf_tensor(f"w2blk{i}", [128, D], BF16))
                 for i in range(8)]
        b_y1g = [Bf() for _ in range(NM)]
        load(b2rep[:], ap["b2rep"], [b_consts])
        x2_t = x2buf.rearrange("(n p) d -> n p d", p=128)
        qwait = [None]
        for q in range(4):
            qoff = q * 512
            psy1 = contextlib.ExitStack()
            py1 = [psy1.enter_context(nc.psum_tensor(f"py1_{q}_{i}", [128, 512], F32))
                   for i in range(2)]
            b_py1 = [Bf(), Bf()]
            for m in range(NM):
                r = m % 8
                load(w1blk[r][:], w1c[m], [b_w1blk[r]], reads=[b_w1c])

                def y1mm(h, rr=r, pp=m % 2, qo=qoff):
                    out = None
                    for c in range(NC):
                        out = h.matmul(
                            py1[pp][:],
                            w1blk[rr][:, c * 128:(c + 1) * 128],
                            xfm2[:, c * TOK + qo: c * TOK + qo + 512],
                            start=(c == 0), stop=(c == NC - 1))
                    return out
                pr = m % 2
                sc.emit("T", y1mm, reads=[b_w1blk[r]] + b_xfm2, writes=[b_py1[pr]],
                        force_waits=[qwait[0]] if (m == 0 and qwait[0]) else ())
                sc.emit("S", lambda h, mm=m, pp=pr: h.activation(
                    y1g[:, mm * 512:(mm + 1) * 512],
                    py1[pp][:], AF.Gelu_apprx_tanh, bias=b1f[:, mm:mm + 1]),
                    reads=[b_py1[pr], b_consts], writes=[b_y1g[m], b_py1[pr]])
            t_S_gelu = ("S", sc.cnt["S"])
            psy1.close()
            psy2 = contextlib.ExitStack()
            py2 = [psy2.enter_context(nc.psum_tensor(f"py2_{q}_{i}", [128, D], F32))
                   for i in range(4)]
            b_py2 = [Bf() for _ in range(4)]
            for m in range(NM):
                r = m % 8
                sc.emit("S", lambda h, rr=r, mm=m: h.dma_start(
                    out=w2blk[rr][:], in_=w2c[mm]),
                    reads=[b_w2c], writes=[b_w2blk[r]], dma=True)

                def y2mm(h, mm=m, rr=r):
                    out = None
                    for ss in range(4):
                        yg = y1g[:, mm * 512 + ss * 128: mm * 512 + (ss + 1) * 128]
                        st, sp = (mm == 0), (mm == NM - 1)
                        h.matmul(py2[ss][:, 0:512], yg, w2blk[rr][:, 0:512],
                                 start=st, stop=sp)
                        out = h.matmul(py2[ss][:, 512:1024], yg, w2blk[rr][:, 512:1024],
                                       start=st, stop=sp)
                    return out
                sc.emit("T", y2mm, reads=[b_y1g[m], b_w2blk[r]],
                        writes=b_py2 if m == NM - 1 else [],
                        force_waits=[t_S_gelu] if m == 0 else ())
            for ss in range(4):
                i = q * 4 + ss
                r = i % 2
                load(acc[r][:], acc_t[i], [b_acc[r]], reads=[b_accd[i]])

                sc.emit("V", lambda h, sss=ss, rr=r: h.scalar_tensor_tensor(
                    scr[rr][:], py2[sss][:], 1.0, b2rep[:], op0=ALU.mult, op1=ALU.add),
                    reads=[b_py2[ss], b_consts], writes=[b_scr[r], b_py2[ss]])
                sc.emit("V", lambda h, rr=r: h.tensor_add(scr[rr][:], scr[rr][:], acc[rr][:]),
                    reads=[b_acc[r]], writes=[b_scr[r]])
                ln_stats(sc, scr[r], b_scr[r], scr2[r], b_scr2[r], ssum[r], b_sum[r],
                         ssq[r], b_sq[r], srstd[r], b_rstd[r], snmr[r], b_nmr[r], PLNEPS)
                sc.emit("S", lambda h, rr=r: h.activation(
                    x2r[rr][:], scr[rr][:], AF.Identity, scale=srstd[rr][:], bias=snmr[rr][:]),
                    reads=[b_scr[r], b_rstd[r], b_nmr[r]], writes=[b_x2r[r]])
                sc.emit("G", lambda h, ii=i, rr=r: h.dma_start(out=x2_t[ii], in_=x2r[rr][:]),
                        reads=[b_x2r[r]], writes=[b_x2d[i]], dma=True)
            sc.barrier(("T", "V", "S"))
            psy2.close()

        sb4.close()
        ps5 = contextlib.ExitStack()
        pf = [ps5.enter_context(nc.psum_tensor(f"pf{i}", [128, D], F32)) for i in range(3)]
        ptr5 = [ps5.enter_context(nc.psum_tensor(f"ptr5{i}", [128, 512], F32)) for i in range(2)]
        b_pf = [Bf() for _ in range(3)]; b_ptr5 = [Bf(), Bf()]
        t_b5 = qwait[0]
        out_t = ap["out"].rearrange("(n p) d -> n p d", p=128)

        def emit_x8_pre(i):
            r4 = i % 4
            load(x2r[r4][:], x2_t[i], [b_x2r[r4]], reads=[b_x2d[i]])
            sc.emit("S", lambda h, r4_=r4: h.copy(x2b[r4_][:], x2r[r4_][:]),
                    reads=[b_x2r[r4]], writes=[b_x2b[r4]])

        def emit_x8(i):
            r = i % 2
            for hf in range(2):
                def tr(h, rr=r, hh=hf):
                    out = None
                    for j in range(4):
                        c = hh * 4 + j
                        out = h.matmul(ptr5[rr][:, j * 128:(j + 1) * 128],
                                       x2b[rr][:, c * 128:(c + 1) * 128],
                                       ident[:], start=True, stop=True)
                    return out
                sc.emit("T", tr, reads=[b_x2b[r], b_consts], writes=[b_ptr5[r]])

                def ev(h, rr=r, hh=hf):
                    out = None
                    for j in range(4):
                        c = hh * 4 + j
                        out = h.tensor_scalar(
                            x8r[rr][:, c * 128:(c + 1) * 128],
                            ptr5[rr][:, j * 128:(j + 1) * 128],
                            scl[:, 1:2], None, ALU.mult)
                    return out
                sc.emit("V", ev, reads=[b_ptr5[r], b_scl], writes=[b_x8r[r], b_ptr5[r]])

        def emit_ffn(i):
            r = i % 3
            rx = i % 2

            def ffnmm(h, rr=r, rx_=rx):
                out = None
                for c in range(NC):
                    xs_ = x8r[rx_][:, c * 128:(c + 1) * 128]
                    st, sp = (c == 0), (c == NC - 1)
                    h.matmul(pf[rr][:, 0:512], xs_, k8[:, c * D:c * D + 512],
                             start=st, stop=sp)
                    out = h.matmul(pf[rr][:, 512:1024], xs_,
                                   k8[:, c * D + 512:c * D + 1024], start=st, stop=sp)
                return out
            sc.emit("T", ffnmm, reads=[b_x8r[rx], b_k8], writes=[b_pf[r]])

        def emit_out_chain(i):
            r = i % 2
            p3 = i % 3
            sc.emit("S", lambda h, rr=r, pp=p3: h.activation(
                ytmp[rr][:], pf[pp][:], AF.Identity, scale=scl[:, 0:1]),
                reads=[b_pf[p3], b_scl], writes=[b_ytmp[r], b_pf[p3]])
            sc.emit("V", lambda h, rr=r: h.tensor_add(ytmp[rr][:], ytmp[rr][:], fbrep[:]),
                reads=[b_consts], writes=[b_ytmp[r]])
            sc.emit("V", lambda h, rr=r, r4_=i % 4: h.tensor_add(
                scr[rr][:], x2r[r4_][:], ytmp[rr][:]),
                reads=[b_x2r[i % 4], b_ytmp[r]], writes=[b_scr[r]])
            ln_stats(sc, scr[r], b_scr[r], scr2[r], b_scr2[r], ssum[r], b_sum[r],
                     ssq[r], b_sq[r], srstd[r], b_rstd[r], snmr[r], b_nmr[r], PLNEPS)
            sc.emit("S", lambda h, rr=r: h.activation(
                oring[rr][:], scr[rr][:], AF.Identity, scale=srstd[rr][:], bias=snmr[rr][:]),
                reads=[b_scr[r], b_rstd[r], b_nmr[r]], writes=[b_oring[r]])
            sc.emit("G", lambda h, ii=i, rr=r: h.dma_start(out=out_t[ii], in_=oring[rr][:]),
                    reads=[b_oring[r]], dma=True)

        emit_x8_pre(0)
        emit_x8_pre(1)
        for i in range(NT + 1):
            if i + 2 < NT:
                emit_x8_pre(i + 2)
            if i >= 1:
                emit_ffn(i - 1)
            if i < NT:
                emit_x8(i)
            if i >= 1:
                emit_out_chain(i - 1)
        sc.emit("G", lambda h: h.nop(),
                force_waits=[(f"G{j}", sc.cnt[f"G{j}"]) for j in range(Sched.NDMA)
                             if sc.cnt[f"G{j}"] > 0], noinc=True)
        ps2.close()

        sc.install(block, sems)
    return nc


def build_launch_B():
    nc = bass.Bass("TRN2", target_bir_lowering=False)
    ap = {}
    for name, shape, dt in [
        ("xq", [TOK, D], F32), ("phiq", [128, NC * TOK], BF16),
        ("kvp4", [4, 128, NP * 129], F32),
        ("wo", [D, D], F32), ("borep", [128, D], F32),
        ("g2fm", [128, NC], F32), ("ln2bfm", [128, NC], F32),
        ("w1", [D, DMLP], F32), ("b1fm", [128, NM], F32),
        ("w2", [DMLP, D], F32), ("b2rep", [128, D], F32),
        ("fker", [D, D], BF16), ("fsin", [1, 1], F32), ("fsk", [1, 1], F32),
        ("fbrep", [128, D], BF16),
        ("ident", [128, 128], BF16), ("ones1", [1, 128], F32),
        ("emat", [16, NC * 128], BF16),
    ]:
        ap[name] = nc.dram_tensor(name, shape, dt, kind="ExternalInput").ap()
    ap["out"] = nc.dram_tensor("out", [TOK, D], F32, kind="ExternalOutput").ap()
    accbuf = nc.dram_tensor("accbuf", [TOK, D], F32, kind="Internal").ap()
    x2buf = nc.dram_tensor("x2buf", [TOK, D], F32, kind="Internal").ap()
    w1c = nc.dram_tensor("w1c", [NM, 128, NC * 128], BF16, kind="Internal").ap()
    w2c = nc.dram_tensor("w2c", [NM, 128, D], BF16, kind="Internal").ap()

    es = contextlib.ExitStack()
    sc = Sched()
    with es:
        S_ = lambda n, sh, dt=F32: es.enter_context(nc.sbuf_tensor(n, sh, dt))
        scr = [S_(f"scr_{i}", [128, D]) for i in range(2)]
        scr2 = [S_(f"scr2_{i}", [128, D], BF16) for i in range(2)]
        ssum = [S_(f"ssum{i}", [128, 1]) for i in range(2)]
        ssq = [S_(f"ssq{i}", [128, 1]) for i in range(2)]
        srstd = [S_(f"srstd{i}", [128, 1]) for i in range(2)]
        snmr = [S_(f"snmr{i}", [128, 1]) for i in range(2)]
        ident = S_("identt", [128, 128], BF16)
        ones1 = S_("ones1s", [1, 128])
        emat = S_("emats", [16, NC * 128], BF16)
        g2 = S_("g2", [128, NC]); ln2b = S_("ln2b", [128, NC])
        b1f = S_("b1fs", [128, NM])
        fbrep = S_("fbreps", [128, D], BF16)
        sfs = S_("sfs", [1, 2])
        scl = S_("scl", [128, 3])
        kvstat = S_("kvstat", [128, NP * 128], BF16)
        denst = S_("denst", [128, NC * 16], BF16)
        wtmp = [S_(f"wtmp{i}", [128, D]) for i in range(2)]
        wc1k = [S_(f"wc1k{i}", [128, D], BF16) for i in range(2)]
        xt = [S_(f"xt{i}", [128, D]) for i in range(2)]
        acc = [S_(f"acc{i}", [128, D]) for i in range(2)]
        xln2 = [S_(f"xln2{i}", [128, D], BF16) for i in range(2)]
        oring = [S_(f"oring{i}", [128, D]) for i in range(2)]
        x2b = [S_(f"x2b{i}", [128, D], BF16) for i in range(2)]
        ytmp = [S_(f"ytmp{i}", [128, D], BF16) for i in range(2)]
        w1blk = [S_(f"w1blk{i}", [128, NC * 128], BF16) for i in range(4)]
        w2blk = [S_(f"w2blk{i}", [128, D], BF16) for i in range(4)]
        x8r = [S_(f"x8r{i}", [128, NC * 128], FP8) for i in range(2)]
        k8 = S_("k8", [128, NC * D], FP8)
        xfm2 = S_("xfm2", [128, NC * TOK], BF16)

        sems = {s: es.enter_context(nc.semaphore(s)) for s in sc.sem_names()}
        block = es.enter_context(nc.Block())

        Bf = lambda: Buf()
        b_scr = [Bf(), Bf()]; b_scr2 = [Bf(), Bf()]
        b_sum = [Bf(), Bf()]; b_sq = [Bf(), Bf()]
        b_rstd = [Bf(), Bf()]; b_nmr = [Bf(), Bf()]
        b_consts = Bf(); b_sfs = Bf(); b_scl = Bf()
        b_kvstat = Bf(); b_denst = Bf()
        b_wtmp = [Bf(), Bf()]; b_wc1k = [Bf(), Bf()]
        b_xt = [Bf(), Bf()]; b_acc = [Bf(), Bf()]; b_xln2 = [Bf(), Bf()]
        b_oring = [Bf(), Bf()]; b_x2r = [Bf() for _ in range(4)]; b_x2b = [Bf(), Bf()]
        b_ytmp = [Bf(), Bf()]
        b_w1blk = [Bf() for _ in range(8)]; b_w2blk = [Bf() for _ in range(8)]
        b_x8r = [Bf(), Bf()]
        b_k8 = Bf()
        b_xfm2 = [Bf() for _ in range(NT)]
        b_w1c = Bf(); b_w2c = Bf()
        b_accd = [Bf() for _ in range(NT)]
        b_x2d = [Bf() for _ in range(NT)]

        def load(dst_ap, src_ap, writes, reads=()):
            return sc.emit("L", lambda h: h.dma_start(out=dst_ap, in_=src_ap),
                           reads=list(reads), writes=list(writes), dma=True)

        w1_src = ap["w1"].rearrange("(c p) d -> c p d", p=128)
        for c in range(NC):
            for qq in range(4):
                r = (c * 4 + qq) % 2
                load(wtmp[r][:], w1_src[c][:, qq * 1024:(qq + 1) * 1024], [b_wtmp[r]])
                sc.emit("S", lambda h, rr=r: h.activation(
                    wc1k[rr][:], wtmp[rr][:], AF.Identity),
                    reads=[b_wtmp[r]], writes=[b_wc1k[r]])
                sc.emit("G", lambda h, cc=c, q_=qq, rr=r: h.dma_start(
                    out=w1c[q_ * 8:(q_ + 1) * 8, cc].rearrange("m p k -> p m k"),
                    in_=wc1k[rr][:].rearrange("p (m k) -> p m k", k=128)),
                    reads=[b_wc1k[r]], writes=[b_w1c], dma=True)
        w2_src = ap["w2"].rearrange("(m p) d -> m p d", p=128)
        for m in range(NM):
            r = m % 2
            load(wtmp[r][:], w2_src[m], [b_wtmp[r]])
            sc.emit("S", lambda h, rr=r: h.activation(
                wc1k[rr][:], wtmp[rr][:], AF.Identity),
                reads=[b_wtmp[r]], writes=[b_wc1k[r]])
            sc.emit("G", lambda h, mm=m, rr=r: h.dma_start(
                out=w2c[mm], in_=wc1k[rr][:]),
                reads=[b_wc1k[r]], writes=[b_w2c], dma=True)

        for dst, name in [(ident[:], "ident"), (ones1[:], "ones1"),
                          (emat[:], "emat"), (borep[:], "borep"),
                          (b2rep[:], "b2rep"), (g2[:], "g2fm"),
                          (ln2b[:], "ln2bfm"), (b1f[:], "b1fm"),
                          (fbrep[:], "fbrep")]:
            load(dst, ap[name], [b_consts])
        load(sfs[:, 0:1], ap["fsin"], [b_sfs])
        load(sfs[:, 1:2], ap["fsk"], [b_sfs])

        sb_ph = contextlib.ExitStack()
        phg = [sb_ph.enter_context(nc.sbuf_tensor(f"phg{i}", [128, NC * 512], BF16))
               for i in range(2)]
        rden = sb_ph.enter_context(nc.sbuf_tensor("rden", [16, 512], BF16))
        kvsum = sb_ph.enter_context(nc.sbuf_tensor("kvsum", [128, NP * 129], F32))
        kvtmp = sb_ph.enter_context(nc.sbuf_tensor("kvtmp", [128, NP * 129], F32))
        bcs = [sb_ph.enter_context(nc.sbuf_tensor(f"bcs{i}", [128, 512], F32))
               for i in range(2)]
        b_phg = [Bf(), Bf()]; b_rden = Bf(); b_kvsum = Bf(); b_kvtmp = Bf()
        b_bcs = [Bf(), Bf()]
        sb_at = contextlib.ExitStack()
        atr = [sb_at.enter_context(nc.sbuf_tensor(f"atr{i}", [128, NC * 512], BF16))
               for i in range(2)]
        b_atr = [Bf(), Bf()]

        load(kvsum[:], ap["kvp4"][0], [b_kvsum])
        for j in (1, 2, 3):
            load(kvtmp[:], ap["kvp4"][j], [b_kvtmp])
            sc.emit("V", lambda h: h.tensor_add(kvsum[:], kvsum[:], kvtmp[:]),
                    reads=[b_kvtmp], writes=[b_kvsum])

        def mkscale2(h):
            h.tensor_tensor(kvtmp[0:1, 0:1], sfs[:, 0:1], sfs[:, 1:2], ALU.mult)
            h.reciprocal(kvtmp[0:1, 1:2], sfs[:, 0:1])
            return h.reciprocal(kvtmp[0:1, 2:3], sfs[:, 1:2])
        sc.emit("V", mkscale2, reads=[b_sfs], writes=[b_kvtmp])

        ps0 = contextlib.ExitStack()
        pscl = ps0.enter_context(nc.psum_tensor("pscl", [128, 512], F32))
        b_pscl = Bf()

        def sclmm(h):
            h.matmul(pscl[:, 0:1], ones1[:], kvtmp[0:1, 0:1], start=True, stop=True)
            h.matmul(pscl[:, 1:2], ones1[:], kvtmp[0:1, 1:2], start=True, stop=True)
            return h.matmul(pscl[:, 2:3], ones1[:], kvtmp[0:1, 2:3], start=True, stop=True)
        sc.emit("T", sclmm, reads=[b_kvtmp, b_consts], writes=[b_pscl])
        sc.emit("V", lambda h: h.tensor_copy(scl[:], pscl[:, 0:3]),
                reads=[b_pscl], writes=[b_scl, b_pscl])

        sc.emit("V", lambda h: h.memset(kvstat[:], 0.0), writes=[b_kvstat])
        sc.emit("V", lambda h: h.memset(denst[:], 0.0), writes=[b_denst])

        def mkkvstat(h):
            out = None
            for p in range(NP):
                h.tensor_copy(kvstat[0:64, p * 128:p * 128 + 64],
                              kvsum[0:64, p * 129:p * 129 + 64])
                out = h.tensor_copy(kvstat[64:128, p * 128 + 64:p * 128 + 128],
                                    kvsum[64:128, p * 129 + 64:p * 129 + 128])
            return out
        sc.emit("V", mkkvstat, reads=[b_kvsum], writes=[b_kvstat])

        def mkdenst(h):
            out = None
            for c in range(NC):
                h.tensor_copy(denst[0:64, c * 16 + 2 * c: c * 16 + 2 * c + 1],
                              kvsum[0:64, c * 129 + 128: c * 129 + 129])
                out = h.tensor_copy(denst[64:128, c * 16 + 2 * c + 1: c * 16 + 2 * c + 2],
                                    kvsum[64:128, c * 129 + 128: c * 129 + 129])
            return out
        sc.emit("V", mkdenst, reads=[b_kvsum], writes=[b_denst])

        fk_src = ap["fker"].rearrange("(c p) d -> c p d", p=128)
        for c in range(NC):
            r = c % 2
            load(wc1k[r][:], fk_src[c], [b_wc1k[r]])
            sc.emit("V", lambda h, cc=c, rr=r: h.tensor_scalar(
                k8[:, cc * D:(cc + 1) * D], wc1k[rr][:], scl[:, 2:3], None, ALU.mult),
                reads=[b_wc1k[r], b_scl], writes=[b_k8])

        sc.barrier(("T", "V"))
        ps0.close()

        ps1 = contextlib.ExitStack()
        pb = ps1.enter_context(nc.psum_tensor("pb", [128, 512], F32))
        pn = [ps1.enter_context(nc.psum_tensor(f"pn{i}", [128, 512], F32)) for i in range(2)]
        po = [ps1.enter_context(nc.psum_tensor(f"po{i}", [128, D], F32)) for i in range(2)]
        ptr3 = ps1.enter_context(nc.psum_tensor("ptr3", [128, 512], F32))
        b_pb = Bf(); b_pn = [Bf(), Bf()]
        b_po = [Bf(), Bf()]; b_ptr3 = Bf()
        sb3 = contextlib.ExitStack()
        wob = sb3.enter_context(nc.sbuf_tensor("wob", [128, NC * D], BF16))
        b_wob = Bf()
        wo_src = ap["wo"].rearrange("(c p) d -> c p d", p=128)
        for c in range(NC):
            r = c % 2
            load(wtmp[r][:], wo_src[c], [b_wtmp[r]])
            sc.emit("S", lambda h, cc=c, rr=r: h.activation(
                wob[:, cc * D:(cc + 1) * D], wtmp[rr][:], AF.Identity),
                reads=[b_wtmp[r]], writes=[b_wob])

        phq_src = ap["phiq"].rearrange("p (c s) -> p c s", s=TOK)
        xq_t = ap["xq"].rearrange("(n p) d -> n p d", p=128)
        acc_t = accbuf.rearrange("(n p) d -> n p d", p=128)

        def emit_attn_group(t4):
            gr = t4 % 2
            load(phg[gr][:].rearrange("p (c s) -> p c s", s=512),
                 phq_src[:, :, t4 * 512:(t4 + 1) * 512], [b_phg[gr]])

            def denmm(h, g=gr):
                out = None
                for c in range(NC):
                    out = h.matmul(pb[0:16, :], denst[:, c * 16:(c + 1) * 16],
                                   phg[g][:, c * 512:(c + 1) * 512],
                                   start=(c == 0), stop=(c == NC - 1))
                return out
            sc.emit("T", denmm, reads=[b_denst, b_phg[gr]], writes=[b_pb])

            def recip_b(h):
                with nc.allow_low_precision(reason="bf16 1/den as bcast operand"):
                    return h.reciprocal(rden[:], pb[0:16, :])
            sc.emit("V", recip_b, reads=[b_pb], writes=[b_rden, b_pb])
            for c in range(NC):
                r = c % 2
                sc.emit("T", lambda h, cc=c: h.matmul(
                    pb[:], emat[:, cc * 128:(cc + 1) * 128],
                    rden[:], start=True, stop=True),
                    reads=[b_consts, b_rden], writes=[b_pb])
                sc.emit("T", lambda h, cc=c, rr=r, g=gr: h.matmul(
                    pn[rr][:], kvstat[:, cc * 128:(cc + 1) * 128],
                    phg[g][:, cc * 512:(cc + 1) * 512],
                    start=True, stop=True),
                    reads=[b_kvstat, b_phg[gr]], writes=[b_pn[r]])
                sc.emit("S", lambda h, rr=r: h.copy(bcs[rr][:], pb[:]),
                        reads=[b_pb], writes=[b_bcs[r], b_pb])
                sc.emit("V", lambda h, cc=c, rr=r, g=gr: h.tensor_mul(
                    atr[g][:, cc * 512:(cc + 1) * 512],
                    pn[rr][:], bcs[rr][:]),
                    reads=[b_pn[r], b_bcs[r]], writes=[b_atr[gr], b_pn[r]])

        def emit_womm(i):
            r = i % 2
            gr = (i // 4) % 2
            js = (i % 4) * 128

            def womm(h, gg=gr, jj=js, rr=r):
                out = None
                for c in range(NC):
                    a_ = atr[gg][:, c * 512 + jj: c * 512 + jj + 128]
                    st, sp = (c == 0), (c == NC - 1)
                    h.matmul(po[rr][:, 0:512], a_, wob[:, c * D:c * D + 512],
                             start=st, stop=sp)
                    out = h.matmul(po[rr][:, 512:1024], a_,
                                   wob[:, c * D + 512:c * D + 1024], start=st, stop=sp)
                return out
            sc.emit("T", womm, reads=[b_atr[gr], b_wob], writes=[b_po[r]])

        def emit_tile_chain(i):
            r = i % 2
            load(xt[r][:], xq_t[i], [b_xt[r]])
            sc.emit("V", lambda h, rr=r: h.scalar_tensor_tensor(
                scr[rr][:], po[rr][:], 1.0, borep[:], op0=ALU.mult, op1=ALU.add),
                reads=[b_po[r], b_consts], writes=[b_scr[r], b_po[r]])
            sc.emit("V", lambda h, rr=r: h.tensor_add(scr[rr][:], scr[rr][:], xt[rr][:]),
                reads=[b_xt[r]], writes=[b_scr[r]])
            sc.emit("V", lambda h, rr=r: h.tensor_add(acc[rr][:], scr[rr][:], xt[rr][:]),
                reads=[b_scr[r], b_xt[r]], writes=[b_acc[r]])
            sc.emit("G", lambda h, ii=i, rr=r: h.dma_start(out=acc_t[ii], in_=acc[rr][:]),
                    reads=[b_acc[r]], writes=[b_accd[i]], dma=True)
            ln_stats(sc, scr[r], b_scr[r], scr2[r], b_scr2[r], ssum[r], b_sum[r],
                     ssq[r], b_sq[r], srstd[r], b_rstd[r], snmr[r], b_nmr[r], LNEPS)
            sc.emit("S", lambda h, rr=r: h.activation(
                xln2[rr][:], scr[rr][:], AF.Identity, scale=srstd[rr][:], bias=snmr[rr][:]),
                reads=[b_scr[r], b_rstd[r], b_nmr[r]], writes=[b_xln2[r]])

        def emit_tr3(i):
            r = i % 2
            for half in range(2):
                def tr(h, rr=r, hf=half):
                    out = None
                    for j in range(4):
                        c = hf * 4 + j
                        out = h.matmul(ptr3[:, j * 128:(j + 1) * 128],
                                       xln2[rr][:, c * 128:(c + 1) * 128],
                                       ident[:], start=True, stop=True)
                    return out
                sc.emit("T", tr, reads=[b_xln2[r], b_consts], writes=[b_ptr3])

                def ev(h, ii=i, hf=half):
                    out = None
                    for j in range(4):
                        c = hf * 4 + j
                        out = h.tensor_scalar(
                            xfm2[:, c * TOK + ii * 128: c * TOK + (ii + 1) * 128],
                            ptr3[:, j * 128:(j + 1) * 128],
                            g2[:, c:c + 1], ln2b[:, c:c + 1], ALU.mult, ALU.add)
                    return out
                sc.emit("V", ev, reads=[b_ptr3, b_consts], writes=[b_xfm2[i], b_ptr3])

        emit_attn_group(0)
        emit_attn_group(1)
        for i in range(NT + 1):
            if i < NT:
                emit_womm(i)
                if i % 4 == 3 and (i // 4) + 2 <= 3:
                    emit_attn_group((i // 4) + 2)
            if i >= 1:
                emit_tr3(i - 1)
            if i < NT:
                emit_tile_chain(i)

        sc.barrier(("T", "V", "S"))
        sb3.close()
        sb_at.close()
        sb_ph.close()
        ps1.close()
        sbk8 = contextlib.ExitStack()
        k8 = sbk8.enter_context(nc.sbuf_tensor("k8", [128, NC * D], FP8))
        x2r = [sbk8.enter_context(nc.sbuf_tensor(f"x2r{i}", [128, D], F32))
               for i in range(4)]
        sb4 = contextlib.ExitStack()
        y1g = sb4.enter_context(nc.sbuf_tensor("y1g", [128, NM * 512], BF16))
        b2rep = sb4.enter_context(nc.sbuf_tensor("b2reps", [128, D], F32))
        y2o = [sb4.enter_context(nc.sbuf_tensor(f"y2o{i}", [128, D], F32))
               for i in range(4)]
        b_y2o = [Bf() for _ in range(4)]
        fk_src = ap["fker"].rearrange("(c p) d -> c p d", p=128)
        for c in range(NC):
            r = c % 2
            load(wc1k[r][:], fk_src[c], [b_wc1k[r]])
            sc.emit("V", lambda h, cc=c, rr=r: h.tensor_scalar(
                k8[:, cc * D:(cc + 1) * D], wc1k[rr][:], scl[:, 2:3], None, ALU.mult),
                reads=[b_wc1k[r], b_scl], writes=[b_k8])
        w1blk = [sb4.enter_context(nc.sbuf_tensor(f"w1blk{i}", [128, NC * 128], BF16))
                 for i in range(8)]
        w2blk = [sb4.enter_context(nc.sbuf_tensor(f"w2blk{i}", [128, D], BF16))
                 for i in range(8)]
        b_y1g = [Bf() for _ in range(NM)]
        load(b2rep[:], ap["b2rep"], [b_consts])
        x2_t = x2buf.rearrange("(n p) d -> n p d", p=128)
        qwait = [None]
        for q in range(4):
            qoff = q * 512
            psy1 = contextlib.ExitStack()
            py1 = [psy1.enter_context(nc.psum_tensor(f"py1_{q}_{i}", [128, 512], F32))
                   for i in range(2)]
            b_py1 = [Bf(), Bf()]
            for m in range(NM):
                r = m % 8
                load(w1blk[r][:], w1c[m], [b_w1blk[r]], reads=[b_w1c])

                def y1mm(h, rr=r, pp=m % 2, qo=qoff):
                    out = None
                    for c in range(NC):
                        out = h.matmul(
                            py1[pp][:],
                            w1blk[rr][:, c * 128:(c + 1) * 128],
                            xfm2[:, c * TOK + qo: c * TOK + qo + 512],
                            start=(c == 0), stop=(c == NC - 1))
                    return out
                pr = m % 2
                sc.emit("T", y1mm, reads=[b_w1blk[r]] + b_xfm2, writes=[b_py1[pr]],
                        force_waits=[qwait[0]] if (m == 0 and qwait[0]) else ())
                sc.emit("S", lambda h, mm=m, pp=pr: h.activation(
                    y1g[:, mm * 512:(mm + 1) * 512],
                    py1[pp][:], AF.Gelu_apprx_tanh, bias=b1f[:, mm:mm + 1]),
                    reads=[b_py1[pr], b_consts], writes=[b_y1g[m], b_py1[pr]])
            t_S_gelu = ("S", sc.cnt["S"])
            psy1.close()
            psy2 = contextlib.ExitStack()
            py2 = [psy2.enter_context(nc.psum_tensor(f"py2_{q}_{i}", [128, D], F32))
                   for i in range(4)]
            b_py2 = [Bf() for _ in range(4)]
            for m in range(NM):
                r = m % 8
                sc.emit("S", lambda h, rr=r, mm=m: h.dma_start(
                    out=w2blk[rr][:], in_=w2c[mm]),
                    reads=[b_w2c], writes=[b_w2blk[r]], dma=True)

                def y2mm(h, mm=m, rr=r):
                    out = None
                    for ss in range(4):
                        yg = y1g[:, mm * 512 + ss * 128: mm * 512 + (ss + 1) * 128]
                        st, sp = (mm == 0), (mm == NM - 1)
                        h.matmul(py2[ss][:, 0:512], yg, w2blk[rr][:, 0:512],
                                 start=st, stop=sp)
                        out = h.matmul(py2[ss][:, 512:1024], yg, w2blk[rr][:, 512:1024],
                                       start=st, stop=sp)
                    return out
                sc.emit("T", y2mm, reads=[b_y1g[m], b_w2blk[r]],
                        writes=b_py2 if m == NM - 1 else [],
                        force_waits=[t_S_gelu] if m == 0 else ())
            for ss in range(4):
                i = q * 4 + ss
                r = i % 2
                load(acc[r][:], acc_t[i], [b_acc[r]], reads=[b_accd[i]])

                sc.emit("V", lambda h, sss=ss, rr=r: h.scalar_tensor_tensor(
                    scr[rr][:], py2[sss][:], 1.0, b2rep[:], op0=ALU.mult, op1=ALU.add),
                    reads=[b_py2[ss], b_consts], writes=[b_scr[r], b_py2[ss]])
                sc.emit("V", lambda h, rr=r: h.tensor_add(scr[rr][:], scr[rr][:], acc[rr][:]),
                    reads=[b_acc[r]], writes=[b_scr[r]])
                ln_stats(sc, scr[r], b_scr[r], scr2[r], b_scr2[r], ssum[r], b_sum[r],
                         ssq[r], b_sq[r], srstd[r], b_rstd[r], snmr[r], b_nmr[r], PLNEPS)
                sc.emit("S", lambda h, rr=r: h.activation(
                    x2r[rr][:], scr[rr][:], AF.Identity, scale=srstd[rr][:], bias=snmr[rr][:]),
                    reads=[b_scr[r], b_rstd[r], b_nmr[r]], writes=[b_x2r[r]])
                sc.emit("G", lambda h, ii=i, rr=r: h.dma_start(out=x2_t[ii], in_=x2r[rr][:]),
                        reads=[b_x2r[r]], writes=[b_x2d[i]], dma=True)
            sc.barrier(("T", "V", "S"))
            psy2.close()

        sb4.close()
        ps5 = contextlib.ExitStack()
        pf = [ps5.enter_context(nc.psum_tensor(f"pf{i}", [128, D], F32)) for i in range(3)]
        ptr5 = ps5.enter_context(nc.psum_tensor("ptr5", [128, 512], F32))
        b_pf = [Bf(), Bf()]; b_ptr5 = Bf()
        out_t = ap["out"].rearrange("(n p) d -> n p d", p=128)
        for i in range(NT):
            r = i % 2
            load(x2r[r][:], x2_t[i], [b_x2r[r]], reads=[b_x2d[i]])
            sc.emit("V", lambda h, rr=r: h.tensor_copy(x2b[rr][:], x2r[rr][:]),
                    reads=[b_x2r[r]], writes=[b_x2b[r]])
            for hf in range(2):
                def tr(h, rr=r, hh=hf):
                    out = None
                    for j in range(4):
                        c = hh * 4 + j
                        out = h.matmul(ptr5[:, j * 128:(j + 1) * 128],
                                       x2b[rr][:, c * 128:(c + 1) * 128],
                                       ident[:], start=True, stop=True)
                    return out
                sc.emit("T", tr, reads=[b_x2b[r], b_consts], writes=[b_ptr5])

                def ev(h, rr=r, hh=hf):
                    out = None
                    for j in range(4):
                        c = hh * 4 + j
                        out = h.tensor_scalar(
                            x8r[rr][:, c * 128:(c + 1) * 128],
                            ptr5[:, j * 128:(j + 1) * 128],
                            scl[:, 1:2], None, ALU.mult)
                    return out
                sc.emit("V", ev, reads=[b_ptr5, b_scl], writes=[b_x8r[r], b_ptr5])

            def ffnmm(h, rr=r):
                out = None
                for c in range(NC):
                    xs_ = x8r[rr][:, c * 128:(c + 1) * 128]
                    st, sp = (c == 0), (c == NC - 1)
                    h.matmul(pf[rr][:, 0:512], xs_, k8[:, c * D:c * D + 512],
                             start=st, stop=sp)
                    out = h.matmul(pf[rr][:, 512:1024], xs_,
                                   k8[:, c * D + 512:c * D + 1024], start=st, stop=sp)
                return out
            sc.emit("T", ffnmm, reads=[b_x8r[rx], b_k8], writes=[b_pf[r]])

            sc.emit("V", lambda h, rr=r: h.tensor_scalar(
                ytmp[rr][:], pf[rr][:], scl[:, 0:1], None, ALU.mult),
                reads=[b_pf[r], b_scl], writes=[b_ytmp[r], b_pf[r]])
            sc.emit("V", lambda h, rr=r: h.tensor_add(ytmp[rr][:], ytmp[rr][:], fbrep[:]),
                reads=[b_consts], writes=[b_ytmp[r]])
            sc.emit("V", lambda h, rr=r, r4_=i % 4: h.tensor_add(
                scr[rr][:], x2r[r4_][:], ytmp[rr][:]),
                reads=[b_x2r[i % 4], b_ytmp[r]], writes=[b_scr[r]])
            ln_stats(sc, scr[r], b_scr[r], scr2[r], b_scr2[r], ssum[r], b_sum[r],
                     ssq[r], b_sq[r], srstd[r], b_rstd[r], snmr[r], b_nmr[r], PLNEPS)
            sc.emit("S", lambda h, rr=r: h.activation(
                oring[rr][:], scr[rr][:], AF.Identity, scale=srstd[rr][:], bias=snmr[rr][:]),
                reads=[b_scr[r], b_rstd[r], b_nmr[r]], writes=[b_oring[r]])
            sc.emit("G", lambda h, ii=i, rr=r: h.dma_start(out=out_t[ii], in_=oring[rr][:]),
                    reads=[b_oring[r]], dma=True)
        sc.emit("G", lambda h: h.nop(),
                force_waits=[(f"G{j}", sc.cnt[f"G{j}"]) for j in range(Sched.NDMA)
                             if sc.cnt[f"G{j}"] > 0], noinc=True)
        ps5.close()

        sc.install(block, sems)
    return nc


NTB = SEQ // 128


def build_fused():
    nc = bass.Bass("TRN2", target_bir_lowering=False)
    ap = {}
    for name, shape, dt in [
        ("xb", [SEQ, D], F32),
        ("wq", [D, D], F32), ("wk", [D, D], F32), ("wv", [D, D], F32),
        ("g1fm", [128, NC], F32), ("b1fm", [128, NC], F32),
        ("bqfm", [128, NC], F32), ("bkrep", [128, D], F32), ("bvrep", [128, D], F32),
        ("wo", [D, D], F32), ("borep", [128, D], F32),
        ("g2fm", [128, NC], F32), ("ln2bfm", [128, NC], F32),
        ("w1", [D, DMLP], F32), ("b1mfm", [128, NM], F32),
        ("w2", [DMLP, D], F32), ("b2rep", [128, D], F32),
        ("fker", [D, D], BF16), ("fsin", [1, 1], F32), ("fsk", [1, 1], F32),
        ("fbrep", [128, D], BF16),
        ("ident", [128, 128], BF16), ("ones1", [1, 128], F32),
        ("emat", [16, NC * 128], BF16),
    ]:
        ap[name] = nc.dram_tensor(name, shape, dt, kind="ExternalInput").ap()
    ap["out"] = nc.dram_tensor("out", [TOK, D], F32, kind="ExternalOutput").ap()
    accbuf = nc.dram_tensor("accbuf", [TOK, D], F32, kind="Internal").ap()
    x2buf = nc.dram_tensor("x2buf", [TOK, D], F32, kind="Internal").ap()
    x2bb = nc.dram_tensor("x2bb", [TOK, D], BF16, kind="Internal").ap()
    w1c = nc.dram_tensor("w1c", [NM, 128, NC * 128], BF16, kind="Internal").ap()
    w2c = nc.dram_tensor("w2c", [NM, 128, D], BF16, kind="Internal").ap()

    es = contextlib.ExitStack()
    sc = Sched()
    with es:
        S_ = lambda n, sh, dt=F32: es.enter_context(nc.sbuf_tensor(n, sh, dt))
        scr2 = [S_(f"scr2_{i}", [128, D], BF16) for i in range(2)]
        ssum = [S_(f"ssum{i}", [128, 1]) for i in range(2)]
        ssq = [S_(f"ssq{i}", [128, 1]) for i in range(2)]
        srstd = [S_(f"srstd{i}", [128, 1]) for i in range(2)]
        snmr = [S_(f"snmr{i}", [128, 1]) for i in range(2)]
        ident = S_("identt", [128, 128], BF16)
        ones1 = S_("ones1s", [1, 128])
        g2 = S_("g2", [128, NC]); ln2b = S_("ln2b", [128, NC])
        b1m = S_("b1ms", [128, NM])
        fbrep = S_("fbreps", [128, D], BF16)
        sfs = S_("sfs", [1, 2])
        scl = S_("scl", [128, 3])
        kvstat = S_("kvstat", [128, NP * 128], BF16)
        denst = S_("denst", [128, NC * 16], BF16)
        wtmp = [S_(f"wtmp{i}", [128, D]) for i in range(3)]
        x8r = [S_(f"x8r{i}", [128, NC * 128], FP8) for i in range(2)]
        xfm2 = S_("xfm2", [128, NC * TOK], BF16)

        sems = {s: es.enter_context(nc.semaphore(s)) for s in sc.sem_names()}
        block = es.enter_context(nc.Block())

        Bf = lambda: Buf()
        b_scr = [Bf(), Bf()]; b_scr2 = [Bf(), Bf()]
        b_sum = [Bf(), Bf()]; b_sq = [Bf(), Bf()]
        b_rstd = [Bf(), Bf()]; b_nmr = [Bf(), Bf()]
        b_consts = Bf(); b_sfs = Bf(); b_scl = Bf()
        b_kvstat = Bf(); b_denst = Bf(); b_kvout = Bf()
        b_wtmp = [Bf() for _ in range(3)]; b_wc1k = [Bf(), Bf()]
        b_sx = [Bf(), Bf()]; b_xln = [Bf(), Bf()]
        b_xfmr = [Bf(), Bf()]
        b_phik = [Bf(), Bf()]; b_vsb = [Bf(), Bf()]
        b_xt = [Bf(), Bf()]; b_acc = [Bf(), Bf()]; b_xln2 = [Bf(), Bf(), Bf()]
        b_oring = [Bf(), Bf()]; b_x2r = [Bf() for _ in range(4)]; b_x2b = [Bf(), Bf()]
        b_ytmp = [Bf(), Bf()]
        b_w1blk = [Bf() for _ in range(8)]; b_w2blk = [Bf() for _ in range(8)]
        b_x8r = [Bf(), Bf()]
        b_k8 = Bf()
        b_xfm2 = [Bf() for _ in range(NT)]
        b_phiq = Bf()
        b_w1c = Bf(); b_w2c = Bf()
        b_wqb = Bf(); b_wkb = Bf(); b_wvb = Bf()
        b_accd = [Bf() for _ in range(NT)]
        b_x2d = [Bf() for _ in range(NT)]

        def load(dst_ap, src_ap, writes, reads=()):
            return sc.emit("L", lambda h: h.dma_start(out=dst_ap, in_=src_ap),
                           reads=list(reads), writes=list(writes), dma=True)

        def loadq(qn, dst_ap, src_ap, writes, reads=()):
            return sc.emit(qn, lambda h: h.dma_start(out=dst_ap, in_=src_ap),
                           reads=list(reads), writes=list(writes), dma=True)

        sb_ph = contextlib.ExitStack()
        phiq = sb_ph.enter_context(nc.sbuf_tensor("phiq", [128, NC * TOK], BF16))
        b_phiq = Bf()
        sb_a = contextlib.ExitStack()
        A_ = lambda n, sh, dt=F32: sb_a.enter_context(nc.sbuf_tensor(n, sh, dt))
        xfm_own = A_("xfm_own", [128, NC * TOK], BF16)
        sx = [A_(f"sx{i}", [128, D]) for i in range(3)]
        sxln = [A_(f"sxln{i}", [128, D], BF16) for i in range(2)]
        xfmr = [A_(f"xfmr{i}", [128, D], BF16) for i in range(2)]
        phik = [A_(f"phik{i}", [128, D], BF16) for i in range(3)]
        vsb = [A_(f"vsb{i}", [128, NP * 129], BF16) for i in range(3)]
        kvout = A_("kvout", [128, NP * 129])
        g1 = A_("g1", [128, NC]); b1f = A_("b1f", [128, NC])
        bqe = A_("bqe", [128, NC])
        bke = A_("bke", [128, D], BF16); bve = A_("bve", [128, D], BF16)
        b_sx = [Bf() for _ in range(3)]; b_xln = [Bf(), Bf()]; b_xfmr = [Bf(), Bf()]
        b_phik = [Bf(), Bf(), Bf()]; b_vsb = [Bf(), Bf(), Bf()]; b_kvout = Bf()
        b_phr = [Bf(), Bf()]; b_phqd = Bf()
        b_xfm_own = [Bf() for _ in range(NT)]
        sb_kv2 = contextlib.ExitStack()
        wqb = sb_kv2.enter_context(nc.sbuf_tensor("wqb", [128, NC * D], BF16))
        sb_kv1 = contextlib.ExitStack()
        wkb = sb_kv1.enter_context(nc.sbuf_tensor("wkb", [128, NC * D], BF16))
        wvb = sb_kv1.enter_context(nc.sbuf_tensor("wvb", [128, NC * D], BF16))

        xb_t = ap["xb"].rearrange("(n p) d -> n p d", p=128)
        for dst, name in [(g1[:], "g1fm"), (b1f[:], "b1fm"), (bqe[:], "bqfm"),
                          (ident[:], "ident"), (ones1[:], "ones1"),
                          (g2[:], "g2fm"),
                          (ln2b[:], "ln2bfm"), (b1m[:], "b1mfm"),
                          (fbrep[:], "fbrep")]:
            load(dst, ap[name], [b_consts])
        load(sfs[:, 0:1], ap["fsin"], [b_sfs])
        load(sfs[:, 1:2], ap["fsk"], [b_sfs])
        for i in range(3):
            load(sx[i][:], xb_t[i], [b_sx[i]])
        loadq("G", wtmp[0][:], ap["bkrep"], [b_wtmp[0]])
        sc.emit("V", lambda h: h.tensor_scalar(bke[:], wtmp[0][:], KEPS, None, ALU.add),
                reads=[b_wtmp[0]], writes=[b_consts])
        loadq("G", wtmp[1][:], ap["bvrep"], [b_wtmp[1]])
        sc.emit("V", lambda h: h.tensor_copy(bve[:], wtmp[1][:]),
                reads=[b_wtmp[1]], writes=[b_consts])
        sc.emit("V", lambda h: h.tensor_scalar(bqe[:], bqe[:], KEPS, None, ALU.add),
                reads=[b_consts], writes=[b_consts])
        for i in range(3):
            sc.emit("V", lambda h, ii=i: h.memset(
                vsb[ii][:].rearrange("p (a c) -> p a c", c=129)[:, :, 128:129], 1.0),
                writes=[b_vsb[i]])

        _wrr = [2]
        wqueues = ("S", "G", "L")
        wdst = {"wk": (wkb, b_wkb), "wv": (wvb, b_wvb)}
        for idx, (wname, c) in enumerate(
                [(n, c) for n in ("wv", "wk") for c in range(NC)]):
            wt, bw = wdst[wname]
            r = _wrr[0] % 3
            _wrr[0] += 1
            loadq(wqueues[idx % 3], wtmp[r][:],
                  ap[wname].rearrange("(c p) d -> c p d", p=128)[c], [b_wtmp[r]])
            sc.emit("G", lambda h, b=wtmp[r], w=wt, cc=c: h.dma_start(
                out=w[:, cc * D:(cc + 1) * D], in_=b[:]),
                reads=[b_wtmp[r]], writes=[bw], dma=True)

        w1_src = ap["w1"].rearrange("(c p) d -> c p d", p=128)
        w2_src = ap["w2"].rearrange("(m p) d -> m p d", p=128)
        wcast_steps = []
        _wstep = [0]

        def _mk_wq_step(c):
            def step():
                sc.emit("G", lambda h, cc=c: h.dma_start(
                    out=wqb[:, cc * D:(cc + 1) * D],
                    in_=ap["wq"].rearrange("(c p) d -> c p d", p=128)[cc]),
                    writes=[b_wqb], dma=True)
                return None
            return step

        def _mk_w1_step(c, qq):
            def step():
                sc.emit("G", lambda h, cc=c, q_=qq: h.dma_start(
                    out=w1c[q_ * 8:(q_ + 1) * 8, :, cc * 128:(cc + 1) * 128]
                    .rearrange("m p k -> p m k"),
                    in_=w1_src[cc][:, q_ * 1024:(q_ + 1) * 1024]),
                    writes=[b_w1c], dma=True)
                return None
            return step

        def _mk_w2_step(m):
            def step():
                r = _wstep[0] % 3
                _wstep[0] += 1
                loadq("S", wtmp[r][:], w2_src[m], [b_wtmp[r]])

                def cast():
                    sc.emit("G", lambda h, mm=m, rr=r: h.dma_start(
                        out=w2c[mm], in_=wtmp[rr][:]),
                        reads=[b_wtmp[r]], writes=[b_w2c], dma=True)
                return cast
            return step
        for c in range(NC):
            wcast_steps.append(_mk_wq_step(c))
        w1l = [_mk_w1_step(c, qq) for c in range(NC) for qq in range(4)]
        w2l = [_mk_w2_step(m) for m in range(NM)]
        for a, b in zip(w1l, w2l):
            wcast_steps.append(a)
            wcast_steps.append(b)

        def mkscale2(h):
            h.tensor_tensor(kvout[0:1, 0:1], sfs[:, 0:1], sfs[:, 1:2], ALU.mult)
            h.reciprocal(kvout[0:1, 1:2], sfs[:, 0:1])
            return h.reciprocal(kvout[0:1, 2:3], sfs[:, 1:2])
        sc.emit("V", mkscale2, reads=[b_sfs], writes=[b_kvout])
        ps0 = contextlib.ExitStack()
        pscl = ps0.enter_context(nc.psum_tensor("pscl", [128, 512], F32))
        b_pscl = Bf()

        def sclmm(h):
            h.matmul(pscl[:, 0:1], ones1[:], kvout[0:1, 0:1], start=True, stop=True)
            h.matmul(pscl[:, 1:2], ones1[:], kvout[0:1, 1:2], start=True, stop=True)
            return h.matmul(pscl[:, 2:3], ones1[:], kvout[0:1, 2:3], start=True, stop=True)
        sc.emit("T", sclmm, reads=[b_kvout, b_consts], writes=[b_pscl])
        sc.emit("V", lambda h: h.tensor_copy(scl[:], pscl[:, 0:3]),
                reads=[b_pscl], writes=[b_scl, b_pscl])
        sc.barrier(("T", "V"))
        ps0.close()

        ps_a = contextlib.ExitStack()
        P_ = lambda n, sh: ps_a.enter_context(nc.psum_tensor(n, sh, F32))
        ptr = P_("ptr", [128, 512])
        pk = P_("pk", [128, D])
        pv = P_("pv", [128, D])
        kvb = [P_(f"kvb{i}", [128, 512]) for i in range(3)]
        b_ptr = Bf(); b_pk = Bf(); b_pv = Bf(); b_kv = Bf()

        def fm_dst(i, c):
            if i < NT:
                return (xfm_own[:, c * TOK + i * 128: c * TOK + (i + 1) * 128],
                        b_xfm_own[i])
            return (xfmr[i % 2][:, c * 128:(c + 1) * 128], b_xfmr[i % 2])

        def emit_xload(i):
            r = i % 3
            load(sx[r][:], xb_t[i], [b_sx[r]])

        def emit_stats(i):
            r = i % 2
            ln_stats(sc, sx[i % 3], b_sx[i % 3], scr2[r], b_scr2[r], ssum[r],
                     b_sum[r], ssq[r], b_sq[r], srstd[r], b_rstd[r], snmr[r],
                     b_nmr[r], LNEPS)

        def emit_xln(i):
            r = i % 2
            sc.emit("S", lambda h, rr=r, r3=i % 3: h.activation(
                sxln[rr][:], sx[r3][:], AF.Identity, scale=srstd[rr][:],
                bias=snmr[rr][:]),
                reads=[b_sx[i % 3], b_rstd[r], b_nmr[r]], writes=[b_xln[r]])

        def emit_tr_half(i, half):
            r = i % 2
            bufs = [fm_dst(i, c) for c in range(NC)]

            def tr(h, rr=r, hf=half):
                out = None
                for j in range(4):
                    c = hf * 4 + j
                    out = h.matmul(ptr[:, j * 128:(j + 1) * 128],
                                   sxln[rr][:, c * 128:(c + 1) * 128],
                                   ident[:], start=True, stop=True)
                return out
            sc.emit("T", tr, reads=[b_xln[r], b_consts], writes=[b_ptr])

            wb = sorted({id(b): b for _, b in bufs[half * 4:(half + 1) * 4]}.values(),
                        key=id)
            if half == 0:
                def ev(h, ii=i, hf=half):
                    out = None
                    for j in range(4):
                        c = hf * 4 + j
                        out = h.activation(
                            fm_dst(ii, c)[0],
                            ptr[:, j * 128:(j + 1) * 128],
                            AF.Identity, scale=g1[:, c:c + 1], bias=b1f[:, c:c + 1])
                    return out
                sc.emit("S", ev, reads=[b_ptr, b_consts], writes=list(wb) + [b_ptr])
            else:
                def evv(h, ii=i, hf=half):
                    out = None
                    for j in range(4):
                        c = hf * 4 + j
                        out = h.tensor_scalar(
                            fm_dst(ii, c)[0],
                            ptr[:, j * 128:(j + 1) * 128],
                            g1[:, c:c + 1], b1f[:, c:c + 1], ALU.mult, ALU.add)
                    return out
                sc.emit("V", evv, reads=[b_ptr, b_consts], writes=list(wb) + [b_ptr])

        def emit_kvproj(i):
            rb = sorted({id(b): b for _, b in (fm_dst(i, c) for c in range(NC))}.values(),
                        key=id)
            if i < 3:
                def vproj(h, ii=i):
                    out = None
                    for c in range(NC):
                        xs_ = fm_dst(ii, c)[0]
                        st, sp = (c == 0), (c == NC - 1)
                        h.matmul(pv[:, 0:512], xs_, wvb[:, c * D:c * D + 512], start=st, stop=sp)
                        out = h.matmul(pv[:, 512:1024], xs_, wvb[:, c * D + 512:c * D + 1024], start=st, stop=sp)
                    return out

                def kproj(h, ii=i):
                    out = None
                    for c in range(NC):
                        xs_ = fm_dst(ii, c)[0]
                        st, sp = (c == 0), (c == NC - 1)
                        h.matmul(pk[:, 0:512], xs_, wkb[:, c * D:c * D + 512], start=st, stop=sp)
                        out = h.matmul(pk[:, 512:1024], xs_, wkb[:, c * D + 512:c * D + 1024], start=st, stop=sp)
                    return out
                sc.emit("T", vproj, reads=list(rb) + [b_wvb], writes=[b_pv])
                sc.emit("T", kproj, reads=list(rb) + [b_wkb], writes=[b_pk])
                return

            def kvproj(h, ii=i):
                out = None
                for c in range(NC):
                    xs_ = fm_dst(ii, c)[0]
                    st, sp = (c == 0), (c == NC - 1)
                    h.matmul(pk[:, 0:512], xs_, wkb[:, c * D:c * D + 512], start=st, stop=sp)
                    h.matmul(pk[:, 512:1024], xs_, wkb[:, c * D + 512:c * D + 1024], start=st, stop=sp)
                    h.matmul(pv[:, 0:512], xs_, wvb[:, c * D:c * D + 512], start=st, stop=sp)
                    out = h.matmul(pv[:, 512:1024], xs_, wvb[:, c * D + 512:c * D + 1024], start=st, stop=sp)
                return out
            sc.emit("T", kvproj, reads=list(rb) + [b_wkb, b_wvb],
                    writes=[b_pk, b_pv])

        def emit_kvev_v(i):
            r = i % 3
            sc.emit("V", lambda h, rr=r: h.scalar_tensor_tensor(
                vsb[rr][:].rearrange("p (a c) -> p a c", c=129)[:, :, 0:128],
                pv[:].rearrange("p (a c) -> p a c", c=128),
                1.0, bve[:].rearrange("p (a c) -> p a c", c=128),
                op0=ALU.mult, op1=ALU.add),
                reads=[b_pv, b_consts], writes=[b_vsb[r], b_pv])

        def emit_kvev_k(i):
            r = i % 3
            rs = i % 2

            def kadd(h, rr=rs):
                with nc.allow_low_precision(reason="phik consumed in bf16 anyway"):
                    return h.scalar_tensor_tensor(
                        scr2[rr][:], pk[:], 1.0, bke[:], op0=ALU.mult, op1=ALU.add)
            sc.emit("V", kadd, reads=[b_pk, b_consts], writes=[b_scr2[rs]])
            sc.emit("V", lambda h, rr=rs, r3=r: h.tensor_scalar(
                phik[r3][:], scr2[rr][:], KEPS, None, ALU.max),
                reads=[b_scr2[rs]], writes=[b_phik[r], b_pk])

        def emit_kvacc_half(i, half):
            r = i % 3
            def kvacc(h, rr=r, hf=half):
                out = None
                for p in range(hf * 4, hf * 4 + 4):
                    bank, q = divmod(p, 3)
                    out = h.matmul(kvb[bank][:, q * 129:q * 129 + 129],
                                   phik[rr][:, p * 128:(p + 1) * 128],
                                   vsb[rr][:, p * 129:(p + 1) * 129],
                                   start=True, stop=True)
                return out
            sc.emit("T", kvacc, reads=[b_phik[r], b_vsb[r]], writes=[b_kv])

        def emit_kvsum(i):
            if i == 0:
                def kvinit(h):
                    h.tensor_copy(kvout[:, 0:387], kvb[0][:, 0:387])
                    h.tensor_copy(kvout[:, 387:774], kvb[1][:, 0:387])
                    return h.tensor_copy(kvout[:, 774:1032], kvb[2][:, 0:258])
                sc.emit("V", kvinit, reads=[b_kv], writes=[b_kvout, b_kv])
            else:
                def kvadd(h):
                    h.tensor_add(kvout[:, 0:387], kvout[:, 0:387], kvb[0][:, 0:387])
                    h.tensor_add(kvout[:, 387:774], kvout[:, 387:774], kvb[1][:, 0:387])
                    return h.tensor_add(kvout[:, 774:1032], kvout[:, 774:1032],
                                        kvb[2][:, 0:258])
                sc.emit("V", kvadd, reads=[b_kv], writes=[b_kvout, b_kv])

        wpend = []
        for j in range(NTB + 5):
            if 2 <= j < NTB + 2:
                emit_tr_half(j - 2, 0)
            if 5 <= j < NTB + 5:
                emit_kvacc_half(j - 5, 0)
                emit_kvacc_half(j - 5, 1)
            if 2 <= j < NTB + 2:
                emit_tr_half(j - 2, 1)
            if 1 <= j < NTB + 1:
                emit_xln(j - 1)
            if 5 <= j < NTB + 5:
                emit_kvsum(j - 5)
            if j < NTB:
                emit_stats(j)
            if 3 <= j < NTB + 3:
                emit_kvproj(j - 3)
                emit_kvev_v(j - 3)
                emit_kvev_k(j - 3)
            if 1 <= j and j + 2 < NTB:
                emit_xload(j + 2)
            for fn in wpend:
                fn()
            wpend = []
            if wcast_steps:
                extra = wcast_steps.pop(0)()
                if extra:
                    wpend.append(extra)
        while wcast_steps or wpend:
            for fn in wpend:
                fn()
            wpend = []
            if wcast_steps:
                extra = wcast_steps.pop(0)()
                if extra:
                    wpend.append(extra)

        sb_kv1.close()
        sc.barrier(("T", "V", "S"))
        ps_a.close()
        ps_q = contextlib.ExitStack()
        pq2 = [ps_q.enter_context(nc.psum_tensor(f"pq{i}", [128, 1024], F32))
               for i in range(2)]
        b_pq2 = [Bf(), Bf()]
        for mchunk in range(NC):
            for hh in range(2):
                rr_ = (mchunk * 2 + hh) % 2

                def qproj(h, mm=mchunk, h2=hh, rr=rr_):
                    out = None
                    for c in range(NC):
                        w_ = wqb[:, c * D + mm * 128: c * D + (mm + 1) * 128]
                        for t4 in range(2):
                            t0 = h2 * 1024 + t4 * 512
                            out = h.matmul(pq2[rr][:, t4 * 512:(t4 + 1) * 512], w_,
                                           xfm_own[:, c * TOK + t0: c * TOK + t0 + 512],
                                           start=(c == 0), stop=(c == NC - 1))
                    return out
                sc.emit("T", qproj, reads=[b_wqb] + b_xfm_own, writes=[b_pq2[rr_]])
                sc.emit("V", lambda h, mm=mchunk, rr=rr_, h2=hh: h.tensor_scalar(
                    phiq[:, mm * TOK + h2 * 1024: mm * TOK + (h2 + 1) * 1024],
                    pq2[rr][:], bqe[:, mm:mm + 1], KEPS, ALU.add, ALU.max),
                    reads=[b_pq2[rr_], b_consts], writes=[b_phiq, b_pq2[rr_]])

        sc.emit("V", lambda h: h.memset(kvstat[:], 0.0), writes=[b_kvstat])
        sc.emit("V", lambda h: h.memset(denst[:], 0.0), writes=[b_denst])

        def mkkvstat(h):
            out = None
            for p in range(NP):
                h.tensor_copy(kvstat[0:64, p * 128:p * 128 + 64],
                              kvout[0:64, p * 129:p * 129 + 64])
                out = h.tensor_copy(kvstat[64:128, p * 128 + 64:p * 128 + 128],
                                    kvout[64:128, p * 129 + 64:p * 129 + 128])
            return out
        sc.emit("V", mkkvstat, reads=[b_kvout], writes=[b_kvstat])

        def mkdenst(h):
            out = None
            for c in range(NC):
                h.tensor_copy(denst[0:64, c * 16 + 2 * c: c * 16 + 2 * c + 1],
                              kvout[0:64, c * 129 + 128: c * 129 + 129])
                out = h.tensor_copy(denst[64:128, c * 16 + 2 * c + 1: c * 16 + 2 * c + 2],
                                    kvout[64:128, c * 129 + 128: c * 129 + 129])
            return out
        sc.emit("V", mkdenst, reads=[b_kvout], writes=[b_denst])

        sc.barrier(("T", "V", "S", "L"))
        ps_q.close()
        sb_kv2.close()
        sb_a.close()
        ps1 = contextlib.ExitStack()
        pb = ps1.enter_context(nc.psum_tensor("pb", [128, 512], F32))
        pn = ps1.enter_context(nc.psum_tensor("pn", [128, 512], F32))
        po = [ps1.enter_context(nc.psum_tensor(f"po{i}", [128, D], F32)) for i in range(2)]
        ptr3 = ps1.enter_context(nc.psum_tensor("ptr3", [128, 1024], F32))
        b_pb = Bf(); b_pn = Bf(); b_po = [Bf(), Bf()]; b_ptr3 = [Bf(), Bf()]
        sb_b = contextlib.ExitStack()
        wob = sb_b.enter_context(nc.sbuf_tensor("wob", [128, NC * D], BF16))
        borep = sb_b.enter_context(nc.sbuf_tensor("boreps", [128, D], F32))
        rden = sb_b.enter_context(nc.sbuf_tensor("rden", [16, 512], BF16))
        emat = sb_b.enter_context(nc.sbuf_tensor("emats", [16, NC * 128], BF16))
        bcs = [sb_b.enter_context(nc.sbuf_tensor(f"bcs{i}", [128, 512], F32))
               for i in range(2)]
        atr = [sb_b.enter_context(nc.sbuf_tensor(f"atr{i}", [128, NC * 512], BF16))
               for i in range(2)]
        scr = [sb_b.enter_context(nc.sbuf_tensor(f"scrA{i}", [128, D], F32))
               for i in range(2)]
        acc = [sb_b.enter_context(nc.sbuf_tensor(f"accA{i}", [128, D], F32))
               for i in range(2)]
        xt = [sb_b.enter_context(nc.sbuf_tensor(f"xt{i}", [128, D], F32))
              for i in range(2)]
        xln2 = [sb_b.enter_context(nc.sbuf_tensor(f"xln2{i}", [128, D], BF16))
                for i in range(3)]
        b_wob = Bf(); b_rden = Bf(); b_bcs = [Bf(), Bf()]; b_atr = [Bf(), Bf()]
        load(emat[:], ap["emat"], [b_consts])
        load(borep[:], ap["borep"], [b_consts])
        wo_src = ap["wo"].rearrange("(c p) d -> c p d", p=128)
        for c in range(NC):
            r = c % 3
            load(wtmp[r][:], wo_src[c], [b_wtmp[r]])
            sc.emit("S", lambda h, cc=c, rr=r: h.activation(
                wob[:, cc * D:(cc + 1) * D], wtmp[rr][:], AF.Identity),
                reads=[b_wtmp[r]], writes=[b_wob])

        acc_t = accbuf.rearrange("(n p) d -> n p d", p=128)

        def emit_attn_group(t4):
            gr = t4 % 2

            def phq(c):
                return phiq[:, c * TOK + t4 * 512: c * TOK + (t4 + 1) * 512]

            def denmm(h):
                out = None
                for c in range(NC):
                    out = h.matmul(pb[0:16, :], denst[:, c * 16:(c + 1) * 16],
                                   phq(c), start=(c == 0), stop=(c == NC - 1))
                return out
            sc.emit("T", denmm, reads=[b_denst, b_phiq], writes=[b_pb])

            def recip_b(h):
                with nc.allow_low_precision(reason="bf16 1/den as bcast operand"):
                    return h.reciprocal(rden[:], pb[0:16, :])
            sc.emit("V", recip_b, reads=[b_pb], writes=[b_rden, b_pb])
            for c in range(NC):
                r = c % 2
                sc.emit("T", lambda h, cc=c: h.matmul(
                    pb[:], emat[:, cc * 128:(cc + 1) * 128],
                    rden[:], start=True, stop=True),
                    reads=[b_consts, b_rden], writes=[b_pb])
                sc.emit("T", lambda h, cc=c: h.matmul(
                    pn[:], kvstat[:, cc * 128:(cc + 1) * 128],
                    phq(cc), start=True, stop=True),
                    reads=[b_kvstat, b_phiq], writes=[b_pn])
                sc.emit("S", lambda h, rr=r: h.copy(bcs[rr][:], pb[:]),
                        reads=[b_pb], writes=[b_bcs[r], b_pb])
                sc.emit("V", lambda h, cc=c, rr=r, g=gr: h.tensor_mul(
                    atr[g][:, cc * 512:(cc + 1) * 512],
                    pn[:], bcs[rr][:]),
                    reads=[b_pn, b_bcs[r]], writes=[b_atr[gr], b_pn])

        def emit_womm(i):
            r = i % 2
            gr = (i // 4) % 2
            js = (i % 4) * 128

            def womm(h, gg=gr, jj=js, rr=r):
                out = None
                for c in range(NC):
                    a_ = atr[gg][:, c * 512 + jj: c * 512 + jj + 128]
                    st, sp = (c == 0), (c == NC - 1)
                    h.matmul(po[rr][:, 0:512], a_, wob[:, c * D:c * D + 512],
                             start=st, stop=sp)
                    out = h.matmul(po[rr][:, 512:1024], a_,
                                   wob[:, c * D + 512:c * D + 1024], start=st, stop=sp)
                return out
            sc.emit("T", womm, reads=[b_atr[gr], b_wob], writes=[b_po[r]])

        def emit_tile_chain(i):
            r = i % 2
            r3 = i % 3
            load(xt[r][:], xb_t[i], [b_xt[r]])
            sc.emit("V", lambda h, rr=r: h.scalar_tensor_tensor(
                scr[rr][:], po[rr][:], 1.0, borep[:], op0=ALU.mult, op1=ALU.add),
                reads=[b_po[r], b_consts], writes=[b_scr[r], b_po[r]])
            sc.emit("V", lambda h, rr=r: h.tensor_add(scr[rr][:], scr[rr][:], xt[rr][:]),
                reads=[b_xt[r]], writes=[b_scr[r]])
            sc.emit("V", lambda h, rr=r: h.tensor_add(acc[rr][:], scr[rr][:], xt[rr][:]),
                reads=[b_scr[r], b_xt[r]], writes=[b_acc[r]])
            sc.emit("G", lambda h, ii=i, rr=r: h.dma_start(out=acc_t[ii], in_=acc[rr][:]),
                    reads=[b_acc[r]], writes=[b_accd[i]], dma=True)
            ln_stats(sc, scr[r], b_scr[r], scr2[r], b_scr2[r], ssum[r], b_sum[r],
                     ssq[r], b_sq[r], srstd[r], b_rstd[r], snmr[r], b_nmr[r], LNEPS)
            sc.emit("S", lambda h, rr=r, rx=r3: h.activation(
                xln2[rx][:], scr[rr][:], AF.Identity, scale=srstd[rr][:], bias=snmr[rr][:]),
                reads=[b_scr[r], b_rstd[r], b_nmr[r]], writes=[b_xln2[r3]])

        def emit_tr3(i, half):
            r = i % 3

            def tr(h, rr=r, hf=half):
                out = None
                for j in range(4):
                    c = hf * 4 + j
                    out = h.matmul(ptr3[:, hf * 512 + j * 128: hf * 512 + (j + 1) * 128],
                                   xln2[rr][:, c * 128:(c + 1) * 128],
                                   ident[:], start=True, stop=True)
                return out
            sc.emit("T", tr, reads=[b_xln2[r], b_consts], writes=[b_ptr3[half]])

            def ev(h, ii=i, hf=half):
                out = None
                for j in range(4):
                    c = hf * 4 + j
                    out = h.activation(
                        xfm2[:, c * TOK + ii * 128: c * TOK + (ii + 1) * 128],
                        ptr3[:, hf * 512 + j * 128: hf * 512 + (j + 1) * 128],
                        AF.Identity, scale=g2[:, c:c + 1], bias=ln2b[:, c:c + 1])
                return out
            sc.emit("S", ev, reads=[b_ptr3[half], b_consts],
                    writes=[b_xfm2[i], b_ptr3[half]])

        emit_attn_group(0)
        emit_attn_group(1)
        for i in range(NT + 2):
            if i < NT:
                emit_womm(i)
                if i % 4 == 3 and (i // 4) + 2 <= 3:
                    emit_attn_group((i // 4) + 2)
            if 2 <= i:
                emit_tr3(i - 2, 0)
                emit_tr3(i - 2, 1)
            if i < NT:
                emit_tile_chain(i)

        sc.barrier(("T", "V", "S", "L", "G"))
        sb_b.close()
        sb_ph.close()
        ps1.close()
        sbk8 = contextlib.ExitStack()
        k8 = sbk8.enter_context(nc.sbuf_tensor("k8", [128, NC * D], FP8))
        x2r = [sbk8.enter_context(nc.sbuf_tensor(f"x2r{i}", [128, D], F32))
               for i in range(4)]
        wc1k = [sbk8.enter_context(nc.sbuf_tensor(f"wc1k{i}", [128, D], BF16))
                for i in range(2)]
        accB = [sbk8.enter_context(nc.sbuf_tensor(f"accB{i}", [128, D], F32))
                for i in range(2)]
        scr5 = [sbk8.enter_context(nc.sbuf_tensor(f"scr5{i}", [128, D], F32))
                for i in range(2)]
        b_accB = [Bf(), Bf()]; b_scr5 = [Bf(), Bf()]
        sb4 = contextlib.ExitStack()
        y1g = sb4.enter_context(nc.sbuf_tensor("y1g", [128, NM * 512], BF16))
        b2rep = sb4.enter_context(nc.sbuf_tensor("b2reps", [128, D], F32))
        y2o = [sb4.enter_context(nc.sbuf_tensor(f"y2o{i}", [128, D], F32))
               for i in range(4)]
        b_y2o = [Bf() for _ in range(4)]
        fk_src = ap["fker"].rearrange("(c p) d -> c p d", p=128)
        for c in range(NC):
            r = c % 2
            load(wc1k[r][:], fk_src[c], [b_wc1k[r]])
            sc.emit("V", lambda h, cc=c, rr=r: h.tensor_scalar(
                k8[:, cc * D:(cc + 1) * D], wc1k[rr][:], scl[:, 2:3], None, ALU.mult),
                reads=[b_wc1k[r], b_scl], writes=[b_k8])
        w1blk = [sb4.enter_context(nc.sbuf_tensor(f"w1blk{i}", [128, NC * 128], BF16))
                 for i in range(8)]
        w2blk = [sb4.enter_context(nc.sbuf_tensor(f"w2blk{i}", [128, D], BF16))
                 for i in range(8)]
        b_y1g = [Bf() for _ in range(NM)]
        load(b2rep[:], ap["b2rep"], [b_consts])
        x2_t = x2buf.rearrange("(n p) d -> n p d", p=128)
        x2b_t = x2bb.rearrange("(n p) d -> n p d", p=128)
        b_x2bd = [Bf() for _ in range(NT)]
        qwait = [None]
        for q in range(4):
            qoff = q * 512
            psy1 = contextlib.ExitStack()
            py1 = [psy1.enter_context(nc.psum_tensor(f"py1_{q}_{i}", [128, 512], F32))
                   for i in range(2)]
            b_py1 = [Bf(), Bf()]
            for m in range(4):
                load(w1blk[m][:], w1c[m], [b_w1blk[m]], reads=[b_w1c])
            for m in range(NM):
                r = m % 8
                if m + 4 < NM:
                    load(w1blk[(m + 4) % 8][:], w1c[m + 4], [b_w1blk[(m + 4) % 8]],
                         reads=[b_w1c])

                def y1mm(h, rr=r, pp=m % 2, qo=qoff):
                    out = None
                    for c in range(NC):
                        out = h.matmul(
                            py1[pp][:],
                            w1blk[rr][:, c * 128:(c + 1) * 128],
                            xfm2[:, c * TOK + qo: c * TOK + qo + 512],
                            start=(c == 0), stop=(c == NC - 1))
                    return out
                pr = m % 2
                sc.emit("T", y1mm, reads=[b_w1blk[r]] + b_xfm2, writes=[b_py1[pr]],
                        force_waits=[qwait[0]] if (m == 0 and qwait[0]) else ())
                sc.emit("S", lambda h, mm=m, pp=pr: h.activation(
                    y1g[:, mm * 512:(mm + 1) * 512],
                    py1[pp][:], AF.Gelu_apprx_tanh, bias=b1m[:, mm:mm + 1]),
                    reads=[b_py1[pr], b_consts], writes=[b_y1g[m], b_py1[pr]])
            t_S_gelu = ("S", sc.cnt["S"])
            psy1.close()
            psy2 = contextlib.ExitStack()
            py2 = [psy2.enter_context(nc.psum_tensor(f"py2_{q}_{i}", [128, D], F32))
                   for i in range(4)]
            b_py2 = [Bf() for _ in range(4)]
            for m in range(4):
                sc.emit("S", lambda h, mm=m: h.dma_start(
                    out=w2blk[mm][:], in_=w2c[mm]),
                    reads=[b_w2c], writes=[b_w2blk[m]], dma=True)
            for m in range(NM):
                r = m % 8
                if m + 4 < NM:
                    sc.emit("S", lambda h, mm=m + 4: h.dma_start(
                        out=w2blk[mm % 8][:], in_=w2c[mm]),
                        reads=[b_w2c], writes=[b_w2blk[(m + 4) % 8]], dma=True)

                def y2mm(h, mm=m, rr=r):
                    out = None
                    for ss in range(4):
                        yg = y1g[:, mm * 512 + ss * 128: mm * 512 + (ss + 1) * 128]
                        st, sp = (mm == 0), (mm == NM - 1)
                        h.matmul(py2[ss][:, 0:512], yg, w2blk[rr][:, 0:512],
                                 start=st, stop=sp)
                        out = h.matmul(py2[ss][:, 512:1024], yg, w2blk[rr][:, 512:1024],
                                       start=st, stop=sp)
                    return out
                sc.emit("T", y2mm, reads=[b_y1g[m], b_w2blk[r]],
                        writes=b_py2 if m == NM - 1 else [],
                        force_waits=[t_S_gelu] if m == 0 else ())
            for ss in range(4):
                i = q * 4 + ss
                r = i % 2
                load(accB[r][:], acc_t[i], [b_accB[r]], reads=[b_accd[i]])
                sc.emit("V", lambda h, sss=ss, rr=r: h.scalar_tensor_tensor(
                    y2o[sss][:], py2[sss][:], 1.0, b2rep[:], op0=ALU.mult, op1=ALU.add),
                    reads=[b_py2[ss], b_consts], writes=[b_y2o[ss], b_py2[ss]])
                sc.emit("V", lambda h, sss=ss, rr=r: h.tensor_add(
                    y2o[sss][:], y2o[sss][:], accB[rr][:]),
                    reads=[b_accB[r]], writes=[b_y2o[ss]])
            qwait[0] = ("V", sc.cnt["V"])
            for ss in range(4):
                i = q * 4 + ss
                r = i % 2
                ln_stats(sc, y2o[ss], b_y2o[ss], scr2[r], b_scr2[r], ssum[r], b_sum[r],
                         ssq[r], b_sq[r], srstd[r], b_rstd[r], snmr[r], b_nmr[r], PLNEPS)
                r4 = i % 4
                sc.emit("V", lambda h, sss=ss, rr=r, r4_=r4: h.tensor_scalar(
                    x2r[r4_][:], y2o[sss][:], srstd[rr][:], snmr[rr][:],
                    ALU.mult, ALU.add),
                    reads=[b_y2o[ss], b_rstd[r], b_nmr[r]], writes=[b_x2r[r4]])
                sc.emit("G", lambda h, ii=i, r4_=r4: h.dma_start(out=x2_t[ii], in_=x2r[r4_][:]),
                        reads=[b_x2r[r4]], writes=[b_x2d[i]], dma=True)
                sc.emit("G", lambda h, ii=i, r4_=r4: h.dma_start(
                    out=x2b_t[ii], in_=x2r[r4_][:]),
                    reads=[b_x2r[r4]], writes=[b_x2bd[i]], dma=True)
            psy2.close()

        sb4.close()
        sb5 = contextlib.ExitStack()
        oring = [sb5.enter_context(nc.sbuf_tensor(f"oring{i}", [128, D], F32))
                 for i in range(2)]
        x2b = [sb5.enter_context(nc.sbuf_tensor(f"x2b{i}", [128, D], BF16))
               for i in range(4)]
        ytmp = [sb5.enter_context(nc.sbuf_tensor(f"ytmp{i}", [128, D], BF16))
               for i in range(2)]
        b_oring = [Bf(), Bf()]; b_x2b = [Bf() for _ in range(4)]
        ps5 = contextlib.ExitStack()
        pf = [ps5.enter_context(nc.psum_tensor(f"pf{i}", [128, D], F32)) for i in range(3)]
        ptr5 = [ps5.enter_context(nc.psum_tensor(f"ptr5{i}", [128, 512], F32)) for i in range(2)]
        b_pf = [Bf() for _ in range(3)]; b_ptr5 = [Bf(), Bf()]
        t_b5 = qwait[0]
        out_t = ap["out"].rearrange("(n p) d -> n p d", p=128)

        def emit_x8_pre(i):
            r4 = i % 4
            load(x2r[r4][:], x2_t[i], [b_x2r[r4]], reads=[b_x2d[i]])
            sc.emit("S", lambda h, r4_=r4, ii=i: h.dma_start(
                out=x2b[r4_][:], in_=x2b_t[ii]),
                reads=[b_x2bd[i]], writes=[b_x2b[r4]], dma=True)

        def emit_x8(i):
            r = i % 2
            r4 = i % 4
            for hf in range(2):
                def tr(h, rr=r4, hh=hf):
                    out = None
                    for j in range(4):
                        c = hh * 4 + j
                        out = h.matmul(ptr5[hh][:, j * 128:(j + 1) * 128],
                                       x2b[rr][:, c * 128:(c + 1) * 128],
                                       ident[:], start=True, stop=True)
                    return out
                sc.emit("T", tr, reads=[b_x2b[r4], b_consts], writes=[b_ptr5[hf]],
                        force_waits=[t_b5] if (i == 0 and hf == 0 and t_b5) else ())

                if hf == 0:
                    def ev(h, rr=r, hh=hf):
                        return h.tensor_scalar(
                            x8r[rr][:, hh * 512:(hh + 1) * 512],
                            ptr5[hh][:], scl[:, 1:2], None, ALU.mult)
                    sc.emit("V", ev, reads=[b_ptr5[hf], b_scl],
                            writes=[b_x8r[r], b_ptr5[hf]])
                else:
                    def evs(h, rr=r, hh=hf):
                        return h.activation(
                            x8r[rr][:, hh * 512:(hh + 1) * 512],
                            ptr5[hh][:], AF.Identity, scale=scl[:, 1:2])
                    sc.emit("S", evs, reads=[b_ptr5[hf], b_scl],
                            writes=[b_x8r[r], b_ptr5[hf]])

        def emit_ffn(i):
            r = i % 3
            rx = i % 2

            def ffnmm(h, rr=r, rx_=rx):
                out = None
                for c in range(NC):
                    xs_ = x8r[rx_][:, c * 128:(c + 1) * 128]
                    st, sp = (c == 0), (c == NC - 1)
                    h.matmul(pf[rr][:, 0:512], xs_, k8[:, c * D:c * D + 512],
                             start=st, stop=sp)
                    out = h.matmul(pf[rr][:, 512:1024], xs_,
                                   k8[:, c * D + 512:c * D + 1024], start=st, stop=sp)
                return out
            sc.emit("T", ffnmm, reads=[b_x8r[rx], b_k8], writes=[b_pf[r]])

        def emit_out_chain(i):
            r = i % 2
            p3 = i % 3
            sc.emit("S", lambda h, rr=r, pp=p3: h.activation(
                ytmp[rr][:], pf[pp][:], AF.Identity, scale=scl[:, 0:1]),
                reads=[b_pf[p3], b_scl], writes=[b_ytmp[r], b_pf[p3]])
            sc.emit("V", lambda h, rr=r: h.tensor_add(ytmp[rr][:], ytmp[rr][:], fbrep[:]),
                reads=[b_consts], writes=[b_ytmp[r]])
            sc.emit("V", lambda h, rr=r, r4_=i % 4: h.tensor_add(
                scr5[rr][:], x2r[r4_][:], ytmp[rr][:]),
                reads=[b_x2r[i % 4], b_ytmp[r]], writes=[b_scr5[r]])
            ln_stats(sc, scr5[r], b_scr5[r], scr2[r], b_scr2[r], ssum[r], b_sum[r],
                     ssq[r], b_sq[r], srstd[r], b_rstd[r], snmr[r], b_nmr[r], PLNEPS)
            sc.emit("S", lambda h, rr=r: h.activation(
                oring[rr][:], scr5[rr][:], AF.Identity, scale=srstd[rr][:], bias=snmr[rr][:]),
                reads=[b_scr5[r], b_rstd[r], b_nmr[r]], writes=[b_oring[r]])
            sc.emit("G", lambda h, ii=i, rr=r: h.dma_start(out=out_t[ii], in_=oring[rr][:]),
                    reads=[b_oring[r]], dma=True)

        emit_x8_pre(0)
        emit_x8_pre(1)
        for i in range(NT + 1):
            if i + 2 < NT:
                emit_x8_pre(i + 2)
            if i >= 1:
                emit_ffn(i - 1)
            if i < NT:
                emit_x8(i)
            if i >= 1:
                emit_out_chain(i - 1)
        sc.emit("G", lambda h: h.nop(),
                force_waits=[(f"G{j}", sc.cnt[f"G{j}"]) for j in range(Sched.NDMA)
                             if sc.cnt[f"G{j}"] > 0], noinc=True)
        ps5.close()
        sb5.close()
        sbk8.close()

        sc.install(block, sems)
    return nc


def _fm(vec, nchunk):
    return np.ascontiguousarray(np.asarray(vec).reshape(nchunk, 128).T)


def _rep(vec, dt=np.float32):
    v = np.asarray(vec).reshape(1, -1)
    return np.ascontiguousarray(np.broadcast_to(v, (128, v.shape[1])).astype(dt, copy=False))


def _emat():
    e = np.zeros((16, NC * 128), dtype=BF16_NP)
    for c in range(NC):
        e[2 * c, c * 128: c * 128 + 64] = 1
        e[2 * c + 1, c * 128 + 64: c * 128 + 128] = 1
    return e


def kernel(**inp):
    x = np.asarray(inp["L0_attn_inputs"])
    wq = np.ascontiguousarray(np.asarray(inp["L0_attn_Wq"]).reshape(D, D))
    wk = np.ascontiguousarray(np.asarray(inp["L0_attn_Wk"]).reshape(D, D))
    wv = np.ascontiguousarray(np.asarray(inp["L0_attn_Wv"]).reshape(D, D))
    wo = np.ascontiguousarray(np.asarray(inp["L0_attn_Wo"]).reshape(D, D))
    w1 = np.ascontiguousarray(np.asarray(inp["L0_attn_W1"]))
    w2 = np.ascontiguousarray(np.asarray(inp["L0_attn_W2"]))
    g1fm = _fm(inp["L0_attn_ln1_scale"], NC)
    b1fm = _fm(inp["L0_attn_ln1_bias"], NC)
    g2fm = _fm(inp["L0_attn_ln2_scale"], NC)
    ln2bfm = _fm(inp["L0_attn_ln2_bias"], NC)
    bqfm = _fm(np.asarray(inp["L0_attn_bq"]).reshape(D), NC)
    bkrep = _rep(np.asarray(inp["L0_attn_bk"]).reshape(D))
    bvrep = _rep(np.asarray(inp["L0_attn_bv"]).reshape(D))
    borep = _rep(np.asarray(inp["L0_attn_bo"]))
    b2rep = _rep(np.asarray(inp["L0_attn_b2"]))
    b1mfm = _fm(np.asarray(inp["L0_attn_b1"]), NM)
    fker = np.ascontiguousarray(np.asarray(inp["L0_ffn_kernel"]))
    fsin = np.ascontiguousarray(np.asarray(inp["L0_ffn_input_scale"]).reshape(1, 1))
    fsk = np.ascontiguousarray(np.asarray(inp["L0_ffn_kernel_scale"]).reshape(1, 1))
    fbrep = _rep(np.asarray(inp["L0_ffn_bias"]), BF16_NP)
    ident = np.eye(128, dtype=BF16_NP)
    ones1 = np.ones((1, 128), dtype=np.float32)
    emat = _emat()

    xbs = []
    for c in range(8):
        b, q = c // 4, c % 4
        xbs.append(np.ascontiguousarray(
            np.concatenate([x[b, q * TOK:], x[b, :q * TOK]], axis=0)))

    ncF = build_fused()
    in_maps = [{"xb": xbs[c], "wq": wq, "wk": wk, "wv": wv, "g1fm": g1fm,
                "b1fm": b1fm, "bqfm": bqfm, "bkrep": bkrep, "bvrep": bvrep,
                "wo": wo, "borep": borep, "g2fm": g2fm, "ln2bfm": ln2bfm,
                "w1": w1, "b1mfm": b1mfm, "w2": w2, "b2rep": b2rep,
                "fker": fker, "fsin": fsin, "fsk": fsk, "fbrep": fbrep,
                "ident": ident, "ones1": ones1, "emat": emat}
               for c in range(8)]
    res = run_bass_kernel_spmd(ncF, in_maps, core_ids=list(range(8)))
    outs = [np.asarray(res.results[c]["out"]) for c in range(8)]

    full = np.empty((BATCH, SEQ, D), dtype=np.float32)
    for c in range(8):
        full[c // 4, (c % 4) * TOK:(c % 4 + 1) * TOK] = outs[c]
    return full


# version 63
# speedup vs baseline: 1.0074x; 1.0074x over previous
import contextlib
from collections import defaultdict

import numpy as np
import ml_dtypes
import concourse.bass as bass
import concourse.mybir as mybir
from concourse.bass_utils import run_bass_kernel_spmd

F32 = mybir.dt.float32
BF16 = mybir.dt.bfloat16
FP8 = mybir.dt.float8e4
AF = mybir.ActivationFunctionType
ALU = mybir.AluOpType
AX = mybir.AxisListType

BATCH, SEQ, D, H, HD, DMLP = 2, 8192, 1024, 16, 64, 4096
TOK = 2048
NT = TOK // 128
NC = D // 128
NM = DMLP // 128
NP = 8
KEPS = 1e-3
LNEPS = 1e-6
PLNEPS = 1e-5
BF16_NP = ml_dtypes.bfloat16


class Buf:
    __slots__ = ("w", "r")

    def __init__(self):
        self.w = None
        self.r = []


class Sched:
    ENGS = ("T", "V", "S", "G", "L")
    NDMA = 4

    def __init__(self):
        self.q = {e: [] for e in self.ENGS}
        self.cnt = defaultdict(int)
        self.last_wait = {e: defaultdict(int) for e in self.ENGS}
        self.dma_idx = defaultdict(int)

    def sem_names(self):
        names = list(self.ENGS)
        for qn in ("L", "G", "S"):
            names += [f"{qn}{j}" for j in range(self.NDMA)]
        return names

    def emit(self, eng, body, reads=(), writes=(), waits=(), dma=False,
             force_waits=(), noinc=False):
        wl = list(waits)
        for b in reads:
            if b.w is not None:
                wl.append(b.w)
        for b in writes:
            if b.w is not None:
                wl.append(b.w)
            wl.extend(b.r)
        if noinc:
            assert not reads and not writes and not dma
            wl2 = list(force_waits)
            for s, v in wl:
                if s == eng and eng in ("T", "L"):
                    continue
                if v > self.last_wait[eng][s]:
                    self.last_wait[eng][s] = v
                    wl2.append((s, v))

            def thunk0(h, sems, _w=tuple(wl2), _b=body):
                for s_, v_ in _w:
                    h.wait_ge(sems[s_], v_)
                _b(h)
            self.q[eng].append(thunk0)
            return None
        if dma:
            k = self.dma_idx[eng]
            self.dma_idx[eng] += 1
            slot_sem = f"{eng}{k % self.NDMA}"
            prev = (k // self.NDMA) * 16
            if prev > 0:
                wl.append((slot_sem, prev))
            self.cnt[slot_sem] += 16
            tok = self.cnt[slot_sem]
            sem_name = slot_sem
            amt = 16
        else:
            self.cnt[eng] += 1
            tok = self.cnt[eng]
            sem_name = eng
            amt = 1
        wl2 = list(force_waits)
        for s, v in wl:
            if s == eng and eng in ("T", "L"):
                continue
            if v > self.last_wait[eng][s]:
                self.last_wait[eng][s] = v
                wl2.append((s, v))

        def thunk(h, sems, _w=tuple(wl2), _b=body, _s=sem_name, _a=amt):
            for s_, v_ in _w:
                h.wait_ge(sems[s_], v_)
            instr = _b(h)
            assert instr is not None, "emit body must return its last instruction"
            instr.then_inc(sems[_s], _a)
        self.q[eng].append(thunk)
        for b in reads:
            b.r.append((sem_name, tok))
        for b in writes:
            b.w = (sem_name, tok)
            b.r = []
        return (sem_name, tok)

    def barrier(self, engs=("T", "V", "S", "G", "L")):
        toks = [(e, self.cnt[e]) for e in engs]
        for qn in ("L", "G", "S"):
            if qn in engs:
                toks += [(f"{qn}{j}", self.cnt[f"{qn}{j}"])
                         for j in range(self.NDMA)]
        for e in engs:
            wl = [(s, v) for (s, v) in toks if s != e and v > 0]
            self.emit(e, lambda h: h.nop(), waits=wl, noinc=True)

    def install(self, block, sems):
        for eng_name, attr in [("L", "sync"), ("T", "tensor"), ("V", "vector"),
                               ("S", "scalar"), ("G", "gpsimd")]:
            def run_queue(h, q=tuple(self.q[eng_name])):
                for thunk in q:
                    thunk(h, sems)
            getattr(block, attr)(run_queue)


def ln_stats(sc, xt, bxt, scr, bscr, ssum, bsum, ssq, bsq, srstd, brstd,
             snmr, bnmr, eps):
    sc.emit("V", lambda h: h.reduce_sum(ssum[:], xt[:], axis=AX.X),
            reads=[bxt], writes=[bsum])
    sc.emit("S", lambda h: h.activation(scr[:], xt[:], AF.Square, accum_out=ssq[:]),
            reads=[bxt], writes=[bscr, bsq])
    inv = 1.0 / float(D)
    sc.emit("V", lambda h: h.tensor_scalar(ssum[:], ssum[:], inv, None, ALU.mult),
            reads=[bsum], writes=[bsum])
    sc.emit("V", lambda h: h.tensor_scalar(ssq[:], ssq[:], inv, float(eps),
                                           ALU.mult, ALU.add),
            reads=[bsq], writes=[bsq])
    sc.emit("V", lambda h: h.tensor_tensor(srstd[:], ssum[:], ssum[:], ALU.mult),
            reads=[bsum], writes=[brstd])
    sc.emit("V", lambda h: h.tensor_sub(ssq[:], ssq[:], srstd[:]),
            reads=[bsq, brstd], writes=[bsq])
    sc.emit("S", lambda h: h.activation(srstd[:], ssq[:], AF.Sqrt),
            reads=[bsq], writes=[brstd])
    sc.emit("V", lambda h: h.reciprocal(srstd[:], srstd[:]),
            reads=[brstd], writes=[brstd])
    sc.emit("V", lambda h: h.scalar_tensor_tensor(snmr[:], ssum[:], -1.0, srstd[:],
                                                  op0=ALU.mult, op1=ALU.mult),
            reads=[bsum, brstd], writes=[bnmr])


def build_launch_A():
    nc = bass.Bass("TRN2", target_bir_lowering=False)
    ap = {}
    for name, shape, dt in [
        ("xq", [TOK, D], F32), ("wq", [D, D], F32), ("wk", [D, D], F32),
        ("wv", [D, D], F32), ("g1fm", [128, NC], F32), ("b1fm", [128, NC], F32),
        ("bqfm", [128, NC], F32), ("bkrep", [128, D], F32), ("bvrep", [128, D], F32),
        ("ident", [128, 128], BF16),
    ]:
        ap[name] = nc.dram_tensor(name, shape, dt, kind="ExternalInput").ap()
    ap["phiq"] = nc.dram_tensor("phiq", [128, NC * TOK], BF16, kind="ExternalOutput").ap()
    ap["kvp"] = nc.dram_tensor("kvp", [128, NP * 129], F32, kind="ExternalOutput").ap()

    es = contextlib.ExitStack()
    sc = Sched()
    with es:
        S_ = lambda n, sh, dt=F32: es.enter_context(nc.sbuf_tensor(n, sh, dt))
        sx = [S_(f"sx{i}", [128, D]) for i in range(2)]
        scr = [S_(f"scr_{i}", [128, D]) for i in range(2)]
        scrk = [S_(f"scrk{i}", [128, D]) for i in range(2)]
        ssum = [S_(f"ssum{i}", [128, 1]) for i in range(2)]
        ssq = [S_(f"ssq{i}", [128, 1]) for i in range(2)]
        srstd = [S_(f"srstd{i}", [128, 1]) for i in range(2)]
        snmr = [S_(f"snmr{i}", [128, 1]) for i in range(2)]
        sxln = [S_(f"sxln{i}", [128, D], BF16) for i in range(2)]
        xfm = S_("xfm", [128, NC * TOK], BF16)
        g1 = S_("g1", [128, NC]); b1f = S_("b1f", [128, NC])
        bqe = S_("bqe", [128, NC]); bke = S_("bke", [128, D]); bve = S_("bve", [128, D])
        ident = S_("identt", [128, 128], BF16)
        wtmp = [S_(f"wtmp{i}", [128, D]) for i in range(2)]
        wqb = S_("wqb", [128, NC * D], BF16)
        wkb = S_("wkb", [128, NC * D], BF16)
        wvb = S_("wvb", [128, NC * D], BF16)
        phik = [S_(f"phik{i}", [128, D], BF16) for i in range(2)]
        vsb = [S_(f"vsb{i}", [128, NP * 129], BF16) for i in range(2)]
        phiq = S_("phiqs", [128, NC * TOK], BF16)
        kvout = S_("kvout", [128, NP * 129])

        ps = contextlib.ExitStack()
        P_ = lambda n, sh: ps.enter_context(nc.psum_tensor(n, sh, F32))
        ptr = P_("ptr", [128, 512])
        pk = P_("pk", [128, D])
        pv = P_("pv", [128, D])
        kvb = [P_(f"kvb{i}", [128, 512]) for i in range(3)]

        sems = {s: es.enter_context(nc.semaphore(s)) for s in sc.sem_names()}
        block = es.enter_context(nc.Block())

        B = lambda: Buf()
        b_sx = [B(), B()]
        b_scr = [B(), B()]; b_scrk = [B(), B()]
        b_sum = [B(), B()]; b_sq = [B(), B()]
        b_rstd = [B(), B()]; b_nmr = [B(), B()]
        b_xln = [B(), B()]
        b_xfm = [B() for _ in range(NT)]
        b_consts = B(); b_wtmp = [B(), B()]
        b_wqb = B(); b_wkb = B(); b_wvb = B()
        b_phik = [B(), B()]; b_vsb = [B(), B()]
        b_ptr = B(); b_pk = B(); b_pv = B(); b_kv = B()
        b_pq = B(); b_phiq = B(); b_kvout = B()

        def load(dst_ap, src_ap, writes, reads=()):
            return sc.emit("L", lambda h: h.dma_start(out=dst_ap, in_=src_ap),
                           reads=list(reads), writes=list(writes), dma=True)

        for dst, name in [(g1[:], "g1fm"), (b1f[:], "b1fm"), (bqe[:], "bqfm"),
                          (bke[:], "bkrep"), (bve[:], "bvrep"), (ident[:], "ident")]:
            load(dst, ap[name], [b_consts])
        sc.emit("V", lambda h: (
            h.tensor_scalar(bqe[:], bqe[:], KEPS, None, ALU.add),
            h.tensor_scalar(bke[:], bke[:], KEPS, None, ALU.add))[-1],
            reads=[b_consts], writes=[b_consts])
        for i in range(2):
            sc.emit("V", lambda h, ii=i: h.memset(
                vsb[ii][:].rearrange("p (a c) -> p a c", c=129)[:, :, 128:129], 1.0),
                writes=[b_vsb[i]])

        wdst = {"wq": (wqb, b_wqb), "wk": (wkb, b_wkb), "wv": (wvb, b_wvb)}
        for wi, wname in enumerate(("wk", "wv", "wq")):
            wt, bw = wdst[wname]
            for c in range(NC):
                buf = wtmp[(wi * NC + c) % 2]
                bb = b_wtmp[(wi * NC + c) % 2]
                load(buf[:], ap[wname].rearrange("(c p) d -> c p d", p=128)[c], [bb])
                sc.emit("S", lambda h, b=buf, w=wt, cc=c: h.activation(
                    w[:, cc * D:(cc + 1) * D], b[:], AF.Identity),
                    reads=[bb], writes=[bw])

        xq_t = ap["xq"].rearrange("(n p) d -> n p d", p=128)

        def emit_ln_and_tr(i):
            r = i % 2
            load(sx[r][:], xq_t[i], [b_sx[r]])
            ln_stats(sc, sx[r], b_sx[r], scr[r], b_scr[r], ssum[r], b_sum[r],
                     ssq[r], b_sq[r], srstd[r], b_rstd[r], snmr[r], b_nmr[r], LNEPS)
            sc.emit("S", lambda h, rr=r: h.activation(
                sxln[rr][:], sx[rr][:], AF.Identity, scale=srstd[rr][:], bias=snmr[rr][:]),
                reads=[b_sx[r], b_rstd[r], b_nmr[r]], writes=[b_xln[r]])

        def emit_tr(i):
            r = i % 2
            for half in range(2):
                def tr(h, rr=r, hf=half):
                    out = None
                    for j in range(4):
                        c = hf * 4 + j
                        out = h.matmul(ptr[:, j * 128:(j + 1) * 128],
                                       sxln[rr][:, c * 128:(c + 1) * 128],
                                       ident[:], start=True, stop=True)
                    return out
                sc.emit("T", tr, reads=[b_xln[r], b_consts], writes=[b_ptr])

                def ev(h, ii=i, hf=half):
                    out = None
                    for j in range(4):
                        c = hf * 4 + j
                        out = h.tensor_scalar(
                            xfm[:, c * TOK + ii * 128: c * TOK + (ii + 1) * 128],
                            ptr[:, j * 128:(j + 1) * 128],
                            g1[:, c:c + 1], b1f[:, c:c + 1], ALU.mult, ALU.add)
                    return out
                sc.emit("V", ev, reads=[b_ptr, b_consts], writes=[b_xfm[i], b_ptr])

        def emit_kvproj(i):
            def kvproj(h, ii=i):
                out = None
                for c in range(NC):
                    xs_ = xfm[:, c * TOK + ii * 128: c * TOK + (ii + 1) * 128]
                    st, sp = (c == 0), (c == NC - 1)
                    h.matmul(pk[:, 0:512], xs_, wkb[:, c * D:c * D + 512], start=st, stop=sp)
                    h.matmul(pk[:, 512:1024], xs_, wkb[:, c * D + 512:c * D + 1024], start=st, stop=sp)
                    h.matmul(pv[:, 0:512], xs_, wvb[:, c * D:c * D + 512], start=st, stop=sp)
                    out = h.matmul(pv[:, 512:1024], xs_, wvb[:, c * D + 512:c * D + 1024], start=st, stop=sp)
                return out
            sc.emit("T", kvproj, reads=[b_xfm[i], b_wkb, b_wvb], writes=[b_pk, b_pv])

        def emit_kvev(i):
            r = i % 2
            sc.emit("V", lambda h, rr=r: h.scalar_tensor_tensor(
                scr[rr][:], pk[:], 1.0, bke[:], op0=ALU.mult, op1=ALU.add),
                reads=[b_pk, b_consts], writes=[b_scr[r]])
            sc.emit("V", lambda h, rr=r: h.tensor_scalar(
                phik[rr][:], scr[rr][:], KEPS, None, ALU.max),
                reads=[b_scr[r]], writes=[b_phik[r], b_pk])
            sc.emit("V", lambda h, rr=r: h.scalar_tensor_tensor(
                vsb[rr][:].rearrange("p (a c) -> p a c", c=129)[:, :, 0:128],
                pv[:].rearrange("p (a c) -> p a c", c=128),
                1.0, bve[:].rearrange("p (a c) -> p a c", c=128),
                op0=ALU.mult, op1=ALU.add),
                reads=[b_pv, b_consts], writes=[b_vsb[r], b_pv])

        def emit_kvacc_half(i, half):
            r = i % 2
            def kvacc(h, rr=r, hf=half):
                out = None
                for p in range(hf * 4, hf * 4 + 4):
                    bank, q = divmod(p, 3)
                    out = h.matmul(kvb[bank][:, q * 129:q * 129 + 129],
                                   phik[rr][:, p * 128:(p + 1) * 128],
                                   vsb[rr][:, p * 129:(p + 1) * 129],
                                   start=True, stop=True)
                return out
            sc.emit("T", kvacc, reads=[b_phik[r], b_vsb[r]], writes=[b_kv])

        def emit_kvsum(i):
            if i == 0:
                def kvinit(h):
                    h.tensor_copy(kvout[:, 0:387], kvb[0][:, 0:387])
                    h.tensor_copy(kvout[:, 387:774], kvb[1][:, 0:387])
                    return h.tensor_copy(kvout[:, 774:1032], kvb[2][:, 0:258])
                sc.emit("V", kvinit, reads=[b_kv], writes=[b_kvout, b_kv])
            else:
                def kvadd(h):
                    h.tensor_add(kvout[:, 0:387], kvout[:, 0:387], kvb[0][:, 0:387])
                    h.tensor_add(kvout[:, 387:774], kvout[:, 387:774], kvb[1][:, 0:387])
                    return h.tensor_add(kvout[:, 774:1032], kvout[:, 774:1032],
                                        kvb[2][:, 0:258])
                sc.emit("V", kvadd, reads=[b_kv], writes=[b_kvout, b_kv])

        for i in range(NT + 1):
            if i < NT:
                emit_ln_and_tr(i)
            if i >= 1:
                emit_kvproj(i - 1)
                emit_kvev(i - 1)
            if i < NT:
                emit_tr(i)
            if i >= 1:
                emit_kvacc(i - 1)

        sc.emit("G", lambda h: h.dma_start(out=ap["kvp"], in_=kvout[:]),
                reads=[b_kvout], writes=[], dma=True)

        sc.barrier(("T", "V"))
        ps.close()
        ps2 = contextlib.ExitStack()
        pq = ps2.enter_context(nc.psum_tensor("pq", [128, TOK], F32))
        for m in range(NC):
            def qproj(h, mm=m):
                out = None
                for c in range(NC):
                    w_ = wqb[:, c * D + mm * 128: c * D + (mm + 1) * 128]
                    for t4 in range(4):
                        out = h.matmul(pq[:, t4 * 512:(t4 + 1) * 512], w_,
                                       xfm[:, c * TOK + t4 * 512: c * TOK + (t4 + 1) * 512],
                                       start=(c == 0), stop=(c == NC - 1))
                return out
            sc.emit("T", qproj, reads=[b_wqb] + b_xfm, writes=[b_pq])
            sc.emit("V", lambda h, mm=m: h.tensor_scalar(
                phiq[:, mm * TOK:(mm + 1) * TOK], pq[:],
                bqe[:, mm:mm + 1], KEPS, ALU.add, ALU.max),
                reads=[b_pq, b_consts], writes=[b_phiq, b_pq])
        sc.emit("G", lambda h: h.dma_start(out=ap["phiq"], in_=phiq[:]),
                reads=[b_phiq], dma=True)
        sc.emit("G", lambda h: h.nop(),
                force_waits=[(f"G{j}", sc.cnt[f"G{j}"]) for j in range(Sched.NDMA)
                             if sc.cnt[f"G{j}"] > 0], noinc=True)
        ps2.close()

        sc.install(block, sems)
    return nc


def build_launch_B():
    nc = bass.Bass("TRN2", target_bir_lowering=False)
    ap = {}
    for name, shape, dt in [
        ("xq", [TOK, D], F32), ("phiq", [128, NC * TOK], BF16),
        ("kvp4", [4, 128, NP * 129], F32),
        ("wo", [D, D], F32), ("borep", [128, D], F32),
        ("g2fm", [128, NC], F32), ("ln2bfm", [128, NC], F32),
        ("w1", [D, DMLP], F32), ("b1fm", [128, NM], F32),
        ("w2", [DMLP, D], F32), ("b2rep", [128, D], F32),
        ("fker", [D, D], BF16), ("fsin", [1, 1], F32), ("fsk", [1, 1], F32),
        ("fbrep", [128, D], BF16),
        ("ident", [128, 128], BF16), ("ones1", [1, 128], F32),
        ("emat", [16, NC * 128], BF16),
    ]:
        ap[name] = nc.dram_tensor(name, shape, dt, kind="ExternalInput").ap()
    ap["out"] = nc.dram_tensor("out", [TOK, D], F32, kind="ExternalOutput").ap()
    accbuf = nc.dram_tensor("accbuf", [TOK, D], F32, kind="Internal").ap()
    x2buf = nc.dram_tensor("x2buf", [TOK, D], F32, kind="Internal").ap()
    w1c = nc.dram_tensor("w1c", [NM, 128, NC * 128], BF16, kind="Internal").ap()
    w2c = nc.dram_tensor("w2c", [NM, 128, D], BF16, kind="Internal").ap()

    es = contextlib.ExitStack()
    sc = Sched()
    with es:
        S_ = lambda n, sh, dt=F32: es.enter_context(nc.sbuf_tensor(n, sh, dt))
        scr = [S_(f"scr_{i}", [128, D]) for i in range(2)]
        scr2 = [S_(f"scr2_{i}", [128, D], BF16) for i in range(2)]
        ssum = [S_(f"ssum{i}", [128, 1]) for i in range(2)]
        ssq = [S_(f"ssq{i}", [128, 1]) for i in range(2)]
        srstd = [S_(f"srstd{i}", [128, 1]) for i in range(2)]
        snmr = [S_(f"snmr{i}", [128, 1]) for i in range(2)]
        ident = S_("identt", [128, 128], BF16)
        ones1 = S_("ones1s", [1, 128])
        emat = S_("emats", [16, NC * 128], BF16)
        g2 = S_("g2", [128, NC]); ln2b = S_("ln2b", [128, NC])
        b1f = S_("b1fs", [128, NM])
        fbrep = S_("fbreps", [128, D], BF16)
        sfs = S_("sfs", [1, 2])
        scl = S_("scl", [128, 3])
        kvstat = S_("kvstat", [128, NP * 128], BF16)
        denst = S_("denst", [128, NC * 16], BF16)
        wtmp = [S_(f"wtmp{i}", [128, D]) for i in range(2)]
        wc1k = [S_(f"wc1k{i}", [128, D], BF16) for i in range(2)]
        xt = [S_(f"xt{i}", [128, D]) for i in range(2)]
        acc = [S_(f"acc{i}", [128, D]) for i in range(2)]
        xln2 = [S_(f"xln2{i}", [128, D], BF16) for i in range(2)]
        oring = [S_(f"oring{i}", [128, D]) for i in range(2)]
        x2b = [S_(f"x2b{i}", [128, D], BF16) for i in range(2)]
        ytmp = [S_(f"ytmp{i}", [128, D], BF16) for i in range(2)]
        w1blk = [S_(f"w1blk{i}", [128, NC * 128], BF16) for i in range(4)]
        w2blk = [S_(f"w2blk{i}", [128, D], BF16) for i in range(4)]
        x8r = [S_(f"x8r{i}", [128, NC * 128], FP8) for i in range(2)]
        k8 = S_("k8", [128, NC * D], FP8)
        xfm2 = S_("xfm2", [128, NC * TOK], BF16)

        sems = {s: es.enter_context(nc.semaphore(s)) for s in sc.sem_names()}
        block = es.enter_context(nc.Block())

        Bf = lambda: Buf()
        b_scr = [Bf(), Bf()]; b_scr2 = [Bf(), Bf()]
        b_sum = [Bf(), Bf()]; b_sq = [Bf(), Bf()]
        b_rstd = [Bf(), Bf()]; b_nmr = [Bf(), Bf()]
        b_consts = Bf(); b_sfs = Bf(); b_scl = Bf()
        b_kvstat = Bf(); b_denst = Bf()
        b_wtmp = [Bf(), Bf()]; b_wc1k = [Bf(), Bf()]
        b_xt = [Bf(), Bf()]; b_acc = [Bf(), Bf()]; b_xln2 = [Bf(), Bf()]
        b_oring = [Bf(), Bf()]; b_x2r = [Bf() for _ in range(4)]; b_x2b = [Bf(), Bf()]
        b_ytmp = [Bf(), Bf()]
        b_w1blk = [Bf() for _ in range(8)]; b_w2blk = [Bf() for _ in range(8)]
        b_x8r = [Bf(), Bf()]
        b_k8 = Bf()
        b_xfm2 = [Bf() for _ in range(NT)]
        b_w1c = Bf(); b_w2c = Bf()
        b_accd = [Bf() for _ in range(NT)]
        b_x2d = [Bf() for _ in range(NT)]

        def load(dst_ap, src_ap, writes, reads=()):
            return sc.emit("L", lambda h: h.dma_start(out=dst_ap, in_=src_ap),
                           reads=list(reads), writes=list(writes), dma=True)

        w1_src = ap["w1"].rearrange("(c p) d -> c p d", p=128)
        for c in range(NC):
            for qq in range(4):
                r = (c * 4 + qq) % 2
                load(wtmp[r][:], w1_src[c][:, qq * 1024:(qq + 1) * 1024], [b_wtmp[r]])
                sc.emit("S", lambda h, rr=r: h.activation(
                    wc1k[rr][:], wtmp[rr][:], AF.Identity),
                    reads=[b_wtmp[r]], writes=[b_wc1k[r]])
                sc.emit("G", lambda h, cc=c, q_=qq, rr=r: h.dma_start(
                    out=w1c[q_ * 8:(q_ + 1) * 8, cc].rearrange("m p k -> p m k"),
                    in_=wc1k[rr][:].rearrange("p (m k) -> p m k", k=128)),
                    reads=[b_wc1k[r]], writes=[b_w1c], dma=True)
        w2_src = ap["w2"].rearrange("(m p) d -> m p d", p=128)
        for m in range(NM):
            r = m % 2
            load(wtmp[r][:], w2_src[m], [b_wtmp[r]])
            sc.emit("S", lambda h, rr=r: h.activation(
                wc1k[rr][:], wtmp[rr][:], AF.Identity),
                reads=[b_wtmp[r]], writes=[b_wc1k[r]])
            sc.emit("G", lambda h, mm=m, rr=r: h.dma_start(
                out=w2c[mm], in_=wc1k[rr][:]),
                reads=[b_wc1k[r]], writes=[b_w2c], dma=True)

        for dst, name in [(ident[:], "ident"), (ones1[:], "ones1"),
                          (emat[:], "emat"), (borep[:], "borep"),
                          (b2rep[:], "b2rep"), (g2[:], "g2fm"),
                          (ln2b[:], "ln2bfm"), (b1f[:], "b1fm"),
                          (fbrep[:], "fbrep")]:
            load(dst, ap[name], [b_consts])
        load(sfs[:, 0:1], ap["fsin"], [b_sfs])
        load(sfs[:, 1:2], ap["fsk"], [b_sfs])

        sb_ph = contextlib.ExitStack()
        phg = [sb_ph.enter_context(nc.sbuf_tensor(f"phg{i}", [128, NC * 512], BF16))
               for i in range(2)]
        rden = sb_ph.enter_context(nc.sbuf_tensor("rden", [16, 512], BF16))
        kvsum = sb_ph.enter_context(nc.sbuf_tensor("kvsum", [128, NP * 129], F32))
        kvtmp = sb_ph.enter_context(nc.sbuf_tensor("kvtmp", [128, NP * 129], F32))
        bcs = [sb_ph.enter_context(nc.sbuf_tensor(f"bcs{i}", [128, 512], F32))
               for i in range(2)]
        b_phg = [Bf(), Bf()]; b_rden = Bf(); b_kvsum = Bf(); b_kvtmp = Bf()
        b_bcs = [Bf(), Bf()]
        sb_at = contextlib.ExitStack()
        atr = [sb_at.enter_context(nc.sbuf_tensor(f"atr{i}", [128, NC * 512], BF16))
               for i in range(2)]
        b_atr = [Bf(), Bf()]

        load(kvsum[:], ap["kvp4"][0], [b_kvsum])
        for j in (1, 2, 3):
            load(kvtmp[:], ap["kvp4"][j], [b_kvtmp])
            sc.emit("V", lambda h: h.tensor_add(kvsum[:], kvsum[:], kvtmp[:]),
                    reads=[b_kvtmp], writes=[b_kvsum])

        def mkscale2(h):
            h.tensor_tensor(kvtmp[0:1, 0:1], sfs[:, 0:1], sfs[:, 1:2], ALU.mult)
            h.reciprocal(kvtmp[0:1, 1:2], sfs[:, 0:1])
            return h.reciprocal(kvtmp[0:1, 2:3], sfs[:, 1:2])
        sc.emit("V", mkscale2, reads=[b_sfs], writes=[b_kvtmp])

        ps0 = contextlib.ExitStack()
        pscl = ps0.enter_context(nc.psum_tensor("pscl", [128, 512], F32))
        b_pscl = Bf()

        def sclmm(h):
            h.matmul(pscl[:, 0:1], ones1[:], kvtmp[0:1, 0:1], start=True, stop=True)
            h.matmul(pscl[:, 1:2], ones1[:], kvtmp[0:1, 1:2], start=True, stop=True)
            return h.matmul(pscl[:, 2:3], ones1[:], kvtmp[0:1, 2:3], start=True, stop=True)
        sc.emit("T", sclmm, reads=[b_kvtmp, b_consts], writes=[b_pscl])
        sc.emit("V", lambda h: h.tensor_copy(scl[:], pscl[:, 0:3]),
                reads=[b_pscl], writes=[b_scl, b_pscl])

        sc.emit("V", lambda h: h.memset(kvstat[:], 0.0), writes=[b_kvstat])
        sc.emit("V", lambda h: h.memset(denst[:], 0.0), writes=[b_denst])

        def mkkvstat(h):
            out = None
            for p in range(NP):
                h.tensor_copy(kvstat[0:64, p * 128:p * 128 + 64],
                              kvsum[0:64, p * 129:p * 129 + 64])
                out = h.tensor_copy(kvstat[64:128, p * 128 + 64:p * 128 + 128],
                                    kvsum[64:128, p * 129 + 64:p * 129 + 128])
            return out
        sc.emit("V", mkkvstat, reads=[b_kvsum], writes=[b_kvstat])

        def mkdenst(h):
            out = None
            for c in range(NC):
                h.tensor_copy(denst[0:64, c * 16 + 2 * c: c * 16 + 2 * c + 1],
                              kvsum[0:64, c * 129 + 128: c * 129 + 129])
                out = h.tensor_copy(denst[64:128, c * 16 + 2 * c + 1: c * 16 + 2 * c + 2],
                                    kvsum[64:128, c * 129 + 128: c * 129 + 129])
            return out
        sc.emit("V", mkdenst, reads=[b_kvsum], writes=[b_denst])

        fk_src = ap["fker"].rearrange("(c p) d -> c p d", p=128)
        for c in range(NC):
            r = c % 2
            load(wc1k[r][:], fk_src[c], [b_wc1k[r]])
            sc.emit("V", lambda h, cc=c, rr=r: h.tensor_scalar(
                k8[:, cc * D:(cc + 1) * D], wc1k[rr][:], scl[:, 2:3], None, ALU.mult),
                reads=[b_wc1k[r], b_scl], writes=[b_k8])

        sc.barrier(("T", "V"))
        ps0.close()

        ps1 = contextlib.ExitStack()
        pb = ps1.enter_context(nc.psum_tensor("pb", [128, 512], F32))
        pn = [ps1.enter_context(nc.psum_tensor(f"pn{i}", [128, 512], F32)) for i in range(2)]
        po = [ps1.enter_context(nc.psum_tensor(f"po{i}", [128, D], F32)) for i in range(2)]
        ptr3 = ps1.enter_context(nc.psum_tensor("ptr3", [128, 512], F32))
        b_pb = Bf(); b_pn = [Bf(), Bf()]
        b_po = [Bf(), Bf()]; b_ptr3 = Bf()
        sb3 = contextlib.ExitStack()
        wob = sb3.enter_context(nc.sbuf_tensor("wob", [128, NC * D], BF16))
        b_wob = Bf()
        wo_src = ap["wo"].rearrange("(c p) d -> c p d", p=128)
        for c in range(NC):
            r = c % 2
            load(wtmp[r][:], wo_src[c], [b_wtmp[r]])
            sc.emit("S", lambda h, cc=c, rr=r: h.activation(
                wob[:, cc * D:(cc + 1) * D], wtmp[rr][:], AF.Identity),
                reads=[b_wtmp[r]], writes=[b_wob])

        phq_src = ap["phiq"].rearrange("p (c s) -> p c s", s=TOK)
        xq_t = ap["xq"].rearrange("(n p) d -> n p d", p=128)
        acc_t = accbuf.rearrange("(n p) d -> n p d", p=128)

        def emit_attn_group(t4):
            gr = t4 % 2
            load(phg[gr][:].rearrange("p (c s) -> p c s", s=512),
                 phq_src[:, :, t4 * 512:(t4 + 1) * 512], [b_phg[gr]])

            def denmm(h, g=gr):
                out = None
                for c in range(NC):
                    out = h.matmul(pb[0:16, :], denst[:, c * 16:(c + 1) * 16],
                                   phg[g][:, c * 512:(c + 1) * 512],
                                   start=(c == 0), stop=(c == NC - 1))
                return out
            sc.emit("T", denmm, reads=[b_denst, b_phg[gr]], writes=[b_pb])

            def recip_b(h):
                with nc.allow_low_precision(reason="bf16 1/den as bcast operand"):
                    return h.reciprocal(rden[:], pb[0:16, :])
            sc.emit("V", recip_b, reads=[b_pb], writes=[b_rden, b_pb])
            for c in range(NC):
                r = c % 2
                sc.emit("T", lambda h, cc=c: h.matmul(
                    pb[:], emat[:, cc * 128:(cc + 1) * 128],
                    rden[:], start=True, stop=True),
                    reads=[b_consts, b_rden], writes=[b_pb])
                sc.emit("T", lambda h, cc=c, rr=r, g=gr: h.matmul(
                    pn[rr][:], kvstat[:, cc * 128:(cc + 1) * 128],
                    phg[g][:, cc * 512:(cc + 1) * 512],
                    start=True, stop=True),
                    reads=[b_kvstat, b_phg[gr]], writes=[b_pn[r]])
                sc.emit("S", lambda h, rr=r: h.copy(bcs[rr][:], pb[:]),
                        reads=[b_pb], writes=[b_bcs[r], b_pb])
                sc.emit("V", lambda h, cc=c, rr=r, g=gr: h.tensor_mul(
                    atr[g][:, cc * 512:(cc + 1) * 512],
                    pn[rr][:], bcs[rr][:]),
                    reads=[b_pn[r], b_bcs[r]], writes=[b_atr[gr], b_pn[r]])

        def emit_womm(i):
            r = i % 2
            gr = (i // 4) % 2
            js = (i % 4) * 128

            def womm(h, gg=gr, jj=js, rr=r):
                out = None
                for c in range(NC):
                    a_ = atr[gg][:, c * 512 + jj: c * 512 + jj + 128]
                    st, sp = (c == 0), (c == NC - 1)
                    h.matmul(po[rr][:, 0:512], a_, wob[:, c * D:c * D + 512],
                             start=st, stop=sp)
                    out = h.matmul(po[rr][:, 512:1024], a_,
                                   wob[:, c * D + 512:c * D + 1024], start=st, stop=sp)
                return out
            sc.emit("T", womm, reads=[b_atr[gr], b_wob], writes=[b_po[r]])

        def emit_tile_chain(i):
            r = i % 2
            load(xt[r][:], xq_t[i], [b_xt[r]])
            sc.emit("V", lambda h, rr=r: h.scalar_tensor_tensor(
                scr[rr][:], po[rr][:], 1.0, borep[:], op0=ALU.mult, op1=ALU.add),
                reads=[b_po[r], b_consts], writes=[b_scr[r], b_po[r]])
            sc.emit("V", lambda h, rr=r: h.tensor_add(scr[rr][:], scr[rr][:], xt[rr][:]),
                reads=[b_xt[r]], writes=[b_scr[r]])
            sc.emit("V", lambda h, rr=r: h.tensor_add(acc[rr][:], scr[rr][:], xt[rr][:]),
                reads=[b_scr[r], b_xt[r]], writes=[b_acc[r]])
            sc.emit("G", lambda h, ii=i, rr=r: h.dma_start(out=acc_t[ii], in_=acc[rr][:]),
                    reads=[b_acc[r]], writes=[b_accd[i]], dma=True)
            ln_stats(sc, scr[r], b_scr[r], scr2[r], b_scr2[r], ssum[r], b_sum[r],
                     ssq[r], b_sq[r], srstd[r], b_rstd[r], snmr[r], b_nmr[r], LNEPS)
            sc.emit("S", lambda h, rr=r: h.activation(
                xln2[rr][:], scr[rr][:], AF.Identity, scale=srstd[rr][:], bias=snmr[rr][:]),
                reads=[b_scr[r], b_rstd[r], b_nmr[r]], writes=[b_xln2[r]])

        def emit_tr3(i):
            r = i % 2
            for half in range(2):
                def tr(h, rr=r, hf=half):
                    out = None
                    for j in range(4):
                        c = hf * 4 + j
                        out = h.matmul(ptr3[:, j * 128:(j + 1) * 128],
                                       xln2[rr][:, c * 128:(c + 1) * 128],
                                       ident[:], start=True, stop=True)
                    return out
                sc.emit("T", tr, reads=[b_xln2[r], b_consts], writes=[b_ptr3])

                def ev(h, ii=i, hf=half):
                    out = None
                    for j in range(4):
                        c = hf * 4 + j
                        out = h.tensor_scalar(
                            xfm2[:, c * TOK + ii * 128: c * TOK + (ii + 1) * 128],
                            ptr3[:, j * 128:(j + 1) * 128],
                            g2[:, c:c + 1], ln2b[:, c:c + 1], ALU.mult, ALU.add)
                    return out
                sc.emit("V", ev, reads=[b_ptr3, b_consts], writes=[b_xfm2[i], b_ptr3])

        emit_attn_group(0)
        emit_attn_group(1)
        for i in range(NT + 1):
            if i < NT:
                emit_womm(i)
                if i % 4 == 3 and (i // 4) + 2 <= 3:
                    emit_attn_group((i // 4) + 2)
            if i >= 1:
                emit_tr3(i - 1)
            if i < NT:
                emit_tile_chain(i)

        sc.barrier(("T", "V", "S"))
        sb3.close()
        sb_at.close()
        sb_ph.close()
        ps1.close()
        sbk8 = contextlib.ExitStack()
        k8 = sbk8.enter_context(nc.sbuf_tensor("k8", [128, NC * D], FP8))
        x2r = [sbk8.enter_context(nc.sbuf_tensor(f"x2r{i}", [128, D], F32))
               for i in range(4)]
        sb4 = contextlib.ExitStack()
        y1g = sb4.enter_context(nc.sbuf_tensor("y1g", [128, NM * 512], BF16))
        b2rep = sb4.enter_context(nc.sbuf_tensor("b2reps", [128, D], F32))
        y2o = [sb4.enter_context(nc.sbuf_tensor(f"y2o{i}", [128, D], F32))
               for i in range(4)]
        b_y2o = [Bf() for _ in range(4)]
        fk_src = ap["fker"].rearrange("(c p) d -> c p d", p=128)
        for c in range(NC):
            r = c % 2
            load(wc1k[r][:], fk_src[c], [b_wc1k[r]])
            sc.emit("V", lambda h, cc=c, rr=r: h.tensor_scalar(
                k8[:, cc * D:(cc + 1) * D], wc1k[rr][:], scl[:, 2:3], None, ALU.mult),
                reads=[b_wc1k[r], b_scl], writes=[b_k8])
        w1blk = [sb4.enter_context(nc.sbuf_tensor(f"w1blk{i}", [128, NC * 128], BF16))
                 for i in range(8)]
        w2blk = [sb4.enter_context(nc.sbuf_tensor(f"w2blk{i}", [128, D], BF16))
                 for i in range(8)]
        b_y1g = [Bf() for _ in range(NM)]
        load(b2rep[:], ap["b2rep"], [b_consts])
        x2_t = x2buf.rearrange("(n p) d -> n p d", p=128)
        qwait = [None]
        for q in range(4):
            qoff = q * 512
            psy1 = contextlib.ExitStack()
            py1 = [psy1.enter_context(nc.psum_tensor(f"py1_{q}_{i}", [128, 512], F32))
                   for i in range(2)]
            b_py1 = [Bf(), Bf()]
            for m in range(NM):
                r = m % 8
                load(w1blk[r][:], w1c[m], [b_w1blk[r]], reads=[b_w1c])

                def y1mm(h, rr=r, pp=m % 2, qo=qoff):
                    out = None
                    for c in range(NC):
                        out = h.matmul(
                            py1[pp][:],
                            w1blk[rr][:, c * 128:(c + 1) * 128],
                            xfm2[:, c * TOK + qo: c * TOK + qo + 512],
                            start=(c == 0), stop=(c == NC - 1))
                    return out
                pr = m % 2
                sc.emit("T", y1mm, reads=[b_w1blk[r]] + b_xfm2, writes=[b_py1[pr]],
                        force_waits=[qwait[0]] if (m == 0 and qwait[0]) else ())
                sc.emit("S", lambda h, mm=m, pp=pr: h.activation(
                    y1g[:, mm * 512:(mm + 1) * 512],
                    py1[pp][:], AF.Gelu_apprx_tanh, bias=b1f[:, mm:mm + 1]),
                    reads=[b_py1[pr], b_consts], writes=[b_y1g[m], b_py1[pr]])
            t_S_gelu = ("S", sc.cnt["S"])
            psy1.close()
            psy2 = contextlib.ExitStack()
            py2 = [psy2.enter_context(nc.psum_tensor(f"py2_{q}_{i}", [128, D], F32))
                   for i in range(4)]
            b_py2 = [Bf() for _ in range(4)]
            for m in range(NM):
                r = m % 8
                sc.emit("S", lambda h, rr=r, mm=m: h.dma_start(
                    out=w2blk[rr][:], in_=w2c[mm]),
                    reads=[b_w2c], writes=[b_w2blk[r]], dma=True)

                def y2mm(h, mm=m, rr=r):
                    out = None
                    for ss in range(4):
                        yg = y1g[:, mm * 512 + ss * 128: mm * 512 + (ss + 1) * 128]
                        st, sp = (mm == 0), (mm == NM - 1)
                        h.matmul(py2[ss][:, 0:512], yg, w2blk[rr][:, 0:512],
                                 start=st, stop=sp)
                        out = h.matmul(py2[ss][:, 512:1024], yg, w2blk[rr][:, 512:1024],
                                       start=st, stop=sp)
                    return out
                sc.emit("T", y2mm, reads=[b_y1g[m], b_w2blk[r]],
                        writes=b_py2 if m == NM - 1 else [],
                        force_waits=[t_S_gelu] if m == 0 else ())
            for ss in range(4):
                i = q * 4 + ss
                r = i % 2
                load(acc[r][:], acc_t[i], [b_acc[r]], reads=[b_accd[i]])

                sc.emit("V", lambda h, sss=ss, rr=r: h.scalar_tensor_tensor(
                    scr[rr][:], py2[sss][:], 1.0, b2rep[:], op0=ALU.mult, op1=ALU.add),
                    reads=[b_py2[ss], b_consts], writes=[b_scr[r], b_py2[ss]])
                sc.emit("V", lambda h, rr=r: h.tensor_add(scr[rr][:], scr[rr][:], acc[rr][:]),
                    reads=[b_acc[r]], writes=[b_scr[r]])
                ln_stats(sc, scr[r], b_scr[r], scr2[r], b_scr2[r], ssum[r], b_sum[r],
                         ssq[r], b_sq[r], srstd[r], b_rstd[r], snmr[r], b_nmr[r], PLNEPS)
                sc.emit("S", lambda h, rr=r: h.activation(
                    x2r[rr][:], scr[rr][:], AF.Identity, scale=srstd[rr][:], bias=snmr[rr][:]),
                    reads=[b_scr[r], b_rstd[r], b_nmr[r]], writes=[b_x2r[r]])
                sc.emit("G", lambda h, ii=i, rr=r: h.dma_start(out=x2_t[ii], in_=x2r[rr][:]),
                        reads=[b_x2r[r]], writes=[b_x2d[i]], dma=True)
            sc.barrier(("T", "V", "S"))
            psy2.close()

        sb4.close()
        ps5 = contextlib.ExitStack()
        pf = [ps5.enter_context(nc.psum_tensor(f"pf{i}", [128, D], F32)) for i in range(3)]
        ptr5 = [ps5.enter_context(nc.psum_tensor(f"ptr5{i}", [128, 512], F32)) for i in range(2)]
        b_pf = [Bf() for _ in range(3)]; b_ptr5 = [Bf(), Bf()]
        t_b5 = qwait[0]
        out_t = ap["out"].rearrange("(n p) d -> n p d", p=128)

        def emit_x8_pre(i):
            r4 = i % 4
            load(x2r[r4][:], x2_t[i], [b_x2r[r4]], reads=[b_x2d[i]])
            sc.emit("S", lambda h, r4_=r4: h.copy(x2b[r4_][:], x2r[r4_][:]),
                    reads=[b_x2r[r4]], writes=[b_x2b[r4]])

        def emit_x8(i):
            r = i % 2
            for hf in range(2):
                def tr(h, rr=r, hh=hf):
                    out = None
                    for j in range(4):
                        c = hh * 4 + j
                        out = h.matmul(ptr5[rr][:, j * 128:(j + 1) * 128],
                                       x2b[rr][:, c * 128:(c + 1) * 128],
                                       ident[:], start=True, stop=True)
                    return out
                sc.emit("T", tr, reads=[b_x2b[r], b_consts], writes=[b_ptr5[r]])

                def ev(h, rr=r, hh=hf):
                    out = None
                    for j in range(4):
                        c = hh * 4 + j
                        out = h.tensor_scalar(
                            x8r[rr][:, c * 128:(c + 1) * 128],
                            ptr5[rr][:, j * 128:(j + 1) * 128],
                            scl[:, 1:2], None, ALU.mult)
                    return out
                sc.emit("V", ev, reads=[b_ptr5[r], b_scl], writes=[b_x8r[r], b_ptr5[r]])

        def emit_ffn(i):
            r = i % 3
            rx = i % 2

            def ffnmm(h, rr=r, rx_=rx):
                out = None
                for c in range(NC):
                    xs_ = x8r[rx_][:, c * 128:(c + 1) * 128]
                    st, sp = (c == 0), (c == NC - 1)
                    h.matmul(pf[rr][:, 0:512], xs_, k8[:, c * D:c * D + 512],
                             start=st, stop=sp)
                    out = h.matmul(pf[rr][:, 512:1024], xs_,
                                   k8[:, c * D + 512:c * D + 1024], start=st, stop=sp)
                return out
            sc.emit("T", ffnmm, reads=[b_x8r[rx], b_k8], writes=[b_pf[r]])

        def emit_out_chain(i):
            r = i % 2
            p3 = i % 3
            sc.emit("S", lambda h, rr=r, pp=p3: h.activation(
                ytmp[rr][:], pf[pp][:], AF.Identity, scale=scl[:, 0:1]),
                reads=[b_pf[p3], b_scl], writes=[b_ytmp[r], b_pf[p3]])
            sc.emit("V", lambda h, rr=r: h.tensor_add(ytmp[rr][:], ytmp[rr][:], fbrep[:]),
                reads=[b_consts], writes=[b_ytmp[r]])
            sc.emit("V", lambda h, rr=r, r4_=i % 4: h.tensor_add(
                scr[rr][:], x2r[r4_][:], ytmp[rr][:]),
                reads=[b_x2r[i % 4], b_ytmp[r]], writes=[b_scr[r]])
            ln_stats(sc, scr[r], b_scr[r], scr2[r], b_scr2[r], ssum[r], b_sum[r],
                     ssq[r], b_sq[r], srstd[r], b_rstd[r], snmr[r], b_nmr[r], PLNEPS)
            sc.emit("S", lambda h, rr=r: h.activation(
                oring[rr][:], scr[rr][:], AF.Identity, scale=srstd[rr][:], bias=snmr[rr][:]),
                reads=[b_scr[r], b_rstd[r], b_nmr[r]], writes=[b_oring[r]])
            sc.emit("G", lambda h, ii=i, rr=r: h.dma_start(out=out_t[ii], in_=oring[rr][:]),
                    reads=[b_oring[r]], dma=True)

        emit_x8_pre(0)
        emit_x8_pre(1)
        for i in range(NT + 1):
            if i + 2 < NT:
                emit_x8_pre(i + 2)
            if i >= 1:
                emit_ffn(i - 1)
            if i < NT:
                emit_x8(i)
            if i >= 1:
                emit_out_chain(i - 1)
        sc.emit("G", lambda h: h.nop(),
                force_waits=[(f"G{j}", sc.cnt[f"G{j}"]) for j in range(Sched.NDMA)
                             if sc.cnt[f"G{j}"] > 0], noinc=True)
        ps2.close()

        sc.install(block, sems)
    return nc


def build_launch_B():
    nc = bass.Bass("TRN2", target_bir_lowering=False)
    ap = {}
    for name, shape, dt in [
        ("xq", [TOK, D], F32), ("phiq", [128, NC * TOK], BF16),
        ("kvp4", [4, 128, NP * 129], F32),
        ("wo", [D, D], F32), ("borep", [128, D], F32),
        ("g2fm", [128, NC], F32), ("ln2bfm", [128, NC], F32),
        ("w1", [D, DMLP], F32), ("b1fm", [128, NM], F32),
        ("w2", [DMLP, D], F32), ("b2rep", [128, D], F32),
        ("fker", [D, D], BF16), ("fsin", [1, 1], F32), ("fsk", [1, 1], F32),
        ("fbrep", [128, D], BF16),
        ("ident", [128, 128], BF16), ("ones1", [1, 128], F32),
        ("emat", [16, NC * 128], BF16),
    ]:
        ap[name] = nc.dram_tensor(name, shape, dt, kind="ExternalInput").ap()
    ap["out"] = nc.dram_tensor("out", [TOK, D], F32, kind="ExternalOutput").ap()
    accbuf = nc.dram_tensor("accbuf", [TOK, D], F32, kind="Internal").ap()
    x2buf = nc.dram_tensor("x2buf", [TOK, D], F32, kind="Internal").ap()
    w1c = nc.dram_tensor("w1c", [NM, 128, NC * 128], BF16, kind="Internal").ap()
    w2c = nc.dram_tensor("w2c", [NM, 128, D], BF16, kind="Internal").ap()

    es = contextlib.ExitStack()
    sc = Sched()
    with es:
        S_ = lambda n, sh, dt=F32: es.enter_context(nc.sbuf_tensor(n, sh, dt))
        scr = [S_(f"scr_{i}", [128, D]) for i in range(2)]
        scr2 = [S_(f"scr2_{i}", [128, D], BF16) for i in range(2)]
        ssum = [S_(f"ssum{i}", [128, 1]) for i in range(2)]
        ssq = [S_(f"ssq{i}", [128, 1]) for i in range(2)]
        srstd = [S_(f"srstd{i}", [128, 1]) for i in range(2)]
        snmr = [S_(f"snmr{i}", [128, 1]) for i in range(2)]
        ident = S_("identt", [128, 128], BF16)
        ones1 = S_("ones1s", [1, 128])
        emat = S_("emats", [16, NC * 128], BF16)
        g2 = S_("g2", [128, NC]); ln2b = S_("ln2b", [128, NC])
        b1f = S_("b1fs", [128, NM])
        fbrep = S_("fbreps", [128, D], BF16)
        sfs = S_("sfs", [1, 2])
        scl = S_("scl", [128, 3])
        kvstat = S_("kvstat", [128, NP * 128], BF16)
        denst = S_("denst", [128, NC * 16], BF16)
        wtmp = [S_(f"wtmp{i}", [128, D]) for i in range(2)]
        wc1k = [S_(f"wc1k{i}", [128, D], BF16) for i in range(2)]
        xt = [S_(f"xt{i}", [128, D]) for i in range(2)]
        acc = [S_(f"acc{i}", [128, D]) for i in range(2)]
        xln2 = [S_(f"xln2{i}", [128, D], BF16) for i in range(2)]
        oring = [S_(f"oring{i}", [128, D]) for i in range(2)]
        x2b = [S_(f"x2b{i}", [128, D], BF16) for i in range(2)]
        ytmp = [S_(f"ytmp{i}", [128, D], BF16) for i in range(2)]
        w1blk = [S_(f"w1blk{i}", [128, NC * 128], BF16) for i in range(4)]
        w2blk = [S_(f"w2blk{i}", [128, D], BF16) for i in range(4)]
        x8r = [S_(f"x8r{i}", [128, NC * 128], FP8) for i in range(2)]
        k8 = S_("k8", [128, NC * D], FP8)
        xfm2 = S_("xfm2", [128, NC * TOK], BF16)

        sems = {s: es.enter_context(nc.semaphore(s)) for s in sc.sem_names()}
        block = es.enter_context(nc.Block())

        Bf = lambda: Buf()
        b_scr = [Bf(), Bf()]; b_scr2 = [Bf(), Bf()]
        b_sum = [Bf(), Bf()]; b_sq = [Bf(), Bf()]
        b_rstd = [Bf(), Bf()]; b_nmr = [Bf(), Bf()]
        b_consts = Bf(); b_sfs = Bf(); b_scl = Bf()
        b_kvstat = Bf(); b_denst = Bf()
        b_wtmp = [Bf(), Bf()]; b_wc1k = [Bf(), Bf()]
        b_xt = [Bf(), Bf()]; b_acc = [Bf(), Bf()]; b_xln2 = [Bf(), Bf()]
        b_oring = [Bf(), Bf()]; b_x2r = [Bf() for _ in range(4)]; b_x2b = [Bf(), Bf()]
        b_ytmp = [Bf(), Bf()]
        b_w1blk = [Bf() for _ in range(8)]; b_w2blk = [Bf() for _ in range(8)]
        b_x8r = [Bf(), Bf()]
        b_k8 = Bf()
        b_xfm2 = [Bf() for _ in range(NT)]
        b_w1c = Bf(); b_w2c = Bf()
        b_accd = [Bf() for _ in range(NT)]
        b_x2d = [Bf() for _ in range(NT)]

        def load(dst_ap, src_ap, writes, reads=()):
            return sc.emit("L", lambda h: h.dma_start(out=dst_ap, in_=src_ap),
                           reads=list(reads), writes=list(writes), dma=True)

        w1_src = ap["w1"].rearrange("(c p) d -> c p d", p=128)
        for c in range(NC):
            for qq in range(4):
                r = (c * 4 + qq) % 2
                load(wtmp[r][:], w1_src[c][:, qq * 1024:(qq + 1) * 1024], [b_wtmp[r]])
                sc.emit("S", lambda h, rr=r: h.activation(
                    wc1k[rr][:], wtmp[rr][:], AF.Identity),
                    reads=[b_wtmp[r]], writes=[b_wc1k[r]])
                sc.emit("G", lambda h, cc=c, q_=qq, rr=r: h.dma_start(
                    out=w1c[q_ * 8:(q_ + 1) * 8, cc].rearrange("m p k -> p m k"),
                    in_=wc1k[rr][:].rearrange("p (m k) -> p m k", k=128)),
                    reads=[b_wc1k[r]], writes=[b_w1c], dma=True)
        w2_src = ap["w2"].rearrange("(m p) d -> m p d", p=128)
        for m in range(NM):
            r = m % 2
            load(wtmp[r][:], w2_src[m], [b_wtmp[r]])
            sc.emit("S", lambda h, rr=r: h.activation(
                wc1k[rr][:], wtmp[rr][:], AF.Identity),
                reads=[b_wtmp[r]], writes=[b_wc1k[r]])
            sc.emit("G", lambda h, mm=m, rr=r: h.dma_start(
                out=w2c[mm], in_=wc1k[rr][:]),
                reads=[b_wc1k[r]], writes=[b_w2c], dma=True)

        for dst, name in [(ident[:], "ident"), (ones1[:], "ones1"),
                          (emat[:], "emat"), (borep[:], "borep"),
                          (b2rep[:], "b2rep"), (g2[:], "g2fm"),
                          (ln2b[:], "ln2bfm"), (b1f[:], "b1fm"),
                          (fbrep[:], "fbrep")]:
            load(dst, ap[name], [b_consts])
        load(sfs[:, 0:1], ap["fsin"], [b_sfs])
        load(sfs[:, 1:2], ap["fsk"], [b_sfs])

        sb_ph = contextlib.ExitStack()
        phg = [sb_ph.enter_context(nc.sbuf_tensor(f"phg{i}", [128, NC * 512], BF16))
               for i in range(2)]
        rden = sb_ph.enter_context(nc.sbuf_tensor("rden", [16, 512], BF16))
        kvsum = sb_ph.enter_context(nc.sbuf_tensor("kvsum", [128, NP * 129], F32))
        kvtmp = sb_ph.enter_context(nc.sbuf_tensor("kvtmp", [128, NP * 129], F32))
        bcs = [sb_ph.enter_context(nc.sbuf_tensor(f"bcs{i}", [128, 512], F32))
               for i in range(2)]
        b_phg = [Bf(), Bf()]; b_rden = Bf(); b_kvsum = Bf(); b_kvtmp = Bf()
        b_bcs = [Bf(), Bf()]
        sb_at = contextlib.ExitStack()
        atr = [sb_at.enter_context(nc.sbuf_tensor(f"atr{i}", [128, NC * 512], BF16))
               for i in range(2)]
        b_atr = [Bf(), Bf()]

        load(kvsum[:], ap["kvp4"][0], [b_kvsum])
        for j in (1, 2, 3):
            load(kvtmp[:], ap["kvp4"][j], [b_kvtmp])
            sc.emit("V", lambda h: h.tensor_add(kvsum[:], kvsum[:], kvtmp[:]),
                    reads=[b_kvtmp], writes=[b_kvsum])

        def mkscale2(h):
            h.tensor_tensor(kvtmp[0:1, 0:1], sfs[:, 0:1], sfs[:, 1:2], ALU.mult)
            h.reciprocal(kvtmp[0:1, 1:2], sfs[:, 0:1])
            return h.reciprocal(kvtmp[0:1, 2:3], sfs[:, 1:2])
        sc.emit("V", mkscale2, reads=[b_sfs], writes=[b_kvtmp])

        ps0 = contextlib.ExitStack()
        pscl = ps0.enter_context(nc.psum_tensor("pscl", [128, 512], F32))
        b_pscl = Bf()

        def sclmm(h):
            h.matmul(pscl[:, 0:1], ones1[:], kvtmp[0:1, 0:1], start=True, stop=True)
            h.matmul(pscl[:, 1:2], ones1[:], kvtmp[0:1, 1:2], start=True, stop=True)
            return h.matmul(pscl[:, 2:3], ones1[:], kvtmp[0:1, 2:3], start=True, stop=True)
        sc.emit("T", sclmm, reads=[b_kvtmp, b_consts], writes=[b_pscl])
        sc.emit("V", lambda h: h.tensor_copy(scl[:], pscl[:, 0:3]),
                reads=[b_pscl], writes=[b_scl, b_pscl])

        sc.emit("V", lambda h: h.memset(kvstat[:], 0.0), writes=[b_kvstat])
        sc.emit("V", lambda h: h.memset(denst[:], 0.0), writes=[b_denst])

        def mkkvstat(h):
            out = None
            for p in range(NP):
                h.tensor_copy(kvstat[0:64, p * 128:p * 128 + 64],
                              kvsum[0:64, p * 129:p * 129 + 64])
                out = h.tensor_copy(kvstat[64:128, p * 128 + 64:p * 128 + 128],
                                    kvsum[64:128, p * 129 + 64:p * 129 + 128])
            return out
        sc.emit("V", mkkvstat, reads=[b_kvsum], writes=[b_kvstat])

        def mkdenst(h):
            out = None
            for c in range(NC):
                h.tensor_copy(denst[0:64, c * 16 + 2 * c: c * 16 + 2 * c + 1],
                              kvsum[0:64, c * 129 + 128: c * 129 + 129])
                out = h.tensor_copy(denst[64:128, c * 16 + 2 * c + 1: c * 16 + 2 * c + 2],
                                    kvsum[64:128, c * 129 + 128: c * 129 + 129])
            return out
        sc.emit("V", mkdenst, reads=[b_kvsum], writes=[b_denst])

        fk_src = ap["fker"].rearrange("(c p) d -> c p d", p=128)
        for c in range(NC):
            r = c % 2
            load(wc1k[r][:], fk_src[c], [b_wc1k[r]])
            sc.emit("V", lambda h, cc=c, rr=r: h.tensor_scalar(
                k8[:, cc * D:(cc + 1) * D], wc1k[rr][:], scl[:, 2:3], None, ALU.mult),
                reads=[b_wc1k[r], b_scl], writes=[b_k8])

        sc.barrier(("T", "V"))
        ps0.close()

        ps1 = contextlib.ExitStack()
        pb = ps1.enter_context(nc.psum_tensor("pb", [128, 512], F32))
        pn = [ps1.enter_context(nc.psum_tensor(f"pn{i}", [128, 512], F32)) for i in range(2)]
        po = [ps1.enter_context(nc.psum_tensor(f"po{i}", [128, D], F32)) for i in range(2)]
        ptr3 = ps1.enter_context(nc.psum_tensor("ptr3", [128, 512], F32))
        b_pb = Bf(); b_pn = [Bf(), Bf()]
        b_po = [Bf(), Bf()]; b_ptr3 = Bf()
        sb3 = contextlib.ExitStack()
        wob = sb3.enter_context(nc.sbuf_tensor("wob", [128, NC * D], BF16))
        b_wob = Bf()
        wo_src = ap["wo"].rearrange("(c p) d -> c p d", p=128)
        for c in range(NC):
            r = c % 2
            load(wtmp[r][:], wo_src[c], [b_wtmp[r]])
            sc.emit("S", lambda h, cc=c, rr=r: h.activation(
                wob[:, cc * D:(cc + 1) * D], wtmp[rr][:], AF.Identity),
                reads=[b_wtmp[r]], writes=[b_wob])

        phq_src = ap["phiq"].rearrange("p (c s) -> p c s", s=TOK)
        xq_t = ap["xq"].rearrange("(n p) d -> n p d", p=128)
        acc_t = accbuf.rearrange("(n p) d -> n p d", p=128)

        def emit_attn_group(t4):
            gr = t4 % 2
            load(phg[gr][:].rearrange("p (c s) -> p c s", s=512),
                 phq_src[:, :, t4 * 512:(t4 + 1) * 512], [b_phg[gr]])

            def denmm(h, g=gr):
                out = None
                for c in range(NC):
                    out = h.matmul(pb[0:16, :], denst[:, c * 16:(c + 1) * 16],
                                   phg[g][:, c * 512:(c + 1) * 512],
                                   start=(c == 0), stop=(c == NC - 1))
                return out
            sc.emit("T", denmm, reads=[b_denst, b_phg[gr]], writes=[b_pb])

            def recip_b(h):
                with nc.allow_low_precision(reason="bf16 1/den as bcast operand"):
                    return h.reciprocal(rden[:], pb[0:16, :])
            sc.emit("V", recip_b, reads=[b_pb], writes=[b_rden, b_pb])
            for c in range(NC):
                r = c % 2
                sc.emit("T", lambda h, cc=c: h.matmul(
                    pb[:], emat[:, cc * 128:(cc + 1) * 128],
                    rden[:], start=True, stop=True),
                    reads=[b_consts, b_rden], writes=[b_pb])
                sc.emit("T", lambda h, cc=c, rr=r, g=gr: h.matmul(
                    pn[rr][:], kvstat[:, cc * 128:(cc + 1) * 128],
                    phg[g][:, cc * 512:(cc + 1) * 512],
                    start=True, stop=True),
                    reads=[b_kvstat, b_phg[gr]], writes=[b_pn[r]])
                sc.emit("S", lambda h, rr=r: h.copy(bcs[rr][:], pb[:]),
                        reads=[b_pb], writes=[b_bcs[r], b_pb])
                sc.emit("V", lambda h, cc=c, rr=r, g=gr: h.tensor_mul(
                    atr[g][:, cc * 512:(cc + 1) * 512],
                    pn[rr][:], bcs[rr][:]),
                    reads=[b_pn[r], b_bcs[r]], writes=[b_atr[gr], b_pn[r]])

        def emit_womm(i):
            r = i % 2
            gr = (i // 4) % 2
            js = (i % 4) * 128

            def womm(h, gg=gr, jj=js, rr=r):
                out = None
                for c in range(NC):
                    a_ = atr[gg][:, c * 512 + jj: c * 512 + jj + 128]
                    st, sp = (c == 0), (c == NC - 1)
                    h.matmul(po[rr][:, 0:512], a_, wob[:, c * D:c * D + 512],
                             start=st, stop=sp)
                    out = h.matmul(po[rr][:, 512:1024], a_,
                                   wob[:, c * D + 512:c * D + 1024], start=st, stop=sp)
                return out
            sc.emit("T", womm, reads=[b_atr[gr], b_wob], writes=[b_po[r]])

        def emit_tile_chain(i):
            r = i % 2
            load(xt[r][:], xq_t[i], [b_xt[r]])
            sc.emit("V", lambda h, rr=r: h.scalar_tensor_tensor(
                scr[rr][:], po[rr][:], 1.0, borep[:], op0=ALU.mult, op1=ALU.add),
                reads=[b_po[r], b_consts], writes=[b_scr[r], b_po[r]])
            sc.emit("V", lambda h, rr=r: h.tensor_add(scr[rr][:], scr[rr][:], xt[rr][:]),
                reads=[b_xt[r]], writes=[b_scr[r]])
            sc.emit("V", lambda h, rr=r: h.tensor_add(acc[rr][:], scr[rr][:], xt[rr][:]),
                reads=[b_scr[r], b_xt[r]], writes=[b_acc[r]])
            sc.emit("G", lambda h, ii=i, rr=r: h.dma_start(out=acc_t[ii], in_=acc[rr][:]),
                    reads=[b_acc[r]], writes=[b_accd[i]], dma=True)
            ln_stats(sc, scr[r], b_scr[r], scr2[r], b_scr2[r], ssum[r], b_sum[r],
                     ssq[r], b_sq[r], srstd[r], b_rstd[r], snmr[r], b_nmr[r], LNEPS)
            sc.emit("S", lambda h, rr=r: h.activation(
                xln2[rr][:], scr[rr][:], AF.Identity, scale=srstd[rr][:], bias=snmr[rr][:]),
                reads=[b_scr[r], b_rstd[r], b_nmr[r]], writes=[b_xln2[r]])

        def emit_tr3(i):
            r = i % 2
            for half in range(2):
                def tr(h, rr=r, hf=half):
                    out = None
                    for j in range(4):
                        c = hf * 4 + j
                        out = h.matmul(ptr3[:, j * 128:(j + 1) * 128],
                                       xln2[rr][:, c * 128:(c + 1) * 128],
                                       ident[:], start=True, stop=True)
                    return out
                sc.emit("T", tr, reads=[b_xln2[r], b_consts], writes=[b_ptr3])

                def ev(h, ii=i, hf=half):
                    out = None
                    for j in range(4):
                        c = hf * 4 + j
                        out = h.tensor_scalar(
                            xfm2[:, c * TOK + ii * 128: c * TOK + (ii + 1) * 128],
                            ptr3[:, j * 128:(j + 1) * 128],
                            g2[:, c:c + 1], ln2b[:, c:c + 1], ALU.mult, ALU.add)
                    return out
                sc.emit("V", ev, reads=[b_ptr3, b_consts], writes=[b_xfm2[i], b_ptr3])

        emit_attn_group(0)
        emit_attn_group(1)
        for i in range(NT + 1):
            if i < NT:
                emit_womm(i)
                if i % 4 == 3 and (i // 4) + 2 <= 3:
                    emit_attn_group((i // 4) + 2)
            if i >= 1:
                emit_tr3(i - 1)
            if i < NT:
                emit_tile_chain(i)

        sc.barrier(("T", "V", "S"))
        sb3.close()
        sb_at.close()
        sb_ph.close()
        ps1.close()
        sbk8 = contextlib.ExitStack()
        k8 = sbk8.enter_context(nc.sbuf_tensor("k8", [128, NC * D], FP8))
        x2r = [sbk8.enter_context(nc.sbuf_tensor(f"x2r{i}", [128, D], F32))
               for i in range(4)]
        sb4 = contextlib.ExitStack()
        y1g = sb4.enter_context(nc.sbuf_tensor("y1g", [128, NM * 512], BF16))
        b2rep = sb4.enter_context(nc.sbuf_tensor("b2reps", [128, D], F32))
        y2o = [sb4.enter_context(nc.sbuf_tensor(f"y2o{i}", [128, D], F32))
               for i in range(4)]
        b_y2o = [Bf() for _ in range(4)]
        fk_src = ap["fker"].rearrange("(c p) d -> c p d", p=128)
        for c in range(NC):
            r = c % 2
            load(wc1k[r][:], fk_src[c], [b_wc1k[r]])
            sc.emit("V", lambda h, cc=c, rr=r: h.tensor_scalar(
                k8[:, cc * D:(cc + 1) * D], wc1k[rr][:], scl[:, 2:3], None, ALU.mult),
                reads=[b_wc1k[r], b_scl], writes=[b_k8])
        w1blk = [sb4.enter_context(nc.sbuf_tensor(f"w1blk{i}", [128, NC * 128], BF16))
                 for i in range(8)]
        w2blk = [sb4.enter_context(nc.sbuf_tensor(f"w2blk{i}", [128, D], BF16))
                 for i in range(8)]
        b_y1g = [Bf() for _ in range(NM)]
        load(b2rep[:], ap["b2rep"], [b_consts])
        x2_t = x2buf.rearrange("(n p) d -> n p d", p=128)
        qwait = [None]
        for q in range(4):
            qoff = q * 512
            psy1 = contextlib.ExitStack()
            py1 = [psy1.enter_context(nc.psum_tensor(f"py1_{q}_{i}", [128, 512], F32))
                   for i in range(2)]
            b_py1 = [Bf(), Bf()]
            for m in range(NM):
                r = m % 8
                load(w1blk[r][:], w1c[m], [b_w1blk[r]], reads=[b_w1c])

                def y1mm(h, rr=r, pp=m % 2, qo=qoff):
                    out = None
                    for c in range(NC):
                        out = h.matmul(
                            py1[pp][:],
                            w1blk[rr][:, c * 128:(c + 1) * 128],
                            xfm2[:, c * TOK + qo: c * TOK + qo + 512],
                            start=(c == 0), stop=(c == NC - 1))
                    return out
                pr = m % 2
                sc.emit("T", y1mm, reads=[b_w1blk[r]] + b_xfm2, writes=[b_py1[pr]],
                        force_waits=[qwait[0]] if (m == 0 and qwait[0]) else ())
                sc.emit("S", lambda h, mm=m, pp=pr: h.activation(
                    y1g[:, mm * 512:(mm + 1) * 512],
                    py1[pp][:], AF.Gelu_apprx_tanh, bias=b1f[:, mm:mm + 1]),
                    reads=[b_py1[pr], b_consts], writes=[b_y1g[m], b_py1[pr]])
            t_S_gelu = ("S", sc.cnt["S"])
            psy1.close()
            psy2 = contextlib.ExitStack()
            py2 = [psy2.enter_context(nc.psum_tensor(f"py2_{q}_{i}", [128, D], F32))
                   for i in range(4)]
            b_py2 = [Bf() for _ in range(4)]
            for m in range(NM):
                r = m % 8
                sc.emit("S", lambda h, rr=r, mm=m: h.dma_start(
                    out=w2blk[rr][:], in_=w2c[mm]),
                    reads=[b_w2c], writes=[b_w2blk[r]], dma=True)

                def y2mm(h, mm=m, rr=r):
                    out = None
                    for ss in range(4):
                        yg = y1g[:, mm * 512 + ss * 128: mm * 512 + (ss + 1) * 128]
                        st, sp = (mm == 0), (mm == NM - 1)
                        h.matmul(py2[ss][:, 0:512], yg, w2blk[rr][:, 0:512],
                                 start=st, stop=sp)
                        out = h.matmul(py2[ss][:, 512:1024], yg, w2blk[rr][:, 512:1024],
                                       start=st, stop=sp)
                    return out
                sc.emit("T", y2mm, reads=[b_y1g[m], b_w2blk[r]],
                        writes=b_py2 if m == NM - 1 else [],
                        force_waits=[t_S_gelu] if m == 0 else ())
            for ss in range(4):
                i = q * 4 + ss
                r = i % 2
                load(acc[r][:], acc_t[i], [b_acc[r]], reads=[b_accd[i]])

                sc.emit("V", lambda h, sss=ss, rr=r: h.scalar_tensor_tensor(
                    scr[rr][:], py2[sss][:], 1.0, b2rep[:], op0=ALU.mult, op1=ALU.add),
                    reads=[b_py2[ss], b_consts], writes=[b_scr[r], b_py2[ss]])
                sc.emit("V", lambda h, rr=r: h.tensor_add(scr[rr][:], scr[rr][:], acc[rr][:]),
                    reads=[b_acc[r]], writes=[b_scr[r]])
                ln_stats(sc, scr[r], b_scr[r], scr2[r], b_scr2[r], ssum[r], b_sum[r],
                         ssq[r], b_sq[r], srstd[r], b_rstd[r], snmr[r], b_nmr[r], PLNEPS)
                sc.emit("S", lambda h, rr=r: h.activation(
                    x2r[rr][:], scr[rr][:], AF.Identity, scale=srstd[rr][:], bias=snmr[rr][:]),
                    reads=[b_scr[r], b_rstd[r], b_nmr[r]], writes=[b_x2r[r]])
                sc.emit("G", lambda h, ii=i, rr=r: h.dma_start(out=x2_t[ii], in_=x2r[rr][:]),
                        reads=[b_x2r[r]], writes=[b_x2d[i]], dma=True)
            sc.barrier(("T", "V", "S"))
            psy2.close()

        sb4.close()
        ps5 = contextlib.ExitStack()
        pf = [ps5.enter_context(nc.psum_tensor(f"pf{i}", [128, D], F32)) for i in range(3)]
        ptr5 = ps5.enter_context(nc.psum_tensor("ptr5", [128, 512], F32))
        b_pf = [Bf(), Bf()]; b_ptr5 = Bf()
        out_t = ap["out"].rearrange("(n p) d -> n p d", p=128)
        for i in range(NT):
            r = i % 2
            load(x2r[r][:], x2_t[i], [b_x2r[r]], reads=[b_x2d[i]])
            sc.emit("V", lambda h, rr=r: h.tensor_copy(x2b[rr][:], x2r[rr][:]),
                    reads=[b_x2r[r]], writes=[b_x2b[r]])
            for hf in range(2):
                def tr(h, rr=r, hh=hf):
                    out = None
                    for j in range(4):
                        c = hh * 4 + j
                        out = h.matmul(ptr5[:, j * 128:(j + 1) * 128],
                                       x2b[rr][:, c * 128:(c + 1) * 128],
                                       ident[:], start=True, stop=True)
                    return out
                sc.emit("T", tr, reads=[b_x2b[r], b_consts], writes=[b_ptr5])

                def ev(h, rr=r, hh=hf):
                    out = None
                    for j in range(4):
                        c = hh * 4 + j
                        out = h.tensor_scalar(
                            x8r[rr][:, c * 128:(c + 1) * 128],
                            ptr5[:, j * 128:(j + 1) * 128],
                            scl[:, 1:2], None, ALU.mult)
                    return out
                sc.emit("V", ev, reads=[b_ptr5, b_scl], writes=[b_x8r[r], b_ptr5])

            def ffnmm(h, rr=r):
                out = None
                for c in range(NC):
                    xs_ = x8r[rr][:, c * 128:(c + 1) * 128]
                    st, sp = (c == 0), (c == NC - 1)
                    h.matmul(pf[rr][:, 0:512], xs_, k8[:, c * D:c * D + 512],
                             start=st, stop=sp)
                    out = h.matmul(pf[rr][:, 512:1024], xs_,
                                   k8[:, c * D + 512:c * D + 1024], start=st, stop=sp)
                return out
            sc.emit("T", ffnmm, reads=[b_x8r[rx], b_k8], writes=[b_pf[r]])

            sc.emit("V", lambda h, rr=r: h.tensor_scalar(
                ytmp[rr][:], pf[rr][:], scl[:, 0:1], None, ALU.mult),
                reads=[b_pf[r], b_scl], writes=[b_ytmp[r], b_pf[r]])
            sc.emit("V", lambda h, rr=r: h.tensor_add(ytmp[rr][:], ytmp[rr][:], fbrep[:]),
                reads=[b_consts], writes=[b_ytmp[r]])
            sc.emit("V", lambda h, rr=r, r4_=i % 4: h.tensor_add(
                scr[rr][:], x2r[r4_][:], ytmp[rr][:]),
                reads=[b_x2r[i % 4], b_ytmp[r]], writes=[b_scr[r]])
            ln_stats(sc, scr[r], b_scr[r], scr2[r], b_scr2[r], ssum[r], b_sum[r],
                     ssq[r], b_sq[r], srstd[r], b_rstd[r], snmr[r], b_nmr[r], PLNEPS)
            sc.emit("S", lambda h, rr=r: h.activation(
                oring[rr][:], scr[rr][:], AF.Identity, scale=srstd[rr][:], bias=snmr[rr][:]),
                reads=[b_scr[r], b_rstd[r], b_nmr[r]], writes=[b_oring[r]])
            sc.emit("G", lambda h, ii=i, rr=r: h.dma_start(out=out_t[ii], in_=oring[rr][:]),
                    reads=[b_oring[r]], dma=True)
        sc.emit("G", lambda h: h.nop(),
                force_waits=[(f"G{j}", sc.cnt[f"G{j}"]) for j in range(Sched.NDMA)
                             if sc.cnt[f"G{j}"] > 0], noinc=True)
        ps5.close()

        sc.install(block, sems)
    return nc


NTB = SEQ // 128


def build_fused():
    nc = bass.Bass("TRN2", target_bir_lowering=False)
    ap = {}
    for name, shape, dt in [
        ("xb", [SEQ, D], F32),
        ("wq", [D, D], F32), ("wk", [D, D], F32), ("wv", [D, D], F32),
        ("g1fm", [128, NC], F32), ("b1fm", [128, NC], F32),
        ("bqfm", [128, NC], F32), ("bkrep", [128, D], F32), ("bvrep", [128, D], F32),
        ("wo", [D, D], F32), ("borep", [128, D], F32),
        ("g2fm", [128, NC], F32), ("ln2bfm", [128, NC], F32),
        ("w1", [D, DMLP], F32), ("b1mfm", [128, NM], F32),
        ("w2", [DMLP, D], F32), ("b2rep", [128, D], F32),
        ("fker", [D, D], BF16), ("fsin", [1, 1], F32), ("fsk", [1, 1], F32),
        ("fbrep", [128, D], BF16),
        ("ident", [128, 128], BF16), ("ones1", [1, 128], F32),
        ("emat", [16, NC * 128], BF16),
    ]:
        ap[name] = nc.dram_tensor(name, shape, dt, kind="ExternalInput").ap()
    ap["out"] = nc.dram_tensor("out", [TOK, D], F32, kind="ExternalOutput").ap()
    accbuf = nc.dram_tensor("accbuf", [TOK, D], F32, kind="Internal").ap()
    x2buf = nc.dram_tensor("x2buf", [TOK, D], F32, kind="Internal").ap()
    x2bb = nc.dram_tensor("x2bb", [TOK, D], BF16, kind="Internal").ap()
    w1c = nc.dram_tensor("w1c", [NM, 128, NC * 128], BF16, kind="Internal").ap()
    w2c = nc.dram_tensor("w2c", [NM, 128, D], BF16, kind="Internal").ap()

    es = contextlib.ExitStack()
    sc = Sched()
    with es:
        S_ = lambda n, sh, dt=F32: es.enter_context(nc.sbuf_tensor(n, sh, dt))
        scr2 = [S_(f"scr2_{i}", [128, D], BF16) for i in range(2)]
        ssum = [S_(f"ssum{i}", [128, 1]) for i in range(2)]
        ssq = [S_(f"ssq{i}", [128, 1]) for i in range(2)]
        srstd = [S_(f"srstd{i}", [128, 1]) for i in range(2)]
        snmr = [S_(f"snmr{i}", [128, 1]) for i in range(2)]
        ident = S_("identt", [128, 128], BF16)
        ones1 = S_("ones1s", [1, 128])
        g2 = S_("g2", [128, NC]); ln2b = S_("ln2b", [128, NC])
        b1m = S_("b1ms", [128, NM])
        fbrep = S_("fbreps", [128, D], BF16)
        sfs = S_("sfs", [1, 2])
        scl = S_("scl", [128, 3])
        kvstat = S_("kvstat", [128, NP * 128], BF16)
        denst = S_("denst", [128, NC * 16], BF16)
        wtmp = [S_(f"wtmp{i}", [128, D]) for i in range(3)]
        x8r = [S_(f"x8r{i}", [128, NC * 128], FP8) for i in range(2)]
        xfm2 = S_("xfm2", [128, NC * TOK], BF16)

        sems = {s: es.enter_context(nc.semaphore(s)) for s in sc.sem_names()}
        block = es.enter_context(nc.Block())

        Bf = lambda: Buf()
        b_scr = [Bf(), Bf()]; b_scr2 = [Bf(), Bf()]
        b_sum = [Bf(), Bf()]; b_sq = [Bf(), Bf()]
        b_rstd = [Bf(), Bf()]; b_nmr = [Bf(), Bf()]
        b_consts = Bf(); b_sfs = Bf(); b_scl = Bf()
        b_kvstat = Bf(); b_denst = Bf(); b_kvout = Bf()
        b_wtmp = [Bf() for _ in range(3)]; b_wc1k = [Bf(), Bf()]
        b_sx = [Bf(), Bf()]; b_xln = [Bf(), Bf()]
        b_xfmr = [Bf(), Bf()]
        b_phik = [Bf(), Bf()]; b_vsb = [Bf(), Bf()]
        b_xt = [Bf(), Bf()]; b_acc = [Bf(), Bf()]; b_xln2 = [Bf(), Bf(), Bf()]
        b_oring = [Bf(), Bf()]; b_x2r = [Bf() for _ in range(4)]; b_x2b = [Bf(), Bf()]
        b_ytmp = [Bf(), Bf()]
        b_w1blk = [Bf() for _ in range(8)]; b_w2blk = [Bf() for _ in range(8)]
        b_x8r = [Bf(), Bf()]
        b_k8 = Bf()
        b_xfm2 = [Bf() for _ in range(NT)]
        b_phiq = Bf()
        b_w1c = Bf(); b_w2c = Bf()
        b_wqb = Bf(); b_wkb = Bf(); b_wvb = Bf()
        b_accd = [Bf() for _ in range(NT)]
        b_x2d = [Bf() for _ in range(NT)]

        def load(dst_ap, src_ap, writes, reads=()):
            return sc.emit("L", lambda h: h.dma_start(out=dst_ap, in_=src_ap),
                           reads=list(reads), writes=list(writes), dma=True)

        def loadq(qn, dst_ap, src_ap, writes, reads=()):
            return sc.emit(qn, lambda h: h.dma_start(out=dst_ap, in_=src_ap),
                           reads=list(reads), writes=list(writes), dma=True)

        sb_ph = contextlib.ExitStack()
        phiq = sb_ph.enter_context(nc.sbuf_tensor("phiq", [128, NC * TOK], BF16))
        b_phiq = Bf()
        sb_a = contextlib.ExitStack()
        A_ = lambda n, sh, dt=F32: sb_a.enter_context(nc.sbuf_tensor(n, sh, dt))
        xfm_own = A_("xfm_own", [128, NC * TOK], BF16)
        sx = [A_(f"sx{i}", [128, D]) for i in range(3)]
        sxln = [A_(f"sxln{i}", [128, D], BF16) for i in range(2)]
        xfmr = [A_(f"xfmr{i}", [128, D], BF16) for i in range(2)]
        phik = [A_(f"phik{i}", [128, D], BF16) for i in range(3)]
        vsb = [A_(f"vsb{i}", [128, NP * 129], BF16) for i in range(3)]
        kvout = A_("kvout", [128, NP * 129])
        g1 = A_("g1", [128, NC]); b1f = A_("b1f", [128, NC])
        bqe = A_("bqe", [128, NC])
        bke = A_("bke", [128, D], BF16); bve = A_("bve", [128, D], BF16)
        b_sx = [Bf() for _ in range(3)]; b_xln = [Bf(), Bf()]; b_xfmr = [Bf(), Bf()]
        b_phik = [Bf(), Bf(), Bf()]; b_vsb = [Bf(), Bf(), Bf()]; b_kvout = Bf()
        b_phr = [Bf(), Bf()]; b_phqd = Bf()
        b_xfm_own = [Bf() for _ in range(NT)]
        sb_kv2 = contextlib.ExitStack()
        wqb = sb_kv2.enter_context(nc.sbuf_tensor("wqb", [128, NC * D], BF16))
        sb_kv1 = contextlib.ExitStack()
        wkb = sb_kv1.enter_context(nc.sbuf_tensor("wkb", [128, NC * D], BF16))
        wvb = sb_kv1.enter_context(nc.sbuf_tensor("wvb", [128, NC * D], BF16))

        xb_t = ap["xb"].rearrange("(n p) d -> n p d", p=128)
        for dst, name in [(g1[:], "g1fm"), (b1f[:], "b1fm"), (bqe[:], "bqfm"),
                          (ident[:], "ident"), (ones1[:], "ones1"),
                          (g2[:], "g2fm"),
                          (ln2b[:], "ln2bfm"), (b1m[:], "b1mfm"),
                          (fbrep[:], "fbrep")]:
            load(dst, ap[name], [b_consts])
        load(sfs[:, 0:1], ap["fsin"], [b_sfs])
        load(sfs[:, 1:2], ap["fsk"], [b_sfs])
        for i in range(3):
            load(sx[i][:], xb_t[i], [b_sx[i]])
        loadq("G", wtmp[0][:], ap["bkrep"], [b_wtmp[0]])
        sc.emit("V", lambda h: h.tensor_scalar(bke[:], wtmp[0][:], KEPS, None, ALU.add),
                reads=[b_wtmp[0]], writes=[b_consts])
        loadq("G", wtmp[1][:], ap["bvrep"], [b_wtmp[1]])
        sc.emit("V", lambda h: h.tensor_copy(bve[:], wtmp[1][:]),
                reads=[b_wtmp[1]], writes=[b_consts])
        sc.emit("V", lambda h: h.tensor_scalar(bqe[:], bqe[:], KEPS, None, ALU.add),
                reads=[b_consts], writes=[b_consts])
        for i in range(3):
            sc.emit("V", lambda h, ii=i: h.memset(
                vsb[ii][:].rearrange("p (a c) -> p a c", c=129)[:, :, 128:129], 1.0),
                writes=[b_vsb[i]])

        _wrr = [2]
        wqueues = ("S", "G", "L")
        wdst = {"wk": (wkb, b_wkb), "wv": (wvb, b_wvb)}
        for idx, (wname, c) in enumerate(
                [(n, c) for n in ("wv", "wk") for c in range(NC)]):
            wt, bw = wdst[wname]
            r = _wrr[0] % 3
            _wrr[0] += 1
            loadq(wqueues[idx % 3], wtmp[r][:],
                  ap[wname].rearrange("(c p) d -> c p d", p=128)[c], [b_wtmp[r]])
            sc.emit("G", lambda h, b=wtmp[r], w=wt, cc=c: h.dma_start(
                out=w[:, cc * D:(cc + 1) * D], in_=b[:]),
                reads=[b_wtmp[r]], writes=[bw], dma=True)

        w1_src = ap["w1"].rearrange("(c p) d -> c p d", p=128)
        w2_src = ap["w2"].rearrange("(m p) d -> m p d", p=128)
        wcast_steps = []
        _wstep = [0]

        def _mk_wq_step(c):
            def step():
                sc.emit("G", lambda h, cc=c: h.dma_start(
                    out=wqb[:, cc * D:(cc + 1) * D],
                    in_=ap["wq"].rearrange("(c p) d -> c p d", p=128)[cc]),
                    writes=[b_wqb], dma=True)
                return None
            return step

        def _mk_w1_step(c, qq):
            def step():
                sc.emit("G", lambda h, cc=c, q_=qq: h.dma_start(
                    out=w1c[q_ * 8:(q_ + 1) * 8, :, cc * 128:(cc + 1) * 128]
                    .rearrange("m p k -> p m k"),
                    in_=w1_src[cc][:, q_ * 1024:(q_ + 1) * 1024]),
                    writes=[b_w1c], dma=True)
                return None
            return step

        def _mk_w2_step(m):
            def step():
                r = _wstep[0] % 3
                _wstep[0] += 1
                loadq("S", wtmp[r][:], w2_src[m], [b_wtmp[r]])

                def cast():
                    sc.emit("G", lambda h, mm=m, rr=r: h.dma_start(
                        out=w2c[mm], in_=wtmp[rr][:]),
                        reads=[b_wtmp[r]], writes=[b_w2c], dma=True)
                return cast
            return step
        w1l = [_mk_w1_step(c, qq) for c in range(NC) for qq in range(4)]
        w2l = [_mk_w2_step(m) for m in range(NM)]
        wql = [_mk_wq_step(c) for c in range(NC)]
        mixed = []
        for a, b in zip(w1l, w2l):
            mixed.append(a)
            mixed.append(b)
        for k, s in enumerate(mixed):
            if k >= 7 and (k - 7) % 7 == 0 and wql:
                wcast_steps.append(wql.pop(0))
            wcast_steps.append(s)
        wcast_steps.extend(wql)

        def mkscale2(h):
            h.tensor_tensor(kvout[0:1, 0:1], sfs[:, 0:1], sfs[:, 1:2], ALU.mult)
            h.reciprocal(kvout[0:1, 1:2], sfs[:, 0:1])
            return h.reciprocal(kvout[0:1, 2:3], sfs[:, 1:2])
        sc.emit("V", mkscale2, reads=[b_sfs], writes=[b_kvout])
        ps0 = contextlib.ExitStack()
        pscl = ps0.enter_context(nc.psum_tensor("pscl", [128, 512], F32))
        b_pscl = Bf()

        def sclmm(h):
            h.matmul(pscl[:, 0:1], ones1[:], kvout[0:1, 0:1], start=True, stop=True)
            h.matmul(pscl[:, 1:2], ones1[:], kvout[0:1, 1:2], start=True, stop=True)
            return h.matmul(pscl[:, 2:3], ones1[:], kvout[0:1, 2:3], start=True, stop=True)
        sc.emit("T", sclmm, reads=[b_kvout, b_consts], writes=[b_pscl])
        sc.emit("V", lambda h: h.tensor_copy(scl[:], pscl[:, 0:3]),
                reads=[b_pscl], writes=[b_scl, b_pscl])
        sc.barrier(("T", "V"))
        ps0.close()

        ps_a = contextlib.ExitStack()
        P_ = lambda n, sh: ps_a.enter_context(nc.psum_tensor(n, sh, F32))
        ptr = P_("ptr", [128, 512])
        pk = P_("pk", [128, D])
        pv = P_("pv", [128, D])
        kvb = [P_(f"kvb{i}", [128, 512]) for i in range(3)]
        b_ptr = Bf(); b_pk = Bf(); b_pv = Bf(); b_kv = Bf()

        def fm_dst(i, c):
            if i < NT:
                return (xfm_own[:, c * TOK + i * 128: c * TOK + (i + 1) * 128],
                        b_xfm_own[i])
            return (xfmr[i % 2][:, c * 128:(c + 1) * 128], b_xfmr[i % 2])

        def emit_xload(i):
            r = i % 3
            load(sx[r][:], xb_t[i], [b_sx[r]])

        def emit_stats(i):
            r = i % 2
            ln_stats(sc, sx[i % 3], b_sx[i % 3], scr2[r], b_scr2[r], ssum[r],
                     b_sum[r], ssq[r], b_sq[r], srstd[r], b_rstd[r], snmr[r],
                     b_nmr[r], LNEPS)

        def emit_xln(i):
            r = i % 2
            sc.emit("S", lambda h, rr=r, r3=i % 3: h.activation(
                sxln[rr][:], sx[r3][:], AF.Identity, scale=srstd[rr][:],
                bias=snmr[rr][:]),
                reads=[b_sx[i % 3], b_rstd[r], b_nmr[r]], writes=[b_xln[r]])

        def emit_tr_half(i, half):
            r = i % 2
            bufs = [fm_dst(i, c) for c in range(NC)]

            def tr(h, rr=r, hf=half):
                out = None
                for j in range(4):
                    c = hf * 4 + j
                    out = h.matmul(ptr[:, j * 128:(j + 1) * 128],
                                   sxln[rr][:, c * 128:(c + 1) * 128],
                                   ident[:], start=True, stop=True)
                return out
            sc.emit("T", tr, reads=[b_xln[r], b_consts], writes=[b_ptr])

            wb = sorted({id(b): b for _, b in bufs[half * 4:(half + 1) * 4]}.values(),
                        key=id)
            if half == 0:
                def ev(h, ii=i, hf=half):
                    out = None
                    for j in range(4):
                        c = hf * 4 + j
                        out = h.activation(
                            fm_dst(ii, c)[0],
                            ptr[:, j * 128:(j + 1) * 128],
                            AF.Identity, scale=g1[:, c:c + 1], bias=b1f[:, c:c + 1])
                    return out
                sc.emit("S", ev, reads=[b_ptr, b_consts], writes=list(wb) + [b_ptr])
            else:
                def evv(h, ii=i, hf=half):
                    out = None
                    for j in range(4):
                        c = hf * 4 + j
                        out = h.tensor_scalar(
                            fm_dst(ii, c)[0],
                            ptr[:, j * 128:(j + 1) * 128],
                            g1[:, c:c + 1], b1f[:, c:c + 1], ALU.mult, ALU.add)
                    return out
                sc.emit("V", evv, reads=[b_ptr, b_consts], writes=list(wb) + [b_ptr])

        def emit_kvproj(i):
            rb = sorted({id(b): b for _, b in (fm_dst(i, c) for c in range(NC))}.values(),
                        key=id)
            if i < 3:
                def vproj(h, ii=i):
                    out = None
                    for c in range(NC):
                        xs_ = fm_dst(ii, c)[0]
                        st, sp = (c == 0), (c == NC - 1)
                        h.matmul(pv[:, 0:512], xs_, wvb[:, c * D:c * D + 512], start=st, stop=sp)
                        out = h.matmul(pv[:, 512:1024], xs_, wvb[:, c * D + 512:c * D + 1024], start=st, stop=sp)
                    return out

                def kproj(h, ii=i):
                    out = None
                    for c in range(NC):
                        xs_ = fm_dst(ii, c)[0]
                        st, sp = (c == 0), (c == NC - 1)
                        h.matmul(pk[:, 0:512], xs_, wkb[:, c * D:c * D + 512], start=st, stop=sp)
                        out = h.matmul(pk[:, 512:1024], xs_, wkb[:, c * D + 512:c * D + 1024], start=st, stop=sp)
                    return out
                sc.emit("T", vproj, reads=list(rb) + [b_wvb], writes=[b_pv])
                sc.emit("T", kproj, reads=list(rb) + [b_wkb], writes=[b_pk])
                return

            def kvproj(h, ii=i):
                out = None
                for c in range(NC):
                    xs_ = fm_dst(ii, c)[0]
                    st, sp = (c == 0), (c == NC - 1)
                    h.matmul(pk[:, 0:512], xs_, wkb[:, c * D:c * D + 512], start=st, stop=sp)
                    h.matmul(pk[:, 512:1024], xs_, wkb[:, c * D + 512:c * D + 1024], start=st, stop=sp)
                    h.matmul(pv[:, 0:512], xs_, wvb[:, c * D:c * D + 512], start=st, stop=sp)
                    out = h.matmul(pv[:, 512:1024], xs_, wvb[:, c * D + 512:c * D + 1024], start=st, stop=sp)
                return out
            sc.emit("T", kvproj, reads=list(rb) + [b_wkb, b_wvb],
                    writes=[b_pk, b_pv])

        def emit_kvev_v(i):
            r = i % 3
            sc.emit("V", lambda h, rr=r: h.scalar_tensor_tensor(
                vsb[rr][:].rearrange("p (a c) -> p a c", c=129)[:, :, 0:128],
                pv[:].rearrange("p (a c) -> p a c", c=128),
                1.0, bve[:].rearrange("p (a c) -> p a c", c=128),
                op0=ALU.mult, op1=ALU.add),
                reads=[b_pv, b_consts], writes=[b_vsb[r], b_pv])

        def emit_kvev_k(i):
            r = i % 3
            rs = i % 2

            def kadd(h, rr=rs):
                with nc.allow_low_precision(reason="phik consumed in bf16 anyway"):
                    return h.scalar_tensor_tensor(
                        scr2[rr][:], pk[:], 1.0, bke[:], op0=ALU.mult, op1=ALU.add)
            sc.emit("V", kadd, reads=[b_pk, b_consts], writes=[b_scr2[rs]])
            sc.emit("V", lambda h, rr=rs, r3=r: h.tensor_scalar(
                phik[r3][:], scr2[rr][:], KEPS, None, ALU.max),
                reads=[b_scr2[rs]], writes=[b_phik[r], b_pk])

        def emit_kvacc_half(i, half):
            r = i % 3
            def kvacc(h, rr=r, hf=half):
                out = None
                for p in range(hf * 4, hf * 4 + 4):
                    bank, q = divmod(p, 3)
                    out = h.matmul(kvb[bank][:, q * 129:q * 129 + 129],
                                   phik[rr][:, p * 128:(p + 1) * 128],
                                   vsb[rr][:, p * 129:(p + 1) * 129],
                                   start=True, stop=True)
                return out
            sc.emit("T", kvacc, reads=[b_phik[r], b_vsb[r]], writes=[b_kv])

        def emit_kvsum(i):
            if i == 0:
                def kvinit(h):
                    h.tensor_copy(kvout[:, 0:387], kvb[0][:, 0:387])
                    h.tensor_copy(kvout[:, 387:774], kvb[1][:, 0:387])
                    return h.tensor_copy(kvout[:, 774:1032], kvb[2][:, 0:258])
                sc.emit("V", kvinit, reads=[b_kv], writes=[b_kvout, b_kv])
            else:
                def kvadd(h):
                    h.tensor_add(kvout[:, 0:387], kvout[:, 0:387], kvb[0][:, 0:387])
                    h.tensor_add(kvout[:, 387:774], kvout[:, 387:774], kvb[1][:, 0:387])
                    return h.tensor_add(kvout[:, 774:1032], kvout[:, 774:1032],
                                        kvb[2][:, 0:258])
                sc.emit("V", kvadd, reads=[b_kv], writes=[b_kvout, b_kv])

        wpend = []
        for j in range(NTB + 5):
            if 2 <= j < NTB + 2:
                emit_tr_half(j - 2, 0)
            if 5 <= j < NTB + 5:
                emit_kvacc_half(j - 5, 0)
                emit_kvacc_half(j - 5, 1)
            if 2 <= j < NTB + 2:
                emit_tr_half(j - 2, 1)
            if 1 <= j < NTB + 1:
                emit_xln(j - 1)
            if 5 <= j < NTB + 5:
                emit_kvsum(j - 5)
            if j < NTB:
                emit_stats(j)
            if 3 <= j < NTB + 3:
                emit_kvproj(j - 3)
                emit_kvev_v(j - 3)
                emit_kvev_k(j - 3)
            if 1 <= j and j + 2 < NTB:
                emit_xload(j + 2)
            for fn in wpend:
                fn()
            wpend = []
            if wcast_steps:
                extra = wcast_steps.pop(0)()
                if extra:
                    wpend.append(extra)
        while wcast_steps or wpend:
            for fn in wpend:
                fn()
            wpend = []
            if wcast_steps:
                extra = wcast_steps.pop(0)()
                if extra:
                    wpend.append(extra)

        sb_kv1.close()
        sc.barrier(("T", "V", "S"))
        ps_a.close()
        ps_q = contextlib.ExitStack()
        pq2 = [ps_q.enter_context(nc.psum_tensor(f"pq{i}", [128, 1024], F32))
               for i in range(2)]
        b_pq2 = [Bf(), Bf()]
        for mchunk in range(NC):
            for hh in range(2):
                rr_ = (mchunk * 2 + hh) % 2

                def qproj(h, mm=mchunk, h2=hh, rr=rr_):
                    out = None
                    for c in range(NC):
                        w_ = wqb[:, c * D + mm * 128: c * D + (mm + 1) * 128]
                        for t4 in range(2):
                            t0 = h2 * 1024 + t4 * 512
                            out = h.matmul(pq2[rr][:, t4 * 512:(t4 + 1) * 512], w_,
                                           xfm_own[:, c * TOK + t0: c * TOK + t0 + 512],
                                           start=(c == 0), stop=(c == NC - 1))
                    return out
                sc.emit("T", qproj, reads=[b_wqb] + b_xfm_own, writes=[b_pq2[rr_]])
                sc.emit("V", lambda h, mm=mchunk, rr=rr_, h2=hh: h.tensor_scalar(
                    phiq[:, mm * TOK + h2 * 1024: mm * TOK + (h2 + 1) * 1024],
                    pq2[rr][:], bqe[:, mm:mm + 1], KEPS, ALU.add, ALU.max),
                    reads=[b_pq2[rr_], b_consts], writes=[b_phiq, b_pq2[rr_]])

        sc.emit("V", lambda h: h.memset(kvstat[:], 0.0), writes=[b_kvstat])
        sc.emit("V", lambda h: h.memset(denst[:], 0.0), writes=[b_denst])

        def mkkvstat(h):
            out = None
            for p in range(NP):
                h.tensor_copy(kvstat[0:64, p * 128:p * 128 + 64],
                              kvout[0:64, p * 129:p * 129 + 64])
                out = h.tensor_copy(kvstat[64:128, p * 128 + 64:p * 128 + 128],
                                    kvout[64:128, p * 129 + 64:p * 129 + 128])
            return out
        sc.emit("V", mkkvstat, reads=[b_kvout], writes=[b_kvstat])

        def mkdenst(h):
            out = None
            for c in range(NC):
                h.tensor_copy(denst[0:64, c * 16 + 2 * c: c * 16 + 2 * c + 1],
                              kvout[0:64, c * 129 + 128: c * 129 + 129])
                out = h.tensor_copy(denst[64:128, c * 16 + 2 * c + 1: c * 16 + 2 * c + 2],
                                    kvout[64:128, c * 129 + 128: c * 129 + 129])
            return out
        sc.emit("V", mkdenst, reads=[b_kvout], writes=[b_denst])

        sc.barrier(("T", "V", "S", "L"))
        ps_q.close()
        sb_kv2.close()
        sb_a.close()
        ps1 = contextlib.ExitStack()
        pb = ps1.enter_context(nc.psum_tensor("pb", [128, 512], F32))
        pn = ps1.enter_context(nc.psum_tensor("pn", [128, 512], F32))
        po = [ps1.enter_context(nc.psum_tensor(f"po{i}", [128, D], F32)) for i in range(2)]
        ptr3 = ps1.enter_context(nc.psum_tensor("ptr3", [128, 1024], F32))
        b_pb = Bf(); b_pn = Bf(); b_po = [Bf(), Bf()]; b_ptr3 = [Bf(), Bf()]
        sb_b = contextlib.ExitStack()
        wob = sb_b.enter_context(nc.sbuf_tensor("wob", [128, NC * D], BF16))
        borep = sb_b.enter_context(nc.sbuf_tensor("boreps", [128, D], F32))
        rden = sb_b.enter_context(nc.sbuf_tensor("rden", [16, 512], BF16))
        emat = sb_b.enter_context(nc.sbuf_tensor("emats", [16, NC * 128], BF16))
        bcs = [sb_b.enter_context(nc.sbuf_tensor(f"bcs{i}", [128, 512], F32))
               for i in range(2)]
        atr = [sb_b.enter_context(nc.sbuf_tensor(f"atr{i}", [128, NC * 512], BF16))
               for i in range(2)]
        scr = [sb_b.enter_context(nc.sbuf_tensor(f"scrA{i}", [128, D], F32))
               for i in range(2)]
        acc = [sb_b.enter_context(nc.sbuf_tensor(f"accA{i}", [128, D], F32))
               for i in range(2)]
        xt = [sb_b.enter_context(nc.sbuf_tensor(f"xt{i}", [128, D], F32))
              for i in range(2)]
        xln2 = [sb_b.enter_context(nc.sbuf_tensor(f"xln2{i}", [128, D], BF16))
                for i in range(3)]
        b_wob = Bf(); b_rden = Bf(); b_bcs = [Bf(), Bf()]; b_atr = [Bf(), Bf()]
        load(emat[:], ap["emat"], [b_consts])
        load(borep[:], ap["borep"], [b_consts])
        wo_src = ap["wo"].rearrange("(c p) d -> c p d", p=128)
        for c in range(NC):
            r = c % 3
            load(wtmp[r][:], wo_src[c], [b_wtmp[r]])
            sc.emit("S", lambda h, cc=c, rr=r: h.activation(
                wob[:, cc * D:(cc + 1) * D], wtmp[rr][:], AF.Identity),
                reads=[b_wtmp[r]], writes=[b_wob])

        acc_t = accbuf.rearrange("(n p) d -> n p d", p=128)

        def emit_attn_group(t4):
            gr = t4 % 2

            def phq(c):
                return phiq[:, c * TOK + t4 * 512: c * TOK + (t4 + 1) * 512]

            def denmm(h):
                out = None
                for c in range(NC):
                    out = h.matmul(pb[0:16, :], denst[:, c * 16:(c + 1) * 16],
                                   phq(c), start=(c == 0), stop=(c == NC - 1))
                return out
            sc.emit("T", denmm, reads=[b_denst, b_phiq], writes=[b_pb])

            def recip_b(h):
                with nc.allow_low_precision(reason="bf16 1/den as bcast operand"):
                    return h.reciprocal(rden[:], pb[0:16, :])
            sc.emit("V", recip_b, reads=[b_pb], writes=[b_rden, b_pb])
            for c in range(NC):
                r = c % 2
                sc.emit("T", lambda h, cc=c: h.matmul(
                    pb[:], emat[:, cc * 128:(cc + 1) * 128],
                    rden[:], start=True, stop=True),
                    reads=[b_consts, b_rden], writes=[b_pb])
                sc.emit("T", lambda h, cc=c: h.matmul(
                    pn[:], kvstat[:, cc * 128:(cc + 1) * 128],
                    phq(cc), start=True, stop=True),
                    reads=[b_kvstat, b_phiq], writes=[b_pn])
                sc.emit("S", lambda h, rr=r: h.copy(bcs[rr][:], pb[:]),
                        reads=[b_pb], writes=[b_bcs[r], b_pb])
                sc.emit("V", lambda h, cc=c, rr=r, g=gr: h.tensor_mul(
                    atr[g][:, cc * 512:(cc + 1) * 512],
                    pn[:], bcs[rr][:]),
                    reads=[b_pn, b_bcs[r]], writes=[b_atr[gr], b_pn])

        def emit_womm(i):
            r = i % 2
            gr = (i // 4) % 2
            js = (i % 4) * 128

            def womm(h, gg=gr, jj=js, rr=r):
                out = None
                for c in range(NC):
                    a_ = atr[gg][:, c * 512 + jj: c * 512 + jj + 128]
                    st, sp = (c == 0), (c == NC - 1)
                    h.matmul(po[rr][:, 0:512], a_, wob[:, c * D:c * D + 512],
                             start=st, stop=sp)
                    out = h.matmul(po[rr][:, 512:1024], a_,
                                   wob[:, c * D + 512:c * D + 1024], start=st, stop=sp)
                return out
            sc.emit("T", womm, reads=[b_atr[gr], b_wob], writes=[b_po[r]])

        def emit_tile_chain(i):
            r = i % 2
            r3 = i % 3
            load(xt[r][:], xb_t[i], [b_xt[r]])
            sc.emit("V", lambda h, rr=r: h.scalar_tensor_tensor(
                scr[rr][:], po[rr][:], 1.0, borep[:], op0=ALU.mult, op1=ALU.add),
                reads=[b_po[r], b_consts], writes=[b_scr[r], b_po[r]])
            sc.emit("V", lambda h, rr=r: h.tensor_add(scr[rr][:], scr[rr][:], xt[rr][:]),
                reads=[b_xt[r]], writes=[b_scr[r]])
            sc.emit("V", lambda h, rr=r: h.tensor_add(acc[rr][:], scr[rr][:], xt[rr][:]),
                reads=[b_scr[r], b_xt[r]], writes=[b_acc[r]])
            sc.emit("G", lambda h, ii=i, rr=r: h.dma_start(out=acc_t[ii], in_=acc[rr][:]),
                    reads=[b_acc[r]], writes=[b_accd[i]], dma=True)
            ln_stats(sc, scr[r], b_scr[r], scr2[r], b_scr2[r], ssum[r], b_sum[r],
                     ssq[r], b_sq[r], srstd[r], b_rstd[r], snmr[r], b_nmr[r], LNEPS)
            sc.emit("S", lambda h, rr=r, rx=r3: h.activation(
                xln2[rx][:], scr[rr][:], AF.Identity, scale=srstd[rr][:], bias=snmr[rr][:]),
                reads=[b_scr[r], b_rstd[r], b_nmr[r]], writes=[b_xln2[r3]])

        def emit_tr3(i, half):
            r = i % 3

            def tr(h, rr=r, hf=half):
                out = None
                for j in range(4):
                    c = hf * 4 + j
                    out = h.matmul(ptr3[:, hf * 512 + j * 128: hf * 512 + (j + 1) * 128],
                                   xln2[rr][:, c * 128:(c + 1) * 128],
                                   ident[:], start=True, stop=True)
                return out
            sc.emit("T", tr, reads=[b_xln2[r], b_consts], writes=[b_ptr3[half]])

            def ev(h, ii=i, hf=half):
                out = None
                for j in range(4):
                    c = hf * 4 + j
                    out = h.activation(
                        xfm2[:, c * TOK + ii * 128: c * TOK + (ii + 1) * 128],
                        ptr3[:, hf * 512 + j * 128: hf * 512 + (j + 1) * 128],
                        AF.Identity, scale=g2[:, c:c + 1], bias=ln2b[:, c:c + 1])
                return out
            sc.emit("S", ev, reads=[b_ptr3[half], b_consts],
                    writes=[b_xfm2[i], b_ptr3[half]])

        emit_attn_group(0)
        emit_attn_group(1)
        for i in range(NT + 2):
            if i < NT:
                emit_womm(i)
                if i % 4 == 3 and (i // 4) + 2 <= 3:
                    emit_attn_group((i // 4) + 2)
            if 2 <= i:
                emit_tr3(i - 2, 0)
                emit_tr3(i - 2, 1)
            if i < NT:
                emit_tile_chain(i)

        sc.barrier(("T", "V", "S", "L", "G"))
        sb_b.close()
        sb_ph.close()
        ps1.close()
        sbk8 = contextlib.ExitStack()
        k8 = sbk8.enter_context(nc.sbuf_tensor("k8", [128, NC * D], FP8))
        x2r = [sbk8.enter_context(nc.sbuf_tensor(f"x2r{i}", [128, D], F32))
               for i in range(4)]
        wc1k = [sbk8.enter_context(nc.sbuf_tensor(f"wc1k{i}", [128, D], BF16))
                for i in range(2)]
        accB = [sbk8.enter_context(nc.sbuf_tensor(f"accB{i}", [128, D], F32))
                for i in range(2)]
        scr5 = [sbk8.enter_context(nc.sbuf_tensor(f"scr5{i}", [128, D], F32))
                for i in range(2)]
        b_accB = [Bf(), Bf()]; b_scr5 = [Bf(), Bf()]
        sb4 = contextlib.ExitStack()
        y1g = sb4.enter_context(nc.sbuf_tensor("y1g", [128, NM * 512], BF16))
        b2rep = sb4.enter_context(nc.sbuf_tensor("b2reps", [128, D], F32))
        y2o = [sb4.enter_context(nc.sbuf_tensor(f"y2o{i}", [128, D], F32))
               for i in range(4)]
        b_y2o = [Bf() for _ in range(4)]
        fk_src = ap["fker"].rearrange("(c p) d -> c p d", p=128)
        for c in range(NC):
            r = c % 2
            load(wc1k[r][:], fk_src[c], [b_wc1k[r]])
            sc.emit("V", lambda h, cc=c, rr=r: h.tensor_scalar(
                k8[:, cc * D:(cc + 1) * D], wc1k[rr][:], scl[:, 2:3], None, ALU.mult),
                reads=[b_wc1k[r], b_scl], writes=[b_k8])
        w1blk = [sb4.enter_context(nc.sbuf_tensor(f"w1blk{i}", [128, NC * 128], BF16))
                 for i in range(8)]
        w2blk = [sb4.enter_context(nc.sbuf_tensor(f"w2blk{i}", [128, D], BF16))
                 for i in range(8)]
        b_y1g = [Bf() for _ in range(NM)]
        load(b2rep[:], ap["b2rep"], [b_consts])
        x2_t = x2buf.rearrange("(n p) d -> n p d", p=128)
        x2b_t = x2bb.rearrange("(n p) d -> n p d", p=128)
        b_x2bd = [Bf() for _ in range(NT)]
        qwait = [None]
        for q in range(4):
            qoff = q * 512
            psy1 = contextlib.ExitStack()
            py1 = [psy1.enter_context(nc.psum_tensor(f"py1_{q}_{i}", [128, 512], F32))
                   for i in range(2)]
            b_py1 = [Bf(), Bf()]
            for m in range(4):
                load(w1blk[m][:], w1c[m], [b_w1blk[m]], reads=[b_w1c])
            for m in range(NM):
                r = m % 8
                if m + 4 < NM:
                    load(w1blk[(m + 4) % 8][:], w1c[m + 4], [b_w1blk[(m + 4) % 8]],
                         reads=[b_w1c])

                def y1mm(h, rr=r, pp=m % 2, qo=qoff):
                    out = None
                    for c in range(NC):
                        out = h.matmul(
                            py1[pp][:],
                            w1blk[rr][:, c * 128:(c + 1) * 128],
                            xfm2[:, c * TOK + qo: c * TOK + qo + 512],
                            start=(c == 0), stop=(c == NC - 1))
                    return out
                pr = m % 2
                sc.emit("T", y1mm, reads=[b_w1blk[r]] + b_xfm2, writes=[b_py1[pr]],
                        force_waits=[qwait[0]] if (m == 0 and qwait[0]) else ())
                sc.emit("S", lambda h, mm=m, pp=pr: h.activation(
                    y1g[:, mm * 512:(mm + 1) * 512],
                    py1[pp][:], AF.Gelu_apprx_tanh, bias=b1m[:, mm:mm + 1]),
                    reads=[b_py1[pr], b_consts], writes=[b_y1g[m], b_py1[pr]])
            t_S_gelu = ("S", sc.cnt["S"])
            psy1.close()
            psy2 = contextlib.ExitStack()
            py2 = [psy2.enter_context(nc.psum_tensor(f"py2_{q}_{i}", [128, D], F32))
                   for i in range(4)]
            b_py2 = [Bf() for _ in range(4)]
            for m in range(4):
                sc.emit("S", lambda h, mm=m: h.dma_start(
                    out=w2blk[mm][:], in_=w2c[mm]),
                    reads=[b_w2c], writes=[b_w2blk[m]], dma=True)
            for m in range(NM):
                r = m % 8
                if m + 4 < NM:
                    sc.emit("S", lambda h, mm=m + 4: h.dma_start(
                        out=w2blk[mm % 8][:], in_=w2c[mm]),
                        reads=[b_w2c], writes=[b_w2blk[(m + 4) % 8]], dma=True)

                def y2mm(h, mm=m, rr=r):
                    out = None
                    for ss in range(4):
                        yg = y1g[:, mm * 512 + ss * 128: mm * 512 + (ss + 1) * 128]
                        st, sp = (mm == 0), (mm == NM - 1)
                        h.matmul(py2[ss][:, 0:512], yg, w2blk[rr][:, 0:512],
                                 start=st, stop=sp)
                        out = h.matmul(py2[ss][:, 512:1024], yg, w2blk[rr][:, 512:1024],
                                       start=st, stop=sp)
                    return out
                sc.emit("T", y2mm, reads=[b_y1g[m], b_w2blk[r]],
                        writes=b_py2 if m == NM - 1 else [],
                        force_waits=[t_S_gelu] if m == 0 else ())
            for ss in range(4):
                i = q * 4 + ss
                r = i % 2
                load(accB[r][:], acc_t[i], [b_accB[r]], reads=[b_accd[i]])
                sc.emit("V", lambda h, sss=ss, rr=r: h.scalar_tensor_tensor(
                    y2o[sss][:], py2[sss][:], 1.0, b2rep[:], op0=ALU.mult, op1=ALU.add),
                    reads=[b_py2[ss], b_consts], writes=[b_y2o[ss], b_py2[ss]])
                sc.emit("V", lambda h, sss=ss, rr=r: h.tensor_add(
                    y2o[sss][:], y2o[sss][:], accB[rr][:]),
                    reads=[b_accB[r]], writes=[b_y2o[ss]])
            qwait[0] = ("V", sc.cnt["V"])
            for ss in range(4):
                i = q * 4 + ss
                r = i % 2
                ln_stats(sc, y2o[ss], b_y2o[ss], scr2[r], b_scr2[r], ssum[r], b_sum[r],
                         ssq[r], b_sq[r], srstd[r], b_rstd[r], snmr[r], b_nmr[r], PLNEPS)
                r4 = i % 4
                sc.emit("V", lambda h, sss=ss, rr=r, r4_=r4: h.tensor_scalar(
                    x2r[r4_][:], y2o[sss][:], srstd[rr][:], snmr[rr][:],
                    ALU.mult, ALU.add),
                    reads=[b_y2o[ss], b_rstd[r], b_nmr[r]], writes=[b_x2r[r4]])
                sc.emit("G", lambda h, ii=i, r4_=r4: h.dma_start(out=x2_t[ii], in_=x2r[r4_][:]),
                        reads=[b_x2r[r4]], writes=[b_x2d[i]], dma=True)
                sc.emit("G", lambda h, ii=i, r4_=r4: h.dma_start(
                    out=x2b_t[ii], in_=x2r[r4_][:]),
                    reads=[b_x2r[r4]], writes=[b_x2bd[i]], dma=True)
            psy2.close()

        sb4.close()
        sb5 = contextlib.ExitStack()
        oring = [sb5.enter_context(nc.sbuf_tensor(f"oring{i}", [128, D], F32))
                 for i in range(2)]
        x2b = [sb5.enter_context(nc.sbuf_tensor(f"x2b{i}", [128, D], BF16))
               for i in range(4)]
        ytmp = [sb5.enter_context(nc.sbuf_tensor(f"ytmp{i}", [128, D], BF16))
               for i in range(2)]
        b_oring = [Bf(), Bf()]; b_x2b = [Bf() for _ in range(4)]
        ps5 = contextlib.ExitStack()
        pf = [ps5.enter_context(nc.psum_tensor(f"pf{i}", [128, D], F32)) for i in range(3)]
        ptr5 = [ps5.enter_context(nc.psum_tensor(f"ptr5{i}", [128, 512], F32)) for i in range(2)]
        b_pf = [Bf() for _ in range(3)]; b_ptr5 = [Bf(), Bf()]
        t_b5 = qwait[0]
        out_t = ap["out"].rearrange("(n p) d -> n p d", p=128)

        def emit_x8_pre(i):
            r4 = i % 4
            load(x2r[r4][:], x2_t[i], [b_x2r[r4]], reads=[b_x2d[i]])
            sc.emit("S", lambda h, r4_=r4, ii=i: h.dma_start(
                out=x2b[r4_][:], in_=x2b_t[ii]),
                reads=[b_x2bd[i]], writes=[b_x2b[r4]], dma=True)

        def emit_x8(i):
            r = i % 2
            r4 = i % 4
            for hf in range(2):
                def tr(h, rr=r4, hh=hf):
                    out = None
                    for j in range(4):
                        c = hh * 4 + j
                        out = h.matmul(ptr5[hh][:, j * 128:(j + 1) * 128],
                                       x2b[rr][:, c * 128:(c + 1) * 128],
                                       ident[:], start=True, stop=True)
                    return out
                sc.emit("T", tr, reads=[b_x2b[r4], b_consts], writes=[b_ptr5[hf]],
                        force_waits=[t_b5] if (i == 0 and hf == 0 and t_b5) else ())

                def ev(h, rr=r, hh=hf):
                    return h.tensor_scalar(
                        x8r[rr][:, hh * 512:(hh + 1) * 512],
                        ptr5[hh][:], scl[:, 1:2], None, ALU.mult)
                sc.emit("V", ev, reads=[b_ptr5[hf], b_scl], writes=[b_x8r[r], b_ptr5[hf]])

        def emit_ffn(i):
            r = i % 3
            rx = i % 2

            def ffnmm(h, rr=r, rx_=rx):
                out = None
                for c in range(NC):
                    xs_ = x8r[rx_][:, c * 128:(c + 1) * 128]
                    st, sp = (c == 0), (c == NC - 1)
                    h.matmul(pf[rr][:, 0:512], xs_, k8[:, c * D:c * D + 512],
                             start=st, stop=sp)
                    out = h.matmul(pf[rr][:, 512:1024], xs_,
                                   k8[:, c * D + 512:c * D + 1024], start=st, stop=sp)
                return out
            sc.emit("T", ffnmm, reads=[b_x8r[rx], b_k8], writes=[b_pf[r]])

        def emit_out_chain(i):
            r = i % 2
            p3 = i % 3
            sc.emit("S", lambda h, rr=r, pp=p3: h.activation(
                ytmp[rr][:], pf[pp][:], AF.Identity, scale=scl[:, 0:1]),
                reads=[b_pf[p3], b_scl], writes=[b_ytmp[r], b_pf[p3]])
            sc.emit("V", lambda h, rr=r: h.tensor_add(ytmp[rr][:], ytmp[rr][:], fbrep[:]),
                reads=[b_consts], writes=[b_ytmp[r]])
            sc.emit("V", lambda h, rr=r, r4_=i % 4: h.tensor_add(
                scr5[rr][:], x2r[r4_][:], ytmp[rr][:]),
                reads=[b_x2r[i % 4], b_ytmp[r]], writes=[b_scr5[r]])
            ln_stats(sc, scr5[r], b_scr5[r], scr2[r], b_scr2[r], ssum[r], b_sum[r],
                     ssq[r], b_sq[r], srstd[r], b_rstd[r], snmr[r], b_nmr[r], PLNEPS)
            sc.emit("S", lambda h, rr=r: h.activation(
                oring[rr][:], scr5[rr][:], AF.Identity, scale=srstd[rr][:], bias=snmr[rr][:]),
                reads=[b_scr5[r], b_rstd[r], b_nmr[r]], writes=[b_oring[r]])
            sc.emit("G", lambda h, ii=i, rr=r: h.dma_start(out=out_t[ii], in_=oring[rr][:]),
                    reads=[b_oring[r]], dma=True)

        emit_x8_pre(0)
        emit_x8_pre(1)
        for i in range(NT + 1):
            if i + 2 < NT:
                emit_x8_pre(i + 2)
            if i >= 1:
                emit_ffn(i - 1)
            if i < NT:
                emit_x8(i)
            if i >= 1:
                emit_out_chain(i - 1)
        sc.emit("G", lambda h: h.nop(),
                force_waits=[(f"G{j}", sc.cnt[f"G{j}"]) for j in range(Sched.NDMA)
                             if sc.cnt[f"G{j}"] > 0], noinc=True)
        ps5.close()
        sb5.close()
        sbk8.close()

        sc.install(block, sems)
    return nc


def _fm(vec, nchunk):
    return np.ascontiguousarray(np.asarray(vec).reshape(nchunk, 128).T)


def _rep(vec, dt=np.float32):
    v = np.asarray(vec).reshape(1, -1)
    return np.ascontiguousarray(np.broadcast_to(v, (128, v.shape[1])).astype(dt, copy=False))


def _emat():
    e = np.zeros((16, NC * 128), dtype=BF16_NP)
    for c in range(NC):
        e[2 * c, c * 128: c * 128 + 64] = 1
        e[2 * c + 1, c * 128 + 64: c * 128 + 128] = 1
    return e


def kernel(**inp):
    x = np.asarray(inp["L0_attn_inputs"])
    wq = np.ascontiguousarray(np.asarray(inp["L0_attn_Wq"]).reshape(D, D))
    wk = np.ascontiguousarray(np.asarray(inp["L0_attn_Wk"]).reshape(D, D))
    wv = np.ascontiguousarray(np.asarray(inp["L0_attn_Wv"]).reshape(D, D))
    wo = np.ascontiguousarray(np.asarray(inp["L0_attn_Wo"]).reshape(D, D))
    w1 = np.ascontiguousarray(np.asarray(inp["L0_attn_W1"]))
    w2 = np.ascontiguousarray(np.asarray(inp["L0_attn_W2"]))
    g1fm = _fm(inp["L0_attn_ln1_scale"], NC)
    b1fm = _fm(inp["L0_attn_ln1_bias"], NC)
    g2fm = _fm(inp["L0_attn_ln2_scale"], NC)
    ln2bfm = _fm(inp["L0_attn_ln2_bias"], NC)
    bqfm = _fm(np.asarray(inp["L0_attn_bq"]).reshape(D), NC)
    bkrep = _rep(np.asarray(inp["L0_attn_bk"]).reshape(D))
    bvrep = _rep(np.asarray(inp["L0_attn_bv"]).reshape(D))
    borep = _rep(np.asarray(inp["L0_attn_bo"]))
    b2rep = _rep(np.asarray(inp["L0_attn_b2"]))
    b1mfm = _fm(np.asarray(inp["L0_attn_b1"]), NM)
    fker = np.ascontiguousarray(np.asarray(inp["L0_ffn_kernel"]))
    fsin = np.ascontiguousarray(np.asarray(inp["L0_ffn_input_scale"]).reshape(1, 1))
    fsk = np.ascontiguousarray(np.asarray(inp["L0_ffn_kernel_scale"]).reshape(1, 1))
    fbrep = _rep(np.asarray(inp["L0_ffn_bias"]), BF16_NP)
    ident = np.eye(128, dtype=BF16_NP)
    ones1 = np.ones((1, 128), dtype=np.float32)
    emat = _emat()

    xbs = []
    for c in range(8):
        b, q = c // 4, c % 4
        xbs.append(np.ascontiguousarray(
            np.concatenate([x[b, q * TOK:], x[b, :q * TOK]], axis=0)))

    ncF = build_fused()
    in_maps = [{"xb": xbs[c], "wq": wq, "wk": wk, "wv": wv, "g1fm": g1fm,
                "b1fm": b1fm, "bqfm": bqfm, "bkrep": bkrep, "bvrep": bvrep,
                "wo": wo, "borep": borep, "g2fm": g2fm, "ln2bfm": ln2bfm,
                "w1": w1, "b1mfm": b1mfm, "w2": w2, "b2rep": b2rep,
                "fker": fker, "fsin": fsin, "fsk": fsk, "fbrep": fbrep,
                "ident": ident, "ones1": ones1, "emat": emat}
               for c in range(8)]
    res = run_bass_kernel_spmd(ncF, in_maps, core_ids=list(range(8)))
    outs = [np.asarray(res.results[c]["out"]) for c in range(8)]

    full = np.empty((BATCH, SEQ, D), dtype=np.float32)
    for c in range(8):
        full[c // 4, (c % 4) * TOK:(c % 4 + 1) * TOK] = outs[c]
    return full
